# version 07
# baseline (speedup 1.0000x reference)
import contextlib
import numpy as np
import ml_dtypes
import concourse.bass as bass
import concourse.mybir as mybir
from concourse.bass_utils import run_bass_kernel_spmd

F32 = mybir.dt.float32
U8 = mybir.dt.uint8
BF16 = mybir.dt.bfloat16
I32 = mybir.dt.int32
AF = mybir.ActivationFunctionType
ALU = mybir.AluOpType
ET = mybir.EngineType
ALL_E = [ET.PE, ET.Activation, ET.Pool, ET.DVE, ET.SP]

BF = ml_dtypes.bfloat16

B, S, D = 4, 4096, 1024
NH, DH = 16, 64
T = 2048
DT = D // 128
F_FFN = 4096
DI = 2048
NS1 = 16
E3, N3, DD = 2048, 16, 64
EPS = 1e-5


class Sem:
    def __init__(self, nc, stack, name):
        self.h = stack.enter_context(nc.semaphore(name))
        self.v = 0

    def inc(self, instr, n=1):
        instr.then_inc(self.h, n)
        self.v += n
        return instr

    def wait(self, eng, v=None):
        eng.wait_ge(self.h, self.v if v is None else v)


class Ctx:
    def __init__(self, nc, stack, fused):
        self.nc = nc
        self.stack = stack
        self.fused = fused
        self.t = nc.tensor
        self.v = nc.vector
        self.a = nc.scalar
        self.g = nc.gpsimd
        self.sp = nc.sync
        self.io = {}
        self.sems = {}
        self.consts = {}
        self.regs = {}
        self._uid = 0

    def sem(self, name):
        key = name if name.startswith("poll") else name.split("_")[-1]
        if key not in self.sems:
            self.sems[key] = Sem(self.nc, self.stack, key)
        return self.sems[key]

    def uid(self):
        self._uid += 1
        return self._uid

    def sbuf(self, name, shape, dtype):
        return self.stack.enter_context(self.nc.sbuf_tensor(name, shape, dtype))

    def par_reg(self, eng):
        key = ("par", eng.engine)
        if key not in self.regs:
            pid = eng.partition_id()
            self.regs[key] = eng.to_reg(pid % 2)
        return self.regs[key]

    def poll_regs(self, eng):
        key = ("poll", eng.engine)
        if key not in self.regs:
            nm = str(eng.engine)
            self.regs[key] = (eng.alloc_register(f"pr_n_{nm}"),
                              eng.alloc_register(f"pr_a_{nm}"),
                              eng.alloc_register(f"pr_b_{nm}"))
        return self.regs[key]


def slot_dma(ctx, eng, sem, dst_slots, src, n=16):
    par = ctx.par_reg(eng)
    with eng.If_cmp(par, 0, "IS_EQ"):
        sem.inc(eng.dma_start(out=dst_slots[0], in_=src), n)
    with eng.Else():
        eng.dma_start(out=dst_slots[1], in_=src).then_inc(sem.h, n)


def other_slot_dma(ctx, eng, sem, dst, src_slots, n=16):
    par = ctx.par_reg(eng)
    with eng.If_cmp(par, 0, "IS_EQ"):
        sem.inc(eng.dma_start(out=dst, in_=src_slots[1]), n)
    with eng.Else():
        eng.dma_start(out=dst, in_=src_slots[0]).then_inc(sem.h, n)


def emit_flag_write(ctx, eng, ex_id):
    if not ctx.fused:
        return
    fl = ctx.io["flags"]
    nb = ctx.consts["nonce"]
    fsem = ctx.sem("flagw")
    slot_dma(ctx, eng, fsem,
             (fl[0:1, ex_id * 16:ex_id * 16 + 1], fl[1:2, ex_id * 16:ex_id * 16 + 1]),
             nb[0:1, ex_id:ex_id + 1])
    fsem.wait(eng)


def emit_poll(ctx, eng, ex_id):
    if not ctx.fused:
        return
    consumer = eng
    g = ctx.g
    fl = ctx.io["flags"]
    nb = ctx.consts["nonce"]
    if "pollbuf" not in ctx.consts:
        ctx.consts["pollbuf"] = ctx.sbuf("c_pollbuf", [1, 2], I32)
    fs = ctx.consts["pollbuf"]
    nreg, f0, f1 = ctx.poll_regs(g)
    g.reg_load(nreg, nb[0:1, ex_id:ex_id + 1])
    u = ctx.uid()
    lp, chk, done, cont = f"plp{u}", f"plk{u}", f"pld{u}", f"plc{u}"
    src = fl[0:2, ex_id * 16:ex_id * 16 + 1].rearrange("a b -> b a")
    engT = g.engine
    pd = ctx.sem("pdone")
    pp = ctx.sem("ppoll")
    g.br(lp)
    with ctx.nc.body(lp, valid_engines=[engT]):
        with ctx.nc.allow_non_contiguous_dma(reason="2-element flag poll"):
            g.dma_start(out=fs[0:1, 0:2], in_=src).then_inc(pp.h, 16)
        g.reg_load(f0, fs[0:1, 0:1])
        g.reg_load(f1, fs[0:1, 1:2])
        g.br_cmp(f0, nreg, on_true=chk, on_false=lp, comp_op="IS_EQ")
    with ctx.nc.body(chk, valid_engines=[engT]):
        g.br_cmp(f1, nreg, on_true=done, on_false=lp, comp_op="IS_EQ")
    with ctx.nc.body(done, valid_engines=[engT]):
        pd.inc(g.nop())
    ctx.nc.switch_bb(f"plafter{u}")
    if consumer is not g:
        pd.wait(consumer)


def _fm(x_btd, b, half):
    sl = x_btd[b, half * T:(half + 1) * T, :]
    return np.ascontiguousarray(sl.T.reshape(DT, 128, T))


def _col8(vec):
    return np.ascontiguousarray(vec.reshape(DT, 128).T)


def prepare_core_inputs(inp, core, nonce):
    b, h = core // 2, core % 2
    d = {}
    d["x0"] = _fm(np.asarray(inp["L0_attn_x"]), b, h)
    d["nonce"] = nonce
    d["ones128"] = np.ones((128, 1), np.float32)
    cp = np.zeros((128, 8), np.float32)
    cp[:, 0] = EPS
    cp[:, 1] = -0.5
    cp[:, 2] = 0.7978845608028654
    cp[:, 3] = 0.5
    cp[:, 4] = -1.0
    cp[:, 5] = 1.0
    cp[:, 6] = -128.0
    d["cpool"] = cp
    d["id128"] = np.eye(128, dtype=np.float32)

    hs = slice(h * 512, h * 512 + 512)
    cs = slice(h * 1024, h * 1024 + 1024)

    wqkv = np.asarray(inp["L0_attn_W_qkv"])
    wq, wk, wv = wqkv[0:D], wqkv[D:2 * D], wqkv[2 * D:3 * D]
    d["l0_wqT"] = np.ascontiguousarray(wq[hs].T)
    d["l0_wkT"] = np.ascontiguousarray(wk[hs].T)
    d["l0_wvT"] = np.ascontiguousarray(wv[hs].T)
    bqkv = np.asarray(inp["L0_attn_b_qkv"])
    d["l0_bq"] = np.ascontiguousarray(bqkv[0:D][hs].reshape(4, 128).T)
    d["l0_bk"] = np.ascontiguousarray(bqkv[D:2 * D][hs].reshape(4, 128).T)
    d["l0_bv"] = np.ascontiguousarray(bqkv[2 * D:3 * D][hs].reshape(4, 128).T)
    d["l0_bv_row"] = np.ascontiguousarray(bqkv[2 * D:3 * D][hs].reshape(1, 512))
    d["l0_woT"] = np.ascontiguousarray(np.asarray(inp["L0_attn_W_o"]).T[hs])
    d["l0_bo"] = _col8(np.asarray(inp["L0_attn_b_o"]))

    for L in (0, 3):
        p = f"L{L}_ffn_"
        d[f"l{L}f_g"] = _col8(np.asarray(inp[p + "ln_gamma"]))
        d[f"l{L}f_b"] = _col8(np.asarray(inp[p + "ln_beta"]))
        d[f"l{L}f_q1"] = np.asarray(inp[p + "fc1_q"])
        d[f"l{L}f_s1"] = np.asarray(inp[p + "fc1_scales"])
        d[f"l{L}f_b1"] = np.ascontiguousarray(np.asarray(inp[p + "fc1_bias"]).reshape(32, 128).T)
        d[f"l{L}f_q2"] = np.asarray(inp[p + "fc2_q"])
        d[f"l{L}f_s2"] = np.asarray(inp[p + "fc2_scales"])
        d[f"l{L}f_b2"] = _col8(np.asarray(inp[p + "fc2_bias"]))

    w_in = np.asarray(inp["L1_attn_W_in"])
    d["l1_wgT"] = np.ascontiguousarray(w_in[:, 0:DI][:, cs])
    d["l1_wiT"] = np.ascontiguousarray(w_in[:, DI:2 * DI][:, cs])
    cw = np.asarray(inp["L1_attn_conv_w"])[:, 0, :][cs]
    d["l1_cw"] = np.ascontiguousarray(cw.reshape(8, 128, 4).transpose(1, 0, 2).reshape(128, 32))
    d["l1_cb"] = np.ascontiguousarray(np.asarray(inp["L1_attn_conv_b"])[cs].reshape(8, 128).T)
    d["l1_wd"] = np.ascontiguousarray(np.asarray(inp["L1_attn_W_delta"])[cs])
    d["l1_wb"] = np.ascontiguousarray(np.asarray(inp["L1_attn_W_B"])[cs])
    d["l1_wc"] = np.ascontiguousarray(np.asarray(inp["L1_attn_W_C"])[cs])
    d["l1_D"] = np.ascontiguousarray(np.asarray(inp["L1_attn_D"])[cs].reshape(8, 128).T)
    d["l1_bd"] = np.ascontiguousarray(np.asarray(inp["L1_attn_b_delta"]).reshape(16, 1))
    d["l1_wout"] = _col8(np.asarray(inp["L1_attn_w_out"]))
    d["l1_logA"] = (-0.5 * np.log1p(np.arange(1, NS1 + 1, dtype=np.float64) / NS1)
                    ).astype(np.float32).reshape(16, 1)

    d["moe_gwT"] = np.ascontiguousarray(np.asarray(inp["L1_ffn_gate_W"]))
    d["moe_gb"] = np.ascontiguousarray(np.asarray(inp["L1_ffn_gate_b"]).reshape(8, 1))
    d["moe_w1"] = np.asarray(inp["L1_ffn_W1"])
    d["moe_b1"] = np.ascontiguousarray(
        np.asarray(inp["L1_ffn_b1"]).reshape(8, 32, 128).transpose(0, 2, 1))
    d["moe_w2"] = np.asarray(inp["L1_ffn_W2"])
    d["moe_b2"] = np.asarray(inp["L1_ffn_b2"])
    sel8 = np.zeros((8, 8, 128), np.float32)
    for e in range(8):
        sel8[e, e, :] = 1.0
    d["moe_sel8"] = np.ascontiguousarray(sel8.transpose(1, 0, 2).reshape(8, 1024))
    d["tri8"] = np.ascontiguousarray(np.tril(np.ones((8, 8), np.float32), -1).T)

    d["l2_wqT"] = np.ascontiguousarray(np.asarray(inp["L2_attn_Wq"])[hs].T)
    d["l2_wkT"] = np.ascontiguousarray(np.asarray(inp["L2_attn_Wk"])[hs].T)
    d["l2_wvT"] = np.ascontiguousarray(np.asarray(inp["L2_attn_Wv"])[hs].T)
    d["l2_woT"] = np.ascontiguousarray(np.asarray(inp["L2_attn_Wo"]).T[hs])
    inv = 1.0 / 10000.0 ** (np.arange(0, DH, 2, dtype=np.float64) / DH)
    tpos = np.arange(S, dtype=np.float64)
    fr = np.outer(inv, tpos)
    emb = np.concatenate([fr, fr], axis=0)
    cos1 = np.cos(emb).astype(np.float32)
    sin1 = np.sin(emb).astype(np.float32)
    d["rope_cos"] = np.ascontiguousarray(np.concatenate([cos1, cos1], axis=0))
    d["rope_sin"] = np.ascontiguousarray(np.concatenate([sin1, sin1], axis=0))
    P = np.zeros((DH, DH), np.float32)
    for i in range(32):
        P[32 + i, i] = -1.0
        P[i, 32 + i] = 1.0
    P2 = np.zeros((128, 128), np.float32)
    P2[0:64, 0:64] = P
    P2[64:128, 64:128] = P
    d["ropeP"] = P2
    kk = np.arange(128).reshape(128, 1, 1)
    qq = np.arange(512).reshape(1, 1, 512)
    jj = np.arange(4).reshape(1, 4, 1)
    d["cmask"] = np.ascontiguousarray(
        (kk + jj * 128 <= qq).astype(np.float32).reshape(128, 4 * 512))
    hsel = np.zeros((128, 2), np.float32)
    hsel[0:64, 0] = 1.0
    hsel[64:128, 1] = 1.0
    d["half_sel"] = hsel

    d["l2f_u1"] = np.asarray(inp["L2_ffn_U1"])
    d["l2f_v1"] = np.asarray(inp["L2_ffn_V1"])
    d["l2f_b1"] = np.ascontiguousarray(np.asarray(inp["L2_ffn_b1"]).reshape(32, 128).T)
    d["l2f_u2"] = np.asarray(inp["L2_ffn_U2"])
    d["l2f_v2"] = np.asarray(inp["L2_ffn_V2"])
    d["l2f_b2"] = _col8(np.asarray(inp["L2_ffn_b2"]))

    es = cs
    d["l3_normw"] = _col8(np.asarray(inp["L3_attn_norm_w"]))
    d["l3_skipT"] = np.ascontiguousarray(np.asarray(inp["L3_attn_skip_w"])[es].T)
    d["l3_inT"] = np.ascontiguousarray(np.asarray(inp["L3_attn_in_w"])[es].T)
    cw3 = np.asarray(inp["L3_attn_conv_w"])[:, 0, :][es]
    d["l3_cw"] = np.ascontiguousarray(cw3.reshape(8, 128, 4).transpose(1, 0, 2).reshape(128, 32))
    d["l3_cb"] = np.ascontiguousarray(np.asarray(inp["L3_attn_conv_b"])[es].reshape(8, 128).T)
    d["l3_wd1T"] = np.ascontiguousarray(np.asarray(inp["L3_attn_Wd1"]).T[es])
    d["l3_wd2T"] = np.ascontiguousarray(np.asarray(inp["L3_attn_Wd2_w"])[es].T)
    d["l3_wd2b"] = np.ascontiguousarray(np.asarray(inp["L3_attn_Wd2_b"])[es].reshape(8, 128).T)
    d["l3_wbT"] = np.ascontiguousarray(np.asarray(inp["L3_attn_WB"]).T[es])
    d["l3_wcT"] = np.ascontiguousarray(np.asarray(inp["L3_attn_WC"]).T[es])
    d["l3_alog"] = np.ascontiguousarray(
        np.asarray(inp["L3_attn_A_log"])[es].reshape(8, 128, 16).transpose(1, 0, 2).reshape(128, 128))
    d["l3_WD"] = np.ascontiguousarray(np.asarray(inp["L3_attn_W_D"])[es].reshape(8, 128).T)
    d["l3_outT"] = np.ascontiguousarray(np.asarray(inp["L3_attn_out_w"]).T[es])
    return d


def declare_inputs(ctx, sample, names=None):
    nc = ctx.nc
    for k, a in sample.items():
        if names is not None and k not in names:
            continue
        dt = {np.dtype(np.float32): F32, np.dtype(np.int32): I32,
              np.dtype(BF): BF16}[a.dtype]
        ctx.io[k] = nc.dram_tensor(k, list(a.shape), dt, kind="ExternalInput").ap()


def declare_work(ctx, external=()):
    nc = ctx.nc
    ext = dict(e.split(":") for e in external)

    def dram(name, shape, dtype, shared=False):
        if name in ctx.io:
            return
        kind = {"in": "ExternalInput", "out": "ExternalOutput"}.get(ext.get(name), "Internal")
        space = "Shared" if (shared and ctx.fused) else "Local"
        ctx.io[name] = nc.dram_tensor(name, shape, dtype, kind=kind,
                                      addr_space=space).ap()

    if "out" not in ctx.io:
        ctx.io["out"] = nc.dram_tensor("out", [DT, 128, T], F32,
                                       kind="ExternalOutput").ap()
    dram("x_dram", [DT, 128, T], F32)
    dram("xn32_dram", [DT, 128, T], F32)
    for l in range(4):
        dram(f"g_dram{l}", [DT, 128, T], F32)
        dram(f"s_own{l}", [DT, 128, T], F32)
        dram(f"xn_sh{l}", [2, DT, 128, T], BF16, shared=True)
        dram(f"s_sh{l}", [2, DT, 128, T], F32, shared=True)
    dram("l1_dbc_sh", [2, 4, 16, S], F32, shared=True)
    for c in range(4):
        dram(f"l3_dbc_sh{c}", [2, 96, 1024], F32, shared=True)
    if ctx.fused:
        ctx.io["flags"] = nc.dram_tensor("flags", [2, 64 * 16], I32,
                                         addr_space="Shared").ap()


def setup_consts(ctx):
    c = ctx.consts
    c["ones128"] = ctx.sbuf("c_ones128", [128, 1], BF16)
    c["ones1"] = ctx.sbuf("c_ones1", [1, 128], F32)
    c["ones1b"] = ctx.sbuf("c_ones1b", [1, 128], BF16)
    c["cpool"] = ctx.sbuf("c_cpool", [128, 8], F32)
    if ctx.fused:
        c["nonce"] = ctx.sbuf("c_nonce", [1, 64], I32)
        c["pollbuf"] = ctx.sbuf("c_pollbuf", [1, 2], I32)
    ld = ctx.sem("const_ld")
    ld.inc(ctx.g.dma_start(out=c["ones128"][:], in_=ctx.io["ones128"]), 16)
    ld.inc(ctx.sp.dma_start(out=c["ones1"][:],
                            in_=ctx.io["ones128"].rearrange("a b -> b a")), 16)
    ld.inc(ctx.g.dma_start(out=c["ones1b"][:],
                           in_=ctx.io["ones128"].rearrange("a b -> b a")), 16)
    ld.inc(ctx.sp.dma_start(out=c["cpool"][:], in_=ctx.io["cpool"]), 16)
    if ctx.fused:
        ld.inc(ctx.sp.dma_start(out=c["nonce"][:], in_=ctx.io["nonce"]), 16)
    for e in (ctx.t, ctx.v, ctx.a, ctx.g, ctx.sp):
        ld.wait(e)


def emit_ln(ctx, layer, x_src, store_f32=False):
    nc, t, v, a, sp = ctx.nc, ctx.t, ctx.v, ctx.a, ctx.sp
    xn_sh = ctx.io[f"xn_sh{layer}"]
    xn32 = ctx.io.get("xn32_dram")
    with contextlib.ExitStack() as st:
        sb = lambda n, s, dt_: st.enter_context(nc.sbuf_tensor(f"ln{layer}{n}", s, dt_))
        xt = sb("xt", [128, DT * 512], F32)
        xw = sb("xw", [128, DT * 512], F32)
        xb = sb("xb", [128, DT * 512], BF16)
        sq = sb("sq", [128, DT * 512], BF16)
        rows = sb("rw", [1, 512 * 7], F32)
        xnt = sb("xn", [128, DT * 512], BF16)
        ps_s = st.enter_context(nc.psum_tensor(f"ln{layer}ps", [1, 1024], F32))
        ps_b = st.enter_context(nc.psum_tensor(f"ln{layer}pb", [128, 1024], F32))
        ld = ctx.sem(f"ln{layer}_ld")
        cv = ctx.sem(f"ln{layer}_cv")
        mm = ctx.sem(f"ln{layer}_mm")
        rw = ctx.sem(f"ln{layer}_rw")
        bc = ctx.sem(f"ln{layer}_bc")
        xn = ctx.sem(f"ln{layer}_xn")
        so = ctx.sem(f"ln{layer}_st")
        R = lambda i: rows[0:1, i * 512:(i + 1) * 512]
        for c in range(4):
            c0 = c * 512
            for k in range(DT):
                ld.inc(sp.dma_start(out=xt[:, k * 512:(k + 1) * 512],
                                    in_=x_src[k, :, c0:c0 + 512]), 16)
            ld.wait(v)
            ld.wait(a)
            for k in range(DT):
                s = slice(k * 512, (k + 1) * 512)
                cv.inc(v.tensor_copy(out=xb[:, s], in_=xt[:, s]))
                cv.inc(a.activation(out=sq[:, s], in_=xb[:, s], func=AF.Square))
            cv.wait(t)
            for k in range(DT):
                s = slice(k * 512, (k + 1) * 512)
                i1 = t.matmul(ps_s[0:1, 0:512], ctx.consts["ones128"][:], xb[:, s],
                              start=(k == 0), stop=(k == DT - 1))
                i2 = t.matmul(ps_s[0:1, 512:1024], ctx.consts["ones128"][:], sq[:, s],
                              start=(k == 0), stop=(k == DT - 1))
            mm.inc(i1)
            mm.inc(i2)
            mm.wait(v)
            v.tensor_scalar_mul(out=R(2), in0=ps_s[0:1, 0:512], scalar1=1.0 / D)
            v.tensor_scalar_mul(out=R(1), in0=ps_s[0:1, 512:1024], scalar1=1.0 / D)
            v.tensor_mul(out=R(0), in0=R(2), in1=R(2))
            rw.inc(v.tensor_sub(out=R(3), in0=R(1), in1=R(0)))
            rw.wait(a)
            a.activation(out=R(6), in_=R(3), func=AF.Ln,
                         bias=ctx.consts["cpool"][0:1, 0:1])
            rw.inc(a.activation(out=R(4), in_=R(6), func=AF.Exp,
                                scale=ctx.consts["cpool"][0:1, 1:2]))
            rw.wait(v)
            rw.inc(v.scalar_tensor_tensor(out=R(5), in0=R(2), scalar=-1.0, in1=R(4),
                                          op0=ALU.mult, op1=ALU.mult))
            rw.wait(t)
            bc.inc(t.matmul(ps_b[:, 0:512], ctx.consts["ones1"][:], R(4),
                            start=True, stop=True))
            bc.inc(t.matmul(ps_b[:, 512:1024], ctx.consts["ones1"][:], R(5),
                            start=True, stop=True))
            bc.wait(v)
            so.wait(v)
            for k in range(DT):
                s = slice(k * 512, (k + 1) * 512)
                v.tensor_mul(out=xw[:, s], in0=xt[:, s], in1=ps_b[:, 0:512])
                if store_f32:
                    v.tensor_add(out=xw[:, s], in0=xw[:, s], in1=ps_b[:, 512:1024])
                    xn.inc(v.tensor_copy(out=xnt[:, s], in_=xw[:, s]))
                else:
                    xn.inc(v.tensor_add(out=xnt[:, s], in0=xw[:, s], in1=ps_b[:, 512:1024]))
            xn.wait(sp)
            for k in range(DT):
                slot0 = xn_sh[0, k, :, c0:c0 + 512]
                slot1 = xn_sh[1, k, :, c0:c0 + 512]
                slot_dma(ctx, sp, so, (slot0, slot1), xnt[:, k * 512:(k + 1) * 512])
                if store_f32:
                    so.inc(sp.dma_start(out=xn32[k, :, c0:c0 + 512],
                                        in_=xw[:, k * 512:(k + 1) * 512]), 16)
        for _e in (t, v, a, sp, ctx.g):
            so.wait(_e)
        emit_flag_write(ctx, ctx.g, layer * 8)
    return so


def emit_ffn03(ctx, layer):
    L = layer
    nc, t, v, a, sp, g = ctx.nc, ctx.t, ctx.v, ctx.a, ctx.sp, ctx.g
    xn_sh = ctx.io[f"xn_sh{L}"]
    g_dram = ctx.io[f"g_dram{L}"]
    q1, s1 = ctx.io[f"l{L}f_q1"], ctx.io[f"l{L}f_s1"]
    q2, s2 = ctx.io[f"l{L}f_q2"], ctx.io[f"l{L}f_s2"]
    with contextlib.ExitStack() as st:
        sb = lambda n, s_, dt_: st.enter_context(nc.sbuf_tensor(f"f{L}{n}", s_, dt_))
        xnt = sb("xn", [128, DT * 512], BF16)
        hbuf = sb("h", [128, DT * T], BF16)
        sq = sb("sq", [128, DT * 512], BF16)
        rows = sb("rw", [1, 512 * 7], F32)
        acc = sb("acc", [128, DT * T], F32)
        w1q = sb("w1q", [128, 8 * 512], BF16)
        s1r = sb("s1r", [128, 8 * 512], BF16)
        w1d = sb("w1d", [128, 2 * 8 * 512], BF16)
        w2q = sb("w2q", [128, 4 * 1024], BF16)
        s2r = sb("s2r", [128, 4 * 1024], BF16)
        w2d = sb("w2d", [128, 2 * 4 * 1024], BF16)
        gact = sb("ga", [128, 2 * 4 * 512], BF16)
        ps_s = st.enter_context(nc.psum_tensor(f"f{L}ps", [1, 1024], F32))
        ps_b = st.enter_context(nc.psum_tensor(f"f{L}pb", [128, 1024], F32))
        ps1 = st.enter_context(nc.psum_tensor(f"f{L}p1", [128, 1024], F32))
        ps2 = st.enter_context(nc.psum_tensor(f"f{L}p2", [128, 1024], F32))
        ld = ctx.sem(f"f{L}_ld")
        wl = ctx.sem(f"f{L}_wl")
        dq = ctx.sem(f"f{L}_dq")
        cv = ctx.sem(f"f{L}_cv")
        mm = ctx.sem(f"f{L}_mm")
        rw = ctx.sem(f"f{L}_rw")
        bc = ctx.sem(f"f{L}_bc")
        m1 = ctx.sem(f"f{L}_m1")
        e1 = ctx.sem(f"f{L}_e1")
        m2 = ctx.sem(f"f{L}_m2")
        e2 = ctx.sem(f"f{L}_e2")
        go = ctx.sem(f"f{L}_go")
        so = ctx.sem(f"f{L}_so")
        R = lambda i: rows[0:1, i * 512:(i + 1) * 512]
        gcol_sb = sb("gc", [128, 16], F32)
        b1col_sb = sb("b1c", [128, 32], F32)
        b2col_sb = sb("b2c", [128, 8], F32)
        ld.inc(sp.dma_start(out=gcol_sb[:, 0:8], in_=ctx.io[f"l{L}f_g"]), 16)
        ld.inc(sp.dma_start(out=gcol_sb[:, 8:16], in_=ctx.io[f"l{L}f_b"]), 16)
        ld.inc(sp.dma_start(out=b1col_sb[:], in_=ctx.io[f"l{L}f_b1"]), 16)
        ld.inc(sp.dma_start(out=b2col_sb[:], in_=ctx.io[f"l{L}f_b2"]), 16)

        def load_own_xn(c0, e2s=None):
            e1.wait(sp)
            for k in range(DT):
                dst = xnt[:, k * 512:(k + 1) * 512]
                par = ctx.par_reg(sp)
                with sp.If_cmp(par, 0, "IS_EQ"):
                    ld.inc(sp.dma_start(out=dst, in_=xn_sh[0, k, :, c0:c0 + 512]), 16)
                with sp.Else():
                    sp.dma_start(out=dst, in_=xn_sh[1, k, :, c0:c0 + 512]).then_inc(ld.h, 16)

        for c in range(4):
            c0 = c * 512
            load_own_xn(c0)
            ld.wait(a)
            for k in range(DT):
                src = xnt[:, k * 512:(k + 1) * 512]
                cv.inc(a.activation(out=sq[:, k * 512:(k + 1) * 512], in_=src,
                                    func=AF.Square))
            cv.wait(t)
            ld.wait(t)
            for k in range(DT):
                src = xnt[:, k * 512:(k + 1) * 512]
                i1 = t.matmul(ps_s[0:1, 0:512], ctx.consts["ones128"][:], src,
                              start=(k == 0), stop=(k == DT - 1))
                i2 = t.matmul(ps_s[0:1, 512:1024], ctx.consts["ones128"][:],
                              sq[:, k * 512:(k + 1) * 512],
                              start=(k == 0), stop=(k == DT - 1))
            mm.inc(i1)
            mm.inc(i2)
            mm.wait(v)
            v.tensor_scalar_mul(out=R(2), in0=ps_s[0:1, 0:512], scalar1=1.0 / D)
            v.tensor_scalar_mul(out=R(1), in0=ps_s[0:1, 512:1024], scalar1=1.0 / D)
            v.tensor_mul(out=R(0), in0=R(2), in1=R(2))
            rw.inc(v.tensor_sub(out=R(3), in0=R(1), in1=R(0)))
            rw.wait(a)
            a.activation(out=R(6), in_=R(3), func=AF.Ln,
                         bias=ctx.consts["cpool"][0:1, 0:1])
            rw.inc(a.activation(out=R(4), in_=R(6), func=AF.Exp,
                                scale=ctx.consts["cpool"][0:1, 1:2]))
            rw.wait(v)
            rw.inc(v.scalar_tensor_tensor(out=R(5), in0=R(2), scalar=-1.0, in1=R(4),
                                          op0=ALU.mult, op1=ALU.mult))
            rw.wait(t)
            bc.inc(t.matmul(ps_b[:, 0:512], ctx.consts["ones1"][:], R(4),
                            start=True, stop=True))
            bc.inc(t.matmul(ps_b[:, 512:1024], ctx.consts["ones1"][:], R(5),
                            start=True, stop=True))
            bc.wait(v)
            if "dbg_rows" in ctx.io:
                rw.wait(sp)
                dr = ctx.sem(f"f{L}_dr")
                dr.inc(sp.dma_start(out=ctx.io["dbg_rows"][c, 0:1, :],
                                    in_=rows[0:1, :]), 16)
                dr.inc(sp.dma_start(out=ctx.io["dbg_xnt"][c, :, :],
                                    in_=xnt[:, 0:DT * 512]), 16)
                dr.wait(sp)
            for k in range(DT):
                src = xnt[:, k * 512:(k + 1) * 512]
                scr = sq[:, k * 512:(k + 1) * 512]
                dsth = hbuf[:, k * T + c0:k * T + c0 + 512]
                v.tensor_mul(out=scr, in0=src, in1=ps_b[:, 0:512])
                v.tensor_add(out=dsth, in0=scr, in1=ps_b[:, 512:1024])
                e1.inc(v.tensor_scalar(out=dsth, in0=dsth, scalar1=gcol_sb[:, k:k + 1],
                                       scalar2=gcol_sb[:, 8 + k:9 + k],
                                       op0=ALU.mult, op1=ALU.add))

        if "dbg_h" in ctx.io:
            e1.wait(sp)
            dbs = ctx.sem(f"f{L}_db")
            for kh in range(DT):
                dbs.inc(sp.dma_start(out=ctx.io["dbg_h"][kh, :, :],
                                     in_=hbuf[:, kh * T:(kh + 1) * T]), 16)
            dbs.wait(sp)
        NFB = 8
        for fb in range(NFB + 1):
            pb = fb % 2
            if fb < NFB:
                f0 = fb * 512
                dq.wait(g)
                for k in range(DT):
                    wl.inc(g.dma_start(out=w1q[:, k * 512:(k + 1) * 512],
                                       in_=q1[k * 128:(k + 1) * 128, f0:f0 + 512]), 16)
                    wl.inc(g.dma_start(
                        out=s1r[:, k * 512:(k + 1) * 512],
                        in_=s1[k:k + 1, f0:f0 + 512].broadcast_to((128, 512))), 16)
                for fk in range(4):
                    fr0 = f0 + fk * 128
                    wl.inc(g.dma_start(out=w2q[:, fk * 1024:(fk + 1) * 1024],
                                       in_=q2[fr0:fr0 + 128, :]), 16)
                    wl.inc(g.dma_start(
                        out=s2r[:, fk * 1024:(fk + 1) * 1024],
                        in_=s2[fr0 // 128:fr0 // 128 + 1, :].broadcast_to((128, 1024))), 16)
                wl.wait(v)
                m1.wait(v)
                m2.wait(v)
                for k in range(DT):
                    dq.inc(v.scalar_tensor_tensor(
                        out=w1d[:, (pb * 8 + k) * 512:(pb * 8 + k + 1) * 512],
                        in0=w1q[:, k * 512:(k + 1) * 512], scalar=-128.0,
                        in1=s1r[:, k * 512:(k + 1) * 512],
                        op0=ALU.add, op1=ALU.mult))
                for fk in range(4):
                    dq.inc(v.scalar_tensor_tensor(
                        out=w2d[:, (pb * 4 + fk) * 1024:(pb * 4 + fk + 1) * 1024],
                        in0=w2q[:, fk * 1024:(fk + 1) * 1024], scalar=-128.0,
                        in1=s2r[:, fk * 1024:(fk + 1) * 1024],
                        op0=ALU.add, op1=ALU.mult))
            if fb == 0:
                continue
            qb = (fb - 1) % 2
            fprev = (fb - 1) * 512
            dq.wait(t)
            e1.wait(t)
            for c in range(4):
                c0 = c * 512
                pc = c % 2
                for fk in range(4):
                    for k in range(DT):
                        i1 = t.matmul(ps1[:, pc * 512:pc * 512 + 512],
                                      w1d[:, (qb * 8 + k) * 512 + fk * 128:
                                          (qb * 8 + k) * 512 + fk * 128 + 128],
                                      hbuf[:, k * T + c0:k * T + c0 + 512],
                                      start=(k == 0), stop=(k == DT - 1))
                    m1.inc(i1)
                    m1.wait(a)
                    e1.inc(a.activation(
                        out=gact[:, (qb * 4 + fk) * 512:(qb * 4 + fk + 1) * 512],
                        in_=ps1[:, pc * 512:pc * 512 + 512], func=AF.Silu,
                        bias=b1col_sb[:, (fprev // 128 + fk):(fprev // 128 + fk) + 1]))
                    e1.wait(t)
                for kh in range(DT):
                    for fk in range(4):
                        i2 = t.matmul(ps2[:, pc * 512:pc * 512 + 512],
                                      w2d[:, (qb * 4 + fk) * 1024 + kh * 128:
                                          (qb * 4 + fk) * 1024 + kh * 128 + 128],
                                      gact[:, (qb * 4 + fk) * 512:(qb * 4 + fk + 1) * 512],
                                      start=(fk == 0), stop=(fk == 3))
                    m2.inc(i2)
                    m2.wait(v)
                    dsta = acc[:, kh * T + c0:kh * T + c0 + 512]
                    if fb == 1:
                        e2.inc(v.tensor_copy(out=dsta,
                                             in_=ps2[:, pc * 512:pc * 512 + 512]))
                    else:
                        e2.inc(v.tensor_add(out=dsta, in0=dsta,
                                            in1=ps2[:, pc * 512:pc * 512 + 512]))
                    e2.wait(t)

        if "dbg_acc" in ctx.io:
            e2.wait(sp)
            dbs2 = ctx.sem(f"f{L}_db2")
            for kh in range(DT):
                dbs2.inc(sp.dma_start(out=ctx.io["dbg_acc"][kh, :, :],
                                      in_=acc[:, kh * T:(kh + 1) * T]), 16)
            dbs2.wait(sp)
        e2.wait(v)
        for c in range(4):
            c0 = c * 512
            go.wait(sp)
            load_own_xn(c0)
            ld.wait(v)
            for kh in range(DT):
                sl = slice(kh * T + c0, kh * T + c0 + 512)
                v.tensor_scalar(out=acc[:, sl], in0=acc[:, sl],
                                scalar1=b2col_sb[:, kh:kh + 1], scalar2=0.5,
                                op0=ALU.add, op1=ALU.mult)
                go.inc(v.tensor_add(out=hbuf[:, sl], in0=acc[:, sl],
                                    in1=xnt[:, kh * 512:(kh + 1) * 512]))
            go.wait(a)
            so.wait(a)
            for kh in range(DT):
                sl = slice(kh * T + c0, kh * T + c0 + 512)
                go.inc(a.activation(out=acc[:, sl], in_=hbuf[:, sl], func=AF.Silu))
            go.wait(sp)
            for kh in range(DT):
                sl = slice(kh * T + c0, kh * T + c0 + 512)
                so.inc(sp.dma_start(out=g_dram[kh, :, c0:c0 + 512],
                                    in_=acc[:, sl]), 16)
        for _e in (t, v, a, sp, g):
            so.wait(_e)
    return so


def load_xn_chunk(ctx, eng, ld, xn_sh, dst, c0, war_sem=None, slot=None):
    if war_sem is not None:
        war_sem.wait(eng)
    for k in range(DT):
        d = dst[:, k * 512:(k + 1) * 512]
        if slot is not None:
            ld.inc(eng.dma_start(out=d, in_=xn_sh[slot, k, :, c0:c0 + 512]), 16)
        else:
            par = ctx.par_reg(eng)
            with eng.If_cmp(par, 0, "IS_EQ"):
                ld.inc(eng.dma_start(out=d, in_=xn_sh[0, k, :, c0:c0 + 512]), 16)
            with eng.Else():
                eng.dma_start(out=d, in_=xn_sh[1, k, :, c0:c0 + 512]).then_inc(ld.h, 16)


def emit_combine(ctx, layer, has_partner, bias_col=None, add_xn=False,
                 x_src_name="x_dram"):
    nc, v, sp, a = ctx.nc, ctx.v, ctx.sp, ctx.a
    L = layer
    x_dst = ctx.io["out"] if L == 3 else ctx.io["x_dram"]
    x_src = ctx.io[x_src_name]
    s_own = ctx.io[f"s_own{L}"]
    s_sh = ctx.io[f"s_sh{L}"]
    g_dram = ctx.io[f"g_dram{L}"]
    xn_sh = ctx.io[f"xn_sh{L}"]
    if has_partner:
        emit_poll(ctx, sp, L * 8 + 1)
    with contextlib.ExitStack() as st:
        sb = lambda n, s_, dt_: st.enter_context(nc.sbuf_tensor(f"cb{L}{n}", s_, dt_))
        xt = sb("x", [128, DT * 512], F32)
        st1 = sb("s1", [128, DT * 512], F32)
        st2 = sb("s2", [128, DT * 512], F32)
        gt = sb("g", [128, DT * 512], F32)
        xnt = sb("xn", [128, DT * 512], BF16) if add_xn else None
        stot = sb("st", [128, DT * 512], F32)
        bcol = None
        ld = ctx.sem(f"cb{L}_ld")
        dn = ctx.sem(f"cb{L}_dn")
        so = ctx.sem(f"cb{L}_so")
        if bias_col is not None:
            bcol = sb("bc", [128, 8], F32)
            ld.inc(sp.dma_start(out=bcol[:], in_=ctx.io[bias_col]), 16)
        for c in range(4):
            c0 = c * 512
            dn.wait(sp)
            so.wait(sp)
            for k in range(DT):
                s = slice(k * 512, (k + 1) * 512)
                ld.inc(sp.dma_start(out=xt[:, s], in_=x_src[k, :, c0:c0 + 512]), 16)
                ld.inc(sp.dma_start(out=st1[:, s], in_=s_own[k, :, c0:c0 + 512]), 16)
                ld.inc(sp.dma_start(out=gt[:, s], in_=g_dram[k, :, c0:c0 + 512]), 16)
                if has_partner:
                    other_slot_dma(ctx, sp, ld, st2[:, s],
                                   (s_sh[0, k, :, c0:c0 + 512], s_sh[1, k, :, c0:c0 + 512]))
            if add_xn:
                load_xn_chunk(ctx, sp, ld, xn_sh, xnt, c0)
            ld.wait(v)
            for k in range(DT):
                s = slice(k * 512, (k + 1) * 512)
                if has_partner:
                    v.tensor_add(out=stot[:, s], in0=st1[:, s], in1=st2[:, s])
                else:
                    v.tensor_copy(out=stot[:, s], in_=st1[:, s])
                if bias_col is not None:
                    v.tensor_scalar_add(out=stot[:, s], in0=stot[:, s],
                                        scalar1=bcol[:, k:k + 1])
                if add_xn:
                    v.tensor_add(out=stot[:, s], in0=stot[:, s], in1=xnt[:, s])
                if f"dbg_s{L}" in ctx.io:
                    dn.inc(v.tensor_copy(out=st1[:, s], in_=stot[:, s]))
                v.tensor_mul(out=stot[:, s], in0=stot[:, s], in1=gt[:, s])
                dn.inc(v.tensor_add(out=xt[:, s], in0=xt[:, s], in1=stot[:, s]))
            dn.wait(sp)
            if f"dbg_s{L}" in ctx.io:
                for k in range(DT):
                    s = slice(k * 512, (k + 1) * 512)
                    dn.inc(sp.dma_start(out=ctx.io[f"dbg_s{L}"][k, :, c0:c0 + 512],
                                        in_=st1[:, s]), 16)
                    dn.inc(sp.dma_start(out=ctx.io[f"dbg_g{L}"][k, :, c0:c0 + 512],
                                        in_=gt[:, s]), 16)
            for k in range(DT):
                s = slice(k * 512, (k + 1) * 512)
                so.inc(sp.dma_start(out=x_dst[k, :, c0:c0 + 512], in_=xt[:, s]), 16)
        for _e in (v, sp, a):
            so.wait(_e)
    return so


def emit_ffn_l2(ctx):
    nc, t, v, a, sp, g = ctx.nc, ctx.t, ctx.v, ctx.a, ctx.sp, ctx.g
    xn_sh = ctx.io["xn_sh2"]
    g_dram = ctx.io["g_dram2"]
    C_G = 0.044715
    with contextlib.ExitStack() as st:
        sb = lambda n, s_, dt_: st.enter_context(nc.sbuf_tensor(f"q2{n}", s_, dt_))
        xnt = sb("xn", [128, DT * 512], BF16)
        u1 = sb("u1", [128, DT * 256], BF16)
        v1 = sb("v1", [128, 2 * 4096], BF16)
        u2 = sb("u2", [128, 32 * 256], BF16)
        v2 = sb("v2", [128, 2 * 1024], BF16)
        b1c = sb("b1", [128, 32], F32)
        b2c = sb("b2", [128, 8], F32)
        pb = sb("pb", [128, 2 * 512], BF16)
        hb = sb("hb", [128, 32 * 512], BF16)
        qb = sb("qb", [128, 2 * 512], BF16)
        xg = sb("xg", [128, 512], BF16)
        x2 = sb("x2", [128, 512], BF16)
        wk = sb("wk", [128, 512], BF16)
        tn = sb("tn", [128, 512], BF16)
        ob = sb("ob", [128, DT * 512], BF16)
        ps_p = st.enter_context(nc.psum_tensor("q2pp", [128, 1024], F32))
        ps_h = st.enter_context(nc.psum_tensor("q2ph", [128, 1024], F32))
        ps_q = st.enter_context(nc.psum_tensor("q2pq", [128, 1024], F32))
        ps_o = st.enter_context(nc.psum_tensor("q2po", [128, 1024], F32))
        wl = ctx.sem("q2_wl")
        ld = ctx.sem("q2_ld")
        mp = ctx.sem("q2_mp")
        ep = ctx.sem("q2_ep")
        mh = ctx.sem("q2_mh")
        eh = ctx.sem("q2_eh")
        mq = ctx.sem("q2_mq")
        eq = ctx.sem("q2_eq")
        mo = ctx.sem("q2_mo")
        eo = ctx.sem("q2_eo")
        so = ctx.sem("q2_so")
        wl.inc(g.dma_start(out=u1[:].rearrange("p (a r) -> p a r", a=DT),
                           in_=ctx.io["l2f_u1"].rearrange("(a p) r -> p a r", p=128)), 16)
        wl.inc(g.dma_start(out=v1[:].rearrange("p (a f) -> p a f", a=2),
                           in_=ctx.io["l2f_v1"].rearrange("(a p) f -> p a f", p=128)), 16)
        wl.inc(g.dma_start(out=u2[:].rearrange("p (a r) -> p a r", a=32),
                           in_=ctx.io["l2f_u2"].rearrange("(a p) r -> p a r", p=128)), 16)
        wl.inc(g.dma_start(out=v2[:].rearrange("p (a d2) -> p a d2", a=2),
                           in_=ctx.io["l2f_v2"].rearrange("(a p) d2 -> p a d2", p=128)), 16)
        ld.inc(sp.dma_start(out=b1c[:], in_=ctx.io["l2f_b1"]), 16)
        ld.inc(sp.dma_start(out=b2c[:], in_=ctx.io["l2f_b2"]), 16)
        LAM = ctx.consts["cpool"][:, 2:3]
        for c in range(4):
            c0 = c * 512
            load_xn_chunk(ctx, sp, ld, xn_sh, xnt, c0, war_sem=mp)
            so.wait(v)
            so.wait(a)
            eo.wait(v)
            ld.wait(t)
            wl.wait(t)
            for r in range(2):
                for k in range(DT):
                    i1 = t.matmul(ps_p[:, r * 512:(r + 1) * 512],
                                  u1[:, k * 256 + r * 128:k * 256 + r * 128 + 128],
                                  xnt[:, k * 512:(k + 1) * 512],
                                  start=(k == 0), stop=(k == DT - 1))
                mp.inc(i1)
            mp.wait(v)
            for r in range(2):
                ep.inc(v.tensor_copy(out=pb[:, r * 512:(r + 1) * 512],
                                     in_=ps_p[:, r * 512:(r + 1) * 512]))
            ep.wait(t)
            for f in range(32):
                pf = f % 2
                for r in range(2):
                    i2 = t.matmul(ps_h[:, pf * 512:(pf + 1) * 512],
                                  v1[:, r * 4096 + f * 128:r * 4096 + f * 128 + 128],
                                  pb[:, r * 512:(r + 1) * 512],
                                  start=(r == 0), stop=(r == 1))
                mh.inc(i2)
                mh.wait(v)
                eh.inc(v.tensor_scalar_add(out=xg[:], in0=ps_h[:, pf * 512:(pf + 1) * 512],
                                           scalar1=b1c[:, f:f + 1]))
                eh.wait(a)
                eh.inc(a.activation(out=x2[:], in_=xg[:], func=AF.Square))
                eh.wait(v)
                v.tensor_scalar(out=wk[:], in0=x2[:], scalar1=C_G, scalar2=1.0,
                                op0=ALU.mult, op1=ALU.add)
                eh.inc(v.tensor_mul(out=wk[:], in0=wk[:], in1=xg[:]))
                eh.wait(a)
                eh.inc(a.activation(out=tn[:], in_=wk[:], func=AF.Tanh, scale=LAM))
                eh.wait(v)
                v.tensor_scalar(out=tn[:], in0=tn[:], scalar1=1.0, scalar2=0.5,
                                op0=ALU.add, op1=ALU.mult)
                eh.inc(v.tensor_mul(out=hb[:, f * 512:(f + 1) * 512], in0=tn[:],
                                    in1=xg[:]))
                eh.wait(t)
            for r in range(2):
                for f in range(32):
                    i3 = t.matmul(ps_q[:, r * 512:(r + 1) * 512],
                                  u2[:, f * 256 + r * 128:f * 256 + r * 128 + 128],
                                  hb[:, f * 512:(f + 1) * 512],
                                  start=(f == 0), stop=(f == 31))
                mq.inc(i3)
            mq.wait(v)
            for r in range(2):
                eq.inc(v.tensor_copy(out=qb[:, r * 512:(r + 1) * 512],
                                     in_=ps_q[:, r * 512:(r + 1) * 512]))
            eq.wait(t)
            for kh in range(DT):
                po = kh % 2
                for r in range(2):
                    i4 = t.matmul(ps_o[:, po * 512:(po + 1) * 512],
                                  v2[:, r * 1024 + kh * 128:r * 1024 + kh * 128 + 128],
                                  qb[:, r * 512:(r + 1) * 512],
                                  start=(r == 0), stop=(r == 1))
                mo.inc(i4)
                mo.wait(v)
                eo.inc(v.tensor_scalar_add(out=ob[:, kh * 512:(kh + 1) * 512],
                                           in0=ps_o[:, po * 512:(po + 1) * 512],
                                           scalar1=b2c[:, kh:kh + 1]))
                eo.wait(t)
            eo.wait(a)
            for kh in range(DT):
                eo.inc(a.activation(out=hb[:, kh * 512:(kh + 1) * 512],
                                    in_=ob[:, kh * 512:(kh + 1) * 512], func=AF.Silu))
            eo.wait(g)
            for kh in range(DT):
                so.inc(g.dma_start(out=g_dram[kh, :, c0:c0 + 512],
                                    in_=hb[:, kh * 512:(kh + 1) * 512]), 16)
        for _e in (t, v, a, sp, g):
            so.wait(_e)
    return so


def emit_moe(ctx):
    nc, t, v, a, sp, g = ctx.nc, ctx.t, ctx.v, ctx.a, ctx.sp, ctx.g
    xn_sh = ctx.io["xn_sh1"]
    g_dram = ctx.io["g_dram1"]
    with contextlib.ExitStack() as st:
        sb = lambda n, s_, dt_: st.enter_context(nc.sbuf_tensor(f"mo{n}", s_, dt_))
        xnb = sb("xn", [128, DT * T], BF16)
        selrow = sb("sr", [8, T], BF16)
        OUT = sb("out", [128, DT * T], BF16)
        acc = sb("acc", [128, DT * T], BF16)
        b2sb = sb("b2", [8, 1024], BF16)
        gbc = sb("gb", [8, 1], F32)
        sel8 = sb("s8", [8, 1024], BF16)
        idb = sb("idb", [128, 128], BF16)
        mask = sb("mk", [128, 512], U8)
        m1c = sb("m1c", [128, 1], F32)
        ge_t = sb("ge", [128, 8], F32)
        selt = sb("sel", [128, 8], BF16)
        ntk = sb("ntk", [128, 1], BF16)
        ld = ctx.sem("mo_ld")
        wl = ctx.sem("mo_wl")
        gm = ctx.sem("mo_gm")
        gv = ctx.sem("mo_gv")
        m1 = ctx.sem("mo_m1")
        e1 = ctx.sem("mo_e1")
        m2 = ctx.sem("mo_m2")
        e2 = ctx.sem("mo_e2")
        mg = ctx.sem("mo_mg")
        so = ctx.sem("mo_so")
        ld.inc(sp.dma_start(out=gbc[:], in_=ctx.io["moe_gb"]), 16)
        ld.inc(g.dma_start(out=sel8[:], in_=ctx.io["moe_sel8"]), 16)
        ld.inc(g.dma_start(out=idb[:], in_=ctx.io["id128"]), 16)
        ld.inc(g.dma_start(out=b2sb[:], in_=ctx.io["moe_b2"]), 16)
        for c in range(4):
            c0 = c * 512
            for k in range(DT):
                dst = xnb[:, k * T + c0:k * T + c0 + 512]
                par = ctx.par_reg(sp)
                with sp.If_cmp(par, 0, "IS_EQ"):
                    ld.inc(sp.dma_start(out=dst, in_=xn_sh[0, k, :, c0:c0 + 512]), 16)
                with sp.Else():
                    sp.dma_start(out=dst, in_=xn_sh[1, k, :, c0:c0 + 512]).then_inc(ld.h, 16)
        with contextlib.ExitStack() as st2:
            ps_g = st2.enter_context(nc.psum_tensor("mopg", [8, 512], F32))
            ps_t1 = st2.enter_context(nc.psum_tensor("mopt1", [128, 8], F32))
            ps_t2 = st2.enter_context(nc.psum_tensor("mopt2", [8, 128], F32))
            idf = st2.enter_context(nc.sbuf_tensor("moidf", [128, 128], F32))
            lgf = st2.enter_context(nc.sbuf_tensor("molgf", [8, T], F32))
            gw = st2.enter_context(nc.sbuf_tensor("mogw", [128, DT * 8], F32))
            xn32 = st2.enter_context(nc.sbuf_tensor("mox32", [128, DT * 512], F32))
            ld.inc(sp.dma_start(out=gw[:].rearrange("p (a e) -> p a e", a=DT),
                                in_=ctx.io["moe_gwT"].rearrange("(a p) e -> p a e", p=128)), 16)
            ld.inc(sp.dma_start(out=idf[:], in_=ctx.io["id128"]), 16)
            ld.wait(t)
            for c in range(4):
                c0 = c * 512
                gm.wait(sp)
                for k in range(DT):
                    ld.inc(sp.dma_start(out=xn32[:, k * 512:(k + 1) * 512],
                                        in_=ctx.io["xn32_dram"][k, :, c0:c0 + 512]), 16)
                ld.wait(t)
                for k in range(DT):
                    i1 = t.matmul(ps_g[:, 0:512], gw[:, k * 8:(k + 1) * 8],
                                  xn32[:, k * 512:(k + 1) * 512],
                                  start=(k == 0), stop=(k == DT - 1))
                gm.inc(i1)
                gm.wait(v)
                gv.inc(v.tensor_scalar_add(out=lgf[0:8, c0:c0 + 512], in0=ps_g[:, 0:512],
                                           scalar1=gbc[:, 0:1]))
                gv.wait(t)
            tr8 = st2.enter_context(nc.sbuf_tensor("motr8", [8, 8], F32))
            geT = st2.enter_context(nc.sbuf_tensor("mogeT", [8, 128], F32))
            eqb = st2.enter_context(nc.sbuf_tensor("moeq", [8, 128], F32))
            ps_t3 = st2.enter_context(nc.psum_tensor("mopt3", [8, 128], F32))
            ld.inc(sp.dma_start(out=tr8[:], in_=ctx.io["tri8"]), 16)
            ld.wait(t)
            ld.wait(v)
            for stt in range(16):
                t0 = stt * 128
                gv.wait(t)
                gm.inc(t.transpose(ps_t1[:, :], lgf[0:8, t0:t0 + 128], idf[0:8, 0:8]))
                gm.wait(v)
                gv.inc(v.reduce_max(out=m1c[:], in_=ps_t1[:, :],
                                    axis=mybir.AxisListType.X))
                gv.wait(v)
                gv.inc(v.tensor_single_scalar(out=ge_t[:], in_=ps_t1[:, :],
                                              scalar=m1c[:, 0:1], op=ALU.is_ge))
                gv.wait(t)
                gm.inc(t.transpose(ps_t2[:, :], ge_t[:], idf[:]))
                gm.wait(v)
                gv.inc(v.tensor_copy(out=geT[:], in_=ps_t2[:, :]))
                gv.wait(t)
                gm.inc(t.matmul(ps_t3[:, :], tr8[:], geT[:], start=True, stop=True))
                gm.wait(v)
                v.tensor_single_scalar(out=eqb[:], in_=ps_t3[:, :], scalar=0.0,
                                       op=ALU.is_equal)
                gv.inc(v.tensor_mul(out=selrow[0:8, t0:t0 + 128], in0=eqb[:],
                                    in1=geT[:]))
            gv.wait(t)
            if "dbg_sel" in ctx.io:
                gv.wait(sp)
                dbs = ctx.sem("mo_dbs")
                dbs.inc(sp.dma_start(out=ctx.io["dbg_sel"], in_=selrow[0:8, :]), 16)
                dbs.inc(sp.dma_start(out=ctx.io["dbg_lgf"], in_=lgf[0:8, :]), 16)
                dbs.wait(sp)
        for _e in (g, sp, t, v, a):
            gv.wait(_e)
            gm.wait(_e)
            ld.wait(_e)
        w1_io = ctx.io["moe_w1"]
        w2_io = ctx.io["moe_w2"]
        w1h = st.enter_context(nc.sbuf_tensor("mow1", [128, 8 * 2048], BF16))
        w2h = st.enter_context(nc.sbuf_tensor("mow2", [128, 16 * 1024], BF16))
        gel = st.enter_context(nc.sbuf_tensor("mogel", [128, 16 * 512], BF16))
        b1c = st.enter_context(nc.sbuf_tensor("mob1", [128, 32], F32))
        ps1 = st.enter_context(nc.psum_tensor("mop1", [128, 1024], F32))
        ps2 = st.enter_context(nc.psum_tensor("mop2", [128, 1024], F32))
        ps_bc = st.enter_context(nc.psum_tensor("mopbc", [128, 512], F32))
        ps_b2 = st.enter_context(nc.psum_tensor("mopb2", [128, 512], F32))
        for e in range(8):
            e1.wait(sp)
            ld.inc(sp.dma_start(out=b1c[:], in_=ctx.io["moe_b1"][e]), 16)
            for fh in range(2):
                fb = fh * 2048
                m1.wait(g)
                for k in range(DT):
                    wl.inc(g.dma_start(out=w1h[:, k * 2048:(k + 1) * 2048],
                                       in_=w1_io[e, k * 128:(k + 1) * 128, fb:fb + 2048]), 16)
                wl.wait(t)
                ld.wait(a)
                m2.wait(g)
                for f in range(16):
                    wl.inc(g.dma_start(out=w2h[:, f * 1024:(f + 1) * 1024],
                                       in_=w2_io[e, fb + f * 128:fb + f * 128 + 128, :]), 16)
                for c in range(4):
                    c0 = c * 512
                    for f in range(16):
                        pf = f % 2
                        for k in range(DT):
                            i1 = t.matmul(ps1[:, pf * 512:(pf + 1) * 512],
                                          w1h[:, k * 2048 + f * 128:k * 2048 + f * 128 + 128],
                                          xnb[:, k * T + c0:k * T + c0 + 512],
                                          start=(k == 0), stop=(k == DT - 1))
                        m1.inc(i1)
                        m1.wait(a)
                        e1.inc(a.activation(out=gel[:, f * 512:(f + 1) * 512],
                                            in_=ps1[:, pf * 512:(pf + 1) * 512],
                                            func=AF.Gelu,
                                            bias=b1c[:, fh * 16 + f:fh * 16 + f + 1]))
                        e1.wait(t, e1.v - 1)
                    if e == 0 and fh == 1 and c == 0 and "dbg_gel" in ctx.io:
                        e1.wait(sp)
                        dbg = ctx.sem("mo_dbg2")
                        dbg.inc(sp.dma_start(out=ctx.io["dbg_gel"], in_=gel[:, 0:8192]), 16)
                        dbg.inc(sp.dma_start(out=ctx.io["dbg_w1h"], in_=w1h[:, 0:2048]), 16)
                        dbg.inc(sp.dma_start(out=ctx.io["dbg_xnb"], in_=xnb[:, 0:512]), 16)
                        dbg.wait(sp)
                    if c == 0:
                        wl.wait(t)
                    e1.wait(t)
                    for kh in range(DT):
                        pk = kh % 2
                        for f in range(16):
                            i2 = t.matmul(ps2[:, pk * 512:(pk + 1) * 512],
                                          w2h[:, f * 1024 + kh * 128:f * 1024 + kh * 128 + 128],
                                          gel[:, f * 512:(f + 1) * 512],
                                          start=(f == 0), stop=(f == 15))
                        m2.inc(i2)
                        m2.wait(v)
                        dsta = acc[:, kh * T + c0:kh * T + c0 + 512]
                        if fh == 0:
                            e2.inc(v.tensor_copy(out=dsta,
                                                 in_=ps2[:, pk * 512:(pk + 1) * 512]))
                        else:
                            if kh == 0:
                                mg.inc(t.matmul(ps_bc[:, 0:512],
                                                sel8[0:8, e * 128:(e + 1) * 128],
                                                selrow[0:8, c0:c0 + 512],
                                                start=True, stop=True))
                                mg.wait(v)
                                e2.inc(v.tensor_copy(out=mask[:], in_=ps_bc[:, 0:512]))
                                e2.wait(t)
                            v.tensor_add(out=dsta, in0=dsta,
                                         in1=ps2[:, pk * 512:(pk + 1) * 512])
                            e2.inc(v.copy_predicated(out=OUT[:, kh * T + c0:kh * T + c0 + 512],
                                                     mask=mask[:], data=dsta))
                        e2.wait(t, e2.v - 1)
                if fh == 1 and e == 0 and "dbg_acc" in ctx.io:
                    e2.wait(sp)
                    dba = ctx.sem("mo_dba")
                    dba.inc(sp.dma_start(out=ctx.io["dbg_acc"], in_=acc[:, 0:T]), 16)
                    dba.wait(sp)
                if fh == 1 and e == 0 and "dbg_mask" in ctx.io:
                    e2.wait(sp)
                    dbm = ctx.sem("mo_dbm")
                    dbm.inc(sp.dma_start(out=ctx.io["dbg_mask"], in_=mask[:]), 16)
                    dbm.inc(sp.dma_start(out=ctx.io["dbg_out0"],
                                         in_=OUT[:, 0:T]), 16)
                    dbm.wait(sp)
        e2.wait(v)
        for c in range(4):
            c0 = c * 512
            for kh in range(DT):
                mg.inc(t.matmul(ps_b2[:, 0:512], b2sb[0:8, kh * 128:(kh + 1) * 128],
                                selrow[0:8, c0:c0 + 512], start=True, stop=True))
                mg.wait(v)
                sl = slice(kh * T + c0, kh * T + c0 + 512)
                e2.inc(v.tensor_add(out=OUT[:, sl], in0=OUT[:, sl], in1=ps_b2[:, 0:512]))
                e2.wait(t)
        e2.wait(a)
        for c in range(4):
            c0 = c * 512
            for kh in range(DT):
                sl = slice(kh * T + c0, kh * T + c0 + 512)
                so.inc(a.activation(out=acc[:, sl], in_=OUT[:, sl], func=AF.Silu))
        so.wait(g)
        for kh in range(DT):
            so.inc(g.dma_start(out=g_dram[kh, :, :], in_=acc[:, kh * T:(kh + 1) * T]), 16)
        for _e in (t, v, a, sp, g):
            so.wait(_e)
            e2.wait(_e)
            mg.wait(_e)
    return so


def emit_attn(ctx, layer):
    L = layer
    nc, t, v, a, sp, g = ctx.nc, ctx.t, ctx.v, ctx.a, ctx.sp, ctx.g
    xn_sh = ctx.io[f"xn_sh{L}"]
    has_bias = (L == 0)
    has_rope = (L == 2)
    pfx = "l0_" if L == 0 else "l2_"
    with contextlib.ExitStack() as st:
        sb = lambda n, s_, dt_: st.enter_context(nc.sbuf_tensor(f"at{L}{n}", s_, dt_))
        QT = sb("qt", [128, 4 * S], BF16)
        KT = sb("kt", [128, 4 * S], BF16)
        VB = sb("vb", [128, 32 * 512], BF16)
        OT = sb("ot", [128, 4 * S], BF16)
        nbnd = sb("nb", [128, 8], F32)
        ld = ctx.sem(f"at{L}_ld")
        wl = ctx.sem(f"at{L}_wl")
        mm = ctx.sem(f"at{L}_mm")
        ev = ctx.sem(f"at{L}_ev")
        sc = ctx.sem(f"at{L}_sc")
        ex = ctx.sem(f"at{L}_ex")
        pv = ctx.sem(f"at{L}_pv")
        on = ctx.sem(f"at{L}_on")
        om = ctx.sem(f"at{L}_om")
        so = ctx.sem(f"at{L}_so")
        with contextlib.ExitStack() as stA:
            sba = lambda n, s_, dt_: stA.enter_context(nc.sbuf_tensor(f"atA{L}{n}", s_, dt_))
            wq = sba("wq", [128, DT * 512], BF16)
            wk = sba("wk", [128, DT * 512], BF16)
            wv = sba("wv", [128, DT * 512], BF16)
            xnt = sba("xn", [128, DT * 512], BF16)
            ps_q = stA.enter_context(nc.psum_tensor(f"atA{L}q", [128, 1024], F32))
            ps_v = stA.enter_context(nc.psum_tensor(f"atA{L}v", [128, 1024], F32))
            ps_r = stA.enter_context(nc.psum_tensor(f"atA{L}r", [128, 512], F32))
            for nm, dst in (("wqT", wq), ("wkT", wk), ("wvT", wv)):
                wl.inc(g.dma_start(out=dst[:].rearrange("p (k m) -> p k m", k=DT),
                                   in_=ctx.io[pfx + nm].rearrange("(k p) m -> p k m", p=128)), 16)
            bq = bk = bvr = None
            if has_bias:
                bq = sba("bq", [128, 4], F32)
                bk = sba("bk", [128, 4], F32)
                bvr = sba("bv", [128, 512], F32)
                ld.inc(sp.dma_start(out=bq[:], in_=ctx.io["l0_bq"]), 16)
                ld.inc(sp.dma_start(out=bk[:], in_=ctx.io["l0_bk"]), 16)
                ld.inc(sp.dma_start(
                    out=bvr[:],
                    in_=ctx.io["l0_bv_row"].broadcast_to((128, 512))), 16)
            cosr = sinr = ropeP = None
            if has_rope:
                cosr = sba("cos", [128, S], BF16)
                sinr = sba("sin", [128, S], BF16)
                ropeP = sba("rp", [128, 128], BF16)
                rsc = sba("rs", [128, 1024], BF16)
                wl.inc(g.dma_start(out=cosr[:], in_=ctx.io["rope_cos"]), 16)
                wl.inc(g.dma_start(out=sinr[:], in_=ctx.io["rope_sin"]), 16)
                wl.inc(g.dma_start(out=ropeP[:], in_=ctx.io["ropeP"]), 16)
            emit_poll(ctx, sp, L * 8)
            for ch in range(8):
                slot, c0 = ch // 4, (ch % 4) * 512
                a0 = ch * 512
                ev.wait(sp)
                for k in range(DT):
                    ld.inc(sp.dma_start(out=xnt[:, k * 512:(k + 1) * 512],
                                        in_=xn_sh[slot, k, :, c0:c0 + 512]), 16)
                ld.wait(t)
                wl.wait(t)
                for which, wT, bcol, dstT in (("q", wq, bq, QT), ("k", wk, bk, KT)):
                    for hp in range(4):
                        reg = (hp % 2) * 512
                        for k in range(DT):
                            i1 = t.matmul(ps_q[:, reg:reg + 512],
                                          wT[:, k * 512 + hp * 128:k * 512 + hp * 128 + 128],
                                          xnt[:, k * 512:(k + 1) * 512],
                                          start=(k == 0), stop=(k == DT - 1))
                        mm.inc(i1)
                        dcol = slice(hp * S + a0, hp * S + a0 + 512)
                        if not has_rope:
                            mm.wait(v)
                            if has_bias:
                                ev.inc(v.tensor_scalar_add(out=dstT[:, dcol],
                                                           in0=ps_q[:, reg:reg + 512],
                                                           scalar1=bcol[:, hp:hp + 1]))
                            else:
                                ev.inc(v.tensor_copy(out=dstT[:, dcol],
                                                     in_=ps_q[:, reg:reg + 512]))
                            ev.wait(t)
                        else:
                            mm.wait(v)
                            v.tensor_mul(out=rsc[:, 0:512], in0=ps_q[:, reg:reg + 512],
                                         in1=sinr[:, a0:a0 + 512])
                            ev.inc(v.tensor_mul(out=rsc[:, 512:1024],
                                                in0=ps_q[:, reg:reg + 512],
                                                in1=cosr[:, a0:a0 + 512]))
                            ev.wait(t)
                            mm.inc(t.matmul(ps_r[:, 0:512], ropeP[:], rsc[:, 0:512],
                                            start=True, stop=True))
                            mm.wait(v)
                            ev.inc(v.tensor_add(out=dstT[:, dcol], in0=rsc[:, 512:1024],
                                                in1=ps_r[:, 0:512]))
                            ev.wait(t)
                for tt in range(4):
                    for half2 in range(2):
                        reg = half2 * 256
                        for k in range(DT):
                            i2 = t.matmul(ps_v[:, reg:reg + 256],
                                          xnt[:, k * 512 + tt * 128:k * 512 + tt * 128 + 128],
                                          wv[:, k * 512 + half2 * 256:k * 512 + half2 * 256 + 256],
                                          start=(k == 0), stop=(k == DT - 1))
                        mm.inc(i2)
                    mm.wait(v)
                    dst = VB[:, (ch * 4 + tt) * 512:(ch * 4 + tt) * 512 + 512]
                    if has_bias:
                        v.tensor_add(out=dst[:, 0:256], in0=ps_v[:, 0:256],
                                     in1=bvr[:, 0:256])
                        ev.inc(v.tensor_add(out=dst[:, 256:512], in0=ps_v[:, 256:512],
                                            in1=bvr[:, 256:512]))
                    else:
                        v.tensor_copy(out=dst[:, 0:256], in_=ps_v[:, 0:256])
                        ev.inc(v.tensor_copy(out=dst[:, 256:512], in_=ps_v[:, 256:512]))
                    ev.wait(t)
            for _e in (t, v, a, sp, g):
                ev.wait(_e)
                mm.wait(_e)
                ld.wait(_e)
                wl.wait(_e)
        with contextlib.ExitStack() as stB:
            sbb = lambda n, s_, dt_: stB.enter_context(nc.sbuf_tensor(f"atB{L}{n}", s_, dt_))
            hsel = sbb("hs", [128, 2], BF16)
            sqv = sbb("sq", [128, 512], BF16)
            nrow = sbb("nr", [1, 4 * 512], F32)
            mrow = sbb("mr", [1, 32], F32)
            ps_n = stB.enter_context(nc.psum_tensor(f"atB{L}n", [1, 512], F32))
            ps_bc = stB.enter_context(nc.psum_tensor(f"atB{L}b", [128, 8], F32))
            wl.inc(g.dma_start(out=hsel[:], in_=ctx.io["half_sel"]), 16)
            wl.wait(t)
            bs = ctx.sem(f"at{L}_bs")
            for hp in range(4):
                for ab in range(2):
                    h8 = hp * 2 + ab
                    for which, srcT in (("q", QT), ("k", KT)):
                        for cc in range(8):
                            colr = slice(hp * S + cc * 512, hp * S + cc * 512 + 512)
                            mm.wait(a)
                            ex.inc(a.activation(out=sqv[:], in_=srcT[:, colr],
                                                func=AF.Square))
                            ex.wait(t)
                            bs.wait(t)
                            i3 = t.matmul(ps_n[0:1, 0:512], hsel[:, ab:ab + 1], sqv[:],
                                          start=True, stop=True)
                            mm.inc(i3)
                            mm.wait(v)
                            bs.inc(v.reduce_max(out=mrow[0:1, cc:cc + 1],
                                                in_=ps_n[0:1, 0:512],
                                                axis=mybir.AxisListType.X))
                        idx = 16 if which == "q" else 24
                        bs.inc(v.reduce_max(out=mrow[0:1, idx + h8:idx + h8 + 1],
                                            in_=mrow[0:1, 0:8], axis=mybir.AxisListType.X))
            bs.wait(v)
            bs.inc(v.tensor_mul(out=mrow[0:1, 0:8], in0=mrow[0:1, 16:24],
                                in1=mrow[0:1, 24:32]))
            bs.wait(a)
            a.activation(out=mrow[0:1, 8:16], in_=mrow[0:1, 0:8], func=AF.Sqrt)
            ex.inc(a.activation(out=mrow[0:1, 8:16], in_=mrow[0:1, 8:16],
                                func=AF.Copy, scale=-0.125))
            ex.wait(t)
            i4 = t.matmul(ps_bc[:, 0:8], ctx.consts["ones1"][:], mrow[0:1, 8:16],
                          start=True, stop=True)
            mm.inc(i4)
            mm.wait(v)
            on.inc(v.tensor_copy(out=nbnd[:], in_=ps_bc[:, 0:8]))
            for _e in (t, v, a, sp, g):
                on.wait(_e)
                bs.wait(_e)
                ex.wait(_e)
                mm.wait(_e)
        with contextlib.ExitStack() as stC:
            sbc = lambda n, s_, dt_: stC.enter_context(nc.sbuf_tensor(f"atC{L}{n}", s_, dt_))
            cmk = sbc("cm", [128, 4 * 512], BF16)
            expa = sbc("ea", [128, 512], BF16)
            expb = sbc("eb", [128, 512], BF16)
            scr = sbc("sc", [128, 512], F32)
            reca = sbc("ra", [1, 512], F32)
            recb = sbc("rb", [1, 512], F32)
            onrm = sbc("on", [128, 512], BF16)
            ps_sa = stC.enter_context(nc.psum_tensor(f"atC{L}a", [128, 512], F32))
            ps_sb2 = stC.enter_context(nc.psum_tensor(f"atC{L}b", [128, 512], F32))
            ps_o = stC.enter_context(nc.psum_tensor(f"atC{L}o", [128, 512], F32))
            ps_m = stC.enter_context(nc.psum_tensor(f"atC{L}m", [1, 1024], F32))
            ps_rep = stC.enter_context(nc.psum_tensor(f"atC{L}r", [128, 512], F32))
            wl.inc(g.dma_start(out=cmk[:], in_=ctx.io["cmask"]), 16)
            wl.wait(t)
            wl.wait(v)
            if "dbg_nb" in ctx.io:
                onb = ctx.sem(f"at{L}_onb")
                on.wait(sp)
                onb.inc(sp.dma_start(out=ctx.io["dbg_nb"], in_=nbnd[:]), 16)
                onb.wait(sp)
            for hp in range(4):
                base_a, base_b = 0, 64
                for qt in range(8):
                    q0 = qt * 512
                    nkt = (q0 + 512) // 128
                    for kt in range(nkt):
                        k0 = kt * 128
                        diag = k0 >= q0
                        ex.wait(t)
                        i5 = t.matmul(ps_sa[:, 0:512],
                                      KT[0:64, hp * S + k0:hp * S + k0 + 128],
                                      QT[0:64, hp * S + q0:hp * S + q0 + 512],
                                      start=True, stop=True)
                        sc.inc(i5)
                        i6 = t.matmul(ps_sb2[:, 0:512],
                                      KT[64:128, hp * S + k0:hp * S + k0 + 128],
                                      QT[64:128, hp * S + q0:hp * S + q0 + 512],
                                      start=True, stop=True)
                        sc.inc(i6)
                        sc.wait(a)
                        pv.wait(a)
                        ex.inc(a.activation(out=expa[:], in_=ps_sa[:, 0:512], func=AF.Exp,
                                            scale=0.125, bias=nbnd[:, hp * 2:hp * 2 + 1]))
                        ex.inc(a.activation(out=expb[:], in_=ps_sb2[:, 0:512], func=AF.Exp,
                                            scale=0.125, bias=nbnd[:, hp * 2 + 1:hp * 2 + 2]))
                        if diag:
                            j = (k0 - q0) // 128
                            ex.wait(v)
                            pv.wait(v)
                            v.tensor_mul(out=expa[:], in0=expa[:],
                                         in1=cmk[:, j * 512:(j + 1) * 512])
                            ex.inc(v.tensor_mul(out=expb[:], in0=expb[:],
                                                in1=cmk[:, j * 512:(j + 1) * 512]))
                        if "dbg_exp" in ctx.io and hp == 0 and qt == 0:
                            dxe = ctx.sem(f"at{L}_dxe")
                            ex.wait(sp)
                            dxe.inc(sp.dma_start(
                                out=ctx.io["dbg_exp"][kt], in_=expa[:]), 16)
                            dxe.wait(sp)
                        ex.wait(t)
                        i7 = t.matmul(ps_o[0:64, 0:512],
                                      VB[:, kt * 512 + hp * 128:kt * 512 + hp * 128 + 64],
                                      expa[:], start=(kt == 0), stop=(kt == nkt - 1))
                        i8 = t.matmul(ps_o[64:128, 0:512],
                                      VB[:, kt * 512 + hp * 128 + 64:kt * 512 + hp * 128 + 128],
                                      expb[:], start=(kt == 0), stop=(kt == nkt - 1),
                                      tile_position=(0, 64))
                        i9 = t.matmul(ps_m[0:1, 0:512], ctx.consts["ones128"][:], expa[:],
                                      start=(kt == 0), stop=(kt == nkt - 1))
                        i10 = t.matmul(ps_m[0:1, 512:1024], ctx.consts["ones128"][:], expb[:],
                                       start=(kt == 0), stop=(kt == nkt - 1))
                        pv.inc(i7)
                        pv.inc(i8)
                        pv.inc(i9)
                        pv.inc(i10)
                    if "dbg_pm" in ctx.io and hp == 0 and qt == 0:
                        pv.wait(v)
                        dpm = ctx.sem(f"at{L}_dpm")
                        dpm.inc(v.tensor_copy(out=scr[:, 0:512], in_=ps_o[:, 0:512]))
                        dpm.wait(sp)
                        dpm.inc(sp.dma_start(out=ctx.io["dbg_po"], in_=scr[:, 0:512]), 16)
                        pv.wait(a)
                        dpm.inc(a.activation(out=scr[0:1, 0:512], in_=ps_m[0:1, 0:512],
                                             func=AF.Copy))
                        dpm.wait(sp)
                        dpm.inc(sp.dma_start(out=ctx.io["dbg_pm"], in_=scr[0:1, 0:512]), 16)
                        dpm.wait(sp)
                        dpm.wait(t)
                        dpm.wait(a)
                    pv.wait(a)
                    a.activation(out=reca[:], in_=ps_m[0:1, 0:512], func=AF.Ln)
                    a.activation(out=recb[:], in_=ps_m[0:1, 512:1024], func=AF.Ln)
                    on.inc(a.activation(out=reca[:], in_=reca[:], func=AF.Exp, scale=-1.0))
                    on.inc(a.activation(out=recb[:], in_=recb[:], func=AF.Exp, scale=-1.0))
                    on.wait(t)
                    i11 = t.matmul(ps_rep[0:64, 0:512], ctx.consts["ones1"][0:1, 0:64],
                                   reca[:], start=True, stop=True)
                    i12 = t.matmul(ps_rep[64:128, 0:512], ctx.consts["ones1"][0:1, 0:64],
                                   recb[:], start=True, stop=True, tile_position=(0, 64))
                    mm.inc(i11)
                    mm.inc(i12)
                    mm.wait(v)
                    pv.wait(v)
                    v.tensor_copy(out=onrm[:], in_=ps_o[:, 0:512])
                    on.inc(v.tensor_mul(out=OT[:, hp * S + q0:hp * S + q0 + 512],
                                        in0=onrm[:], in1=ps_rep[:, 0:512]))
                    on.wait(t)
                    on.wait(a)
            for _e in (t, v, a, sp, g):
                on.wait(_e)
                pv.wait(_e)
                sc.wait(_e)
                ex.wait(_e)
            if "dbg_ot" in ctx.io:
                dot = ctx.sem(f"at{L}_dot")
                dot.inc(sp.dma_start(out=ctx.io["dbg_ot"], in_=OT[:, 0:S]), 16)
                dot.wait(sp)
        with contextlib.ExitStack() as stD:
            sbd = lambda n, s_, dt_: stD.enter_context(nc.sbuf_tensor(f"atD{L}{n}", s_, dt_))
            wo = sbd("wo", [128, 4 * 1024], BF16)
            oev = sbd("oe", [128, 512], F32)
            ps_p = stD.enter_context(nc.psum_tensor(f"atD{L}p", [128, 1024], F32))
            wl.inc(g.dma_start(out=wo[:].rearrange("p (k m) -> p k m", k=4),
                               in_=ctx.io[pfx + "woT"].rearrange("(k p) m -> p k m", p=128)), 16)
            wl.wait(t)
            s_own = ctx.io[f"s_own{L}"]
            s_sh = ctx.io[f"s_sh{L}"]
            for ch in range(8):
                a0 = ch * 512
                for kd in range(DT):
                    reg = (kd % 2) * 512
                    for hp in range(4):
                        i13 = t.matmul(ps_p[:, reg:reg + 512],
                                       wo[:, hp * 1024 + kd * 128:hp * 1024 + kd * 128 + 128],
                                       OT[:, hp * S + a0:hp * S + a0 + 512],
                                       start=(hp == 0), stop=(hp == 3))
                    om.inc(i13)
                    om.wait(v)
                    so.wait(v)
                    ev.inc(v.tensor_copy(out=oev[:], in_=ps_p[:, reg:reg + 512]))
                    ev.wait(sp)
                    par = ctx.par_reg(sp)
                    with sp.If_cmp(par, 0, "IS_EQ"):
                        dst = (s_own[kd, :, a0:a0 + 512] if ch < 4
                               else s_sh[0, kd, :, a0 - T:a0 - T + 512])
                        so.inc(sp.dma_start(out=dst, in_=oev[:]), 16)
                    with sp.Else():
                        dst = (s_sh[1, kd, :, a0:a0 + 512] if ch < 4
                               else s_own[kd, :, a0 - T:a0 - T + 512])
                        sp.dma_start(out=dst, in_=oev[:]).then_inc(so.h, 16)
                    om.wait(t)
                    ev.wait(t)
            for _e in (t, v, a, sp, g):
                so.wait(_e)
                om.wait(_e)
                ev.wait(_e)
            emit_flag_write(ctx, g, L * 8 + 1)
    return so


def emit_l1_mixer(ctx):
    nc, t, v, a, sp, g = ctx.nc, ctx.t, ctx.v, ctx.a, ctx.sp, ctx.g
    xn_sh = ctx.io["xn_sh1"]
    s_own = ctx.io["s_own1"]
    dbc_sh = ctx.io["l1_dbc_sh"]
    with contextlib.ExitStack() as st:
        sb = lambda n, s_, dt_: st.enter_context(nc.sbuf_tensor(f"m1{n}", s_, dt_))
        ld = ctx.sem("m1_ld")
        wl = ctx.sem("m1_wl")
        mm = ctx.sem("m1_mm")
        ev = ctx.sem("m1_ev")
        cv = ctx.sem("m1_cv")
        pj = ctx.sem("m1_pj")
        pe = ctx.sem("m1_pe")
        sx = ctx.sem("m1_sx")
        so = ctx.sem("m1_so")
        with contextlib.ExitStack() as s1:
            sb1 = lambda n, s_, dt_: s1.enter_context(nc.sbuf_tensor(f"m1a{n}", s_, dt_))
            wg = sb1("wg", [128, DT * 1024], BF16)
            wi = sb1("wi", [128, DT * 1024], BF16)
            wd = sb1("wd", [128, DT * 16], BF16)
            wb = sb1("wb", [128, DT * 16], BF16)
            wc = sb1("wc", [128, DT * 16], BF16)
            wD = sb1("wD", [128, DT], BF16)
            cw = sb1("cw", [128, 32], F32)
            cb8 = sb1("cb8", [128, 8], F32)
            xnt = sb1("xn", [128, DT * 512], BF16)
            sg = sb1("sg", [128, 512], BF16)
            xg = sb1("xg", [128, DT * 515], BF16)
            xc = sb1("xc", [128, DT * 512], BF16)
            dbuf = sb1("db", [16, S], F32)
            bbuf = sb1("bb", [16, S], F32)
            cbuf = sb1("cb", [16, S], F32)
            Dbuf = sb1("Db", [1, S], F32)
            ps_g = s1.enter_context(nc.psum_tensor("m1pg", [128, 512], F32))
            ps_i = s1.enter_context(nc.psum_tensor("m1pi", [128, 512], F32))
            ps_d = s1.enter_context(nc.psum_tensor("m1pd", [16, 512], F32))
            ps_b = s1.enter_context(nc.psum_tensor("m1pb", [16, 512], F32))
            ps_c = s1.enter_context(nc.psum_tensor("m1pc", [16, 512], F32))
            ps_D = s1.enter_context(nc.psum_tensor("m1pD", [1, 512], F32))
            wl.inc(g.dma_start(out=wg[:].rearrange("p (k m) -> p k m", k=DT),
                               in_=ctx.io["l1_wgT"].rearrange("(k p) m -> p k m", p=128)), 16)
            wl.inc(g.dma_start(out=wi[:].rearrange("p (k m) -> p k m", k=DT),
                               in_=ctx.io["l1_wiT"].rearrange("(k p) m -> p k m", p=128)), 16)
            wl.inc(g.dma_start(out=wd[:].rearrange("p (k m) -> p k m", k=DT),
                               in_=ctx.io["l1_wd"].rearrange("(k p) m -> p k m", p=128)), 16)
            wl.inc(g.dma_start(out=wb[:].rearrange("p (k m) -> p k m", k=DT),
                               in_=ctx.io["l1_wb"].rearrange("(k p) m -> p k m", p=128)), 16)
            wl.inc(g.dma_start(out=wc[:].rearrange("p (k m) -> p k m", k=DT),
                               in_=ctx.io["l1_wc"].rearrange("(k p) m -> p k m", p=128)), 16)
            wl.inc(g.dma_start(out=wD[:], in_=ctx.io["l1_D"]), 16)
            ld.inc(sp.dma_start(out=cw[:], in_=ctx.io["l1_cw"]), 16)
            ld.inc(sp.dma_start(out=cb8[:], in_=ctx.io["l1_cb"]), 16)
            ld.wait(v)
            for ct in range(DT):
                cv.inc(v.memset(xg[:, ct * 515:ct * 515 + 3], 0.0))
            emit_poll(ctx, sp, 1 * 8)
            for ch in range(8):
                slot, c0 = ch // 4, (ch % 4) * 512
                load_xn_chunk(ctx, sp, ld, xn_sh, xnt, c0,
                              war_sem=mm, slot=slot)
                ld.wait(t)
                wl.wait(t)
                for ct in range(DT):
                    for k in range(DT):
                        i1 = t.matmul(ps_g[:, 0:512],
                                      wg[:, k * 1024 + ct * 128:k * 1024 + ct * 128 + 128],
                                      xnt[:, k * 512:(k + 1) * 512],
                                      start=(k == 0), stop=(k == DT - 1))
                        i2 = t.matmul(ps_i[:, 0:512],
                                      wi[:, k * 1024 + ct * 128:k * 1024 + ct * 128 + 128],
                                      xnt[:, k * 512:(k + 1) * 512],
                                      start=(k == 0), stop=(k == DT - 1))
                    mm.inc(i1)
                    mm.inc(i2)
                    mm.wait(a)
                    cv.wait(a)
                    ev.inc(a.activation(out=sg[:], in_=ps_g[:, 0:512], func=AF.Silu))
                    ev.wait(v)
                    cv.wait(v)
                    cv.inc(v.tensor_mul(out=xg[:, ct * 515 + 3:ct * 515 + 515],
                                        in0=sg[:], in1=ps_i[:, 0:512]))
                    cv.wait(t)
                    ev.wait(t)
                    xbase = ct * 515
                    v.tensor_scalar_mul(out=xc[:, ct * 512:(ct + 1) * 512],
                                        in0=xg[:, xbase:xbase + 512],
                                        scalar1=cw[:, ct * 4:ct * 4 + 1])
                    for j in (1, 2, 3):
                        v.scalar_tensor_tensor(out=xc[:, ct * 512:(ct + 1) * 512],
                                               in0=xg[:, xbase + j:xbase + j + 512],
                                               scalar=cw[:, ct * 4 + j:ct * 4 + j + 1],
                                               in1=xc[:, ct * 512:(ct + 1) * 512],
                                               op0=ALU.mult, op1=ALU.add)
                    cv.inc(v.tensor_scalar_add(out=xc[:, ct * 512:(ct + 1) * 512],
                                               in0=xc[:, ct * 512:(ct + 1) * 512],
                                               scalar1=cb8[:, ct:ct + 1]))
                    cv.inc(v.tensor_copy(out=xg[:, xbase:xbase + 3],
                                         in_=xg[:, xbase + 512:xbase + 515]))
                cv.wait(t)
                for ct in range(DT):
                    i3 = t.matmul(ps_d[:, 0:512], wd[:, ct * 16:(ct + 1) * 16],
                                  xc[:, ct * 512:(ct + 1) * 512],
                                  start=(ct == 0), stop=(ct == DT - 1))
                    i4 = t.matmul(ps_b[:, 0:512], wb[:, ct * 16:(ct + 1) * 16],
                                  xc[:, ct * 512:(ct + 1) * 512],
                                  start=(ct == 0), stop=(ct == DT - 1))
                    i5 = t.matmul(ps_c[:, 0:512], wc[:, ct * 16:(ct + 1) * 16],
                                  xc[:, ct * 512:(ct + 1) * 512],
                                  start=(ct == 0), stop=(ct == DT - 1))
                    i6 = t.matmul(ps_D[:, 0:512], wD[:, ct:ct + 1],
                                  xc[:, ct * 512:(ct + 1) * 512],
                                  start=(ct == 0), stop=(ct == DT - 1))
                for i_ in (i3, i4, i5, i6):
                    pj.inc(i_)
                pj.wait(v)
                a0 = ch * 512
                pe.inc(v.tensor_copy(out=dbuf[0:16, a0:a0 + 512], in_=ps_d[:, 0:512]))
                pe.inc(v.tensor_copy(out=bbuf[0:16, a0:a0 + 512], in_=ps_b[:, 0:512]))
                pe.inc(v.tensor_copy(out=cbuf[0:16, a0:a0 + 512], in_=ps_c[:, 0:512]))
                pe.inc(v.tensor_copy(out=Dbuf[0:1, a0:a0 + 512], in_=ps_D[:, 0:512]))
                pe.wait(t)
            if "dbg1_xc" in ctx.io:
                cv.wait(sp)
                pe.wait(sp)
                dx = ctx.sem("m1_dx")
                dx.inc(sp.dma_start(out=ctx.io["dbg1_xc"], in_=xc[:]), 16)
                dx.inc(sp.dma_start(out=ctx.io["dbg1_d"], in_=dbuf[0:16, :]), 16)
                dx.wait(sp)
            pe.wait(sp)
            slot_dma(ctx, sp, sx, (dbc_sh[0, 0], dbc_sh[1, 0]), dbuf[0:16, :])
            slot_dma(ctx, sp, sx, (dbc_sh[0, 1], dbc_sh[1, 1]), bbuf[0:16, :])
            slot_dma(ctx, sp, sx, (dbc_sh[0, 2], dbc_sh[1, 2]), cbuf[0:16, :])
            slot_dma(ctx, sp, sx, (dbc_sh[0, 3, 0:1], dbc_sh[1, 3, 0:1]), Dbuf[0:1, :])
            sx.wait(sp)
            emit_flag_write(ctx, sp, 1 * 8 + 2)
            emit_poll(ctx, sp, 1 * 8 + 2)
            pd = sb1("pd", [16, S], F32)
            if not ctx.fused:
                sx.wait(v)
                sx.inc(v.memset(pd[0:16, :], 0.0))
                sx.wait(sp)
                other_slot_dma(ctx, sp, sx, None, None) if False else None
                for idx in range(3):
                    par = ctx.par_reg(sp)
                    with sp.If_cmp(par, 0, "IS_EQ"):
                        sx.inc(sp.dma_start(out=dbc_sh[1, idx], in_=pd[0:16, :]), 16)
                    with sp.Else():
                        sp.dma_start(out=dbc_sh[0, idx], in_=pd[0:16, :]).then_inc(sx.h, 16)
                par = ctx.par_reg(sp)
                with sp.If_cmp(par, 0, "IS_EQ"):
                    sx.inc(sp.dma_start(out=dbc_sh[1, 3, 0:1], in_=pd[0:1, :]), 16)
                with sp.Else():
                    sp.dma_start(out=dbc_sh[0, 3, 0:1], in_=pd[0:1, :]).then_inc(sx.h, 16)
                sx.wait(sp)
            bdcol = sb1("bd", [16, 1], F32)
            ld.inc(sp.dma_start(out=bdcol[:], in_=ctx.io["l1_bd"]), 16)
            for idx, buf in ((0, dbuf), (1, bbuf), (2, cbuf)):
                other_slot_dma(ctx, sp, sx, pd[0:16, :],
                               (dbc_sh[0, idx], dbc_sh[1, idx]))
                sx.wait(v)
                ld.wait(v)
                instr = v.tensor_add(out=buf[0:16, :], in0=buf[0:16, :], in1=pd[0:16, :])
                sx.inc(instr)
                sx.wait(sp)
            other_slot_dma(ctx, sp, sx, pd[0:1, :],
                           (dbc_sh[0, 3, 0:1], dbc_sh[1, 3, 0:1]))
            sx.wait(v)
            sx.inc(v.tensor_add(out=Dbuf[0:1, :], in0=Dbuf[0:1, :], in1=pd[0:1, :]))
            sx.inc(v.tensor_scalar_add(out=dbuf[0:16, :], in0=dbuf[0:16, :],
                                       scalar1=bdcol[0:16, 0:1]))
            la = sb1("la", [16, 1], F32)
            ld.inc(sp.dma_start(out=la[:], in_=ctx.io["l1_logA"]), 16)
            a_t = sb1("at", [16, S], BF16)
            Dbf = sb1("Dbf", [1, S], BF16)
            b_t = sb1("bt", [16, S], BF16)
            h_t = sb1("ht", [16, S], BF16)
            y_t = sb1("yt", [1, S], F32)
            ps_y = s1.enter_context(nc.psum_tensor("m1py", [1, 512], F32))
            sx.wait(a)
            ld.wait(a)
            sx.inc(a.activation(out=a_t[0:16, :], in_=dbuf[0:16, :], func=AF.Exp,
                                scale=la[0:16, 0:1]))
            sx.wait(v)
            sx.inc(v.tensor_mul(out=b_t[0:16, :], in0=dbuf[0:16, :], in1=bbuf[0:16, :]))
            sx.inc(v.tensor_tensor_scan(out=h_t[0:16, :], data0=a_t[0:16, :],
                                        data1=b_t[0:16, :], initial=0.0,
                                        op0=ALU.mult, op1=ALU.add))
            sx.inc(v.tensor_mul(out=h_t[0:16, :], in0=h_t[0:16, :], in1=cbuf[0:16, :]))
            sx.inc(v.tensor_copy(out=Dbf[0:1, :], in_=Dbuf[0:1, :]))
            sx.wait(t)
            for ch in range(8):
                a0 = ch * 512
                i7 = t.matmul(ps_y[0:1, 0:512], ctx.consts["ones128"][0:16, 0:1],
                              h_t[0:16, a0:a0 + 512], start=True, stop=False)
                i8 = t.matmul(ps_y[0:1, 0:512], ctx.consts["ones1b"][0:1, 0:1],
                              Dbf[0:1, a0:a0 + 512],
                              start=False, stop=True)
                mm.inc(i8)
                mm.wait(v)
                pe.inc(v.tensor_copy(out=y_t[0:1, a0:a0 + 512], in_=ps_y[0:1, 0:512]))
                pe.wait(t)
            if "dbg1_y" in ctx.io:
                pe.wait(sp)
                sx.wait(sp)
                dy = ctx.sem("m1_dy")
                dy.inc(sp.dma_start(out=ctx.io["dbg1_y"], in_=y_t[0:1, :]), 16)
                dy.inc(sp.dma_start(out=ctx.io["dbg1_h"], in_=h_t[0:16, :]), 16)
                dy.wait(sp)
            woc = sb1("wo", [128, 8], F32)
            sev = sb1("se", [128, 512], F32)
            ld.inc(sp.dma_start(out=woc[:], in_=ctx.io["l1_wout"]), 16)
            ps_r = s1.enter_context(nc.psum_tensor("m1pr", [128, 512], F32))
            ld.wait(v)
            pe.wait(t)
            for c in range(4):
                for kd in range(DT):
                    par = ctx.par_reg(t)
                    with t.If_cmp(par, 0, "IS_EQ"):
                        i9 = t.matmul(ps_r[:, 0:512], ctx.consts["ones1"][:],
                                      y_t[0:1, c * 512:c * 512 + 512],
                                      start=True, stop=True)
                        mm.inc(i9)
                    with t.Else():
                        t.matmul(ps_r[:, 0:512], ctx.consts["ones1"][:],
                                 y_t[0:1, T + c * 512:T + c * 512 + 512],
                                 start=True, stop=True).then_inc(mm.h, 1)
                    mm.wait(v)
                    so.wait(v)
                    ev.inc(v.tensor_scalar_mul(out=sev[:], in0=ps_r[:, 0:512],
                                               scalar1=woc[:, kd:kd + 1]))
                    ev.wait(sp)
                    so.inc(sp.dma_start(out=s_own[kd, :, c * 512:(c + 1) * 512],
                                        in_=sev[:]), 16)
                    ev.wait(t)
            for _e in (t, v, a, sp, g):
                so.wait(_e)
                mm.wait(_e)
                ev.wait(_e)
                sx.wait(_e)
                pe.wait(_e)
                cv.wait(_e)
    return so


def emit_l3_mixer(ctx):
    nc, t, v, a, sp, g = ctx.nc, ctx.t, ctx.v, ctx.a, ctx.sp, ctx.g
    xn_sh = ctx.io["xn_sh3"]
    s_own = ctx.io["s_own3"]
    s_sh = ctx.io["s_sh3"]
    TC = 512
    l3tot = ctx.nc.dram_tensor("l3_tot", [8, 96, TC], F32).ap()
    with contextlib.ExitStack() as st:
        sb = lambda n, s_, dt_: st.enter_context(nc.sbuf_tensor(f"m3{n}", s_, dt_))
        ld = ctx.sem("m3_ld")
        wl = ctx.sem("m3_wl")
        mm = ctx.sem("m3_mm")
        ev = ctx.sem("m3_ev")
        rw = ctx.sem("m3_rw")
        pj = ctx.sem("m3_pj")
        pe = ctx.sem("m3_pe")
        sx = ctx.sem("m3_sx")
        sy = ctx.sem("m3_sy")
        so = ctx.sem("m3_so")
        win = sb("win", [128, DT * 1024], BF16)
        wsk = sb("wsk", [128, DT * 1024], BF16)
        wou = sb("wou", [128, DT * 1024], BF16)
        wd1 = sb("wd1", [128, DT * 64], BF16)
        wbc = sb("wbc", [128, DT * 32], BF16)
        wd2 = sb("wd2", [64, 1024], BF16)
        nwc = sb("nwc", [128, 8], F32)
        cw = sb("cw", [128, 32], F32)
        cb8 = sb("cb8", [128, 8], F32)
        d2b = sb("d2b", [128, 8], F32)
        wdc = sb("wdc", [128, 8], F32)
        alog = sb("alog", [128, 128], F32)
        A_sb = sb("A", [128, 128], F32)
        hcar = sb("hc", [128, 128], F32)
        wl.inc(g.dma_start(out=win[:].rearrange("p (k m) -> p k m", k=DT),
                           in_=ctx.io["l3_inT"].rearrange("(k p) m -> p k m", p=128)), 16)
        wl.inc(g.dma_start(out=wsk[:].rearrange("p (k m) -> p k m", k=DT),
                           in_=ctx.io["l3_skipT"].rearrange("(k p) m -> p k m", p=128)), 16)
        wl.inc(g.dma_start(out=wou[:].rearrange("p (k m) -> p k m", k=DT),
                           in_=ctx.io["l3_outT"].rearrange("(k p) m -> p k m", p=128)), 16)
        wl.inc(g.dma_start(out=wd1[:].rearrange("p (k m) -> p k m", k=DT),
                           in_=ctx.io["l3_wd1T"].rearrange("(k p) m -> p k m", p=128)), 16)
        wl.inc(g.dma_start(out=wbc[:].rearrange("p (k m) -> p k m", k=DT)[:, :, 0:16],
                           in_=ctx.io["l3_wbT"].rearrange("(k p) m -> p k m", p=128)), 16)
        wl.inc(g.dma_start(out=wbc[:].rearrange("p (k m) -> p k m", k=DT)[:, :, 16:32],
                           in_=ctx.io["l3_wcT"].rearrange("(k p) m -> p k m", p=128)), 16)
        wl.inc(g.dma_start(out=wd2[:], in_=ctx.io["l3_wd2T"]), 16)
        ld.inc(sp.dma_start(out=nwc[:], in_=ctx.io["l3_normw"]), 16)
        ld.inc(sp.dma_start(out=cw[:], in_=ctx.io["l3_cw"]), 16)
        ld.inc(sp.dma_start(out=cb8[:], in_=ctx.io["l3_cb"]), 16)
        ld.inc(sp.dma_start(out=d2b[:], in_=ctx.io["l3_wd2b"]), 16)
        ld.inc(sp.dma_start(out=wdc[:], in_=ctx.io["l3_WD"]), 16)
        ld.inc(sp.dma_start(out=alog[:], in_=ctx.io["l3_alog"]), 16)
        ld.wait(a)
        rw.inc(a.activation(out=A_sb[:], in_=alog[:], func=AF.Exp))
        rw.wait(v)
        rw.inc(v.tensor_scalar_mul(out=A_sb[:], in0=A_sb[:], scalar1=-1.0))
        rw.inc(v.memset(hcar[:], 0.0))
        xnt = sb("xnt", [128, DT * 512], BF16)
        rows = sb("rws", [1, 512 * 3], F32)
        xr = sb("xr", [128, DT * 512], BF16)
        skips = sb("skp", [128, DT * TC], BF16)
        xin = sb("xin", [128, DT * (TC + 3)], BF16)
        xc = sb("xc", [128, DT * TC], BF16)
        d1f = sb("d1f", [64, TC], BF16)
        dl = sb("dl", [128, DT * TC], F32)
        zs = sb("zs", [128, 512], F32)
        u1 = sb("u1", [128, 512], F32)
        ux = sb("ux", [128, DT * TC], BF16)
        at_ = sb("at", [128, TC], F32)
        bt_ = sb("bt", [128, TC], F32)
        ht_ = sb("ht", [128, TC], F32)
        Brep = sb("Br", [128, TC], BF16)
        Crep = sb("Cr", [128, TC], BF16)
        yb = sb("yb", [128, DT * TC], BF16)
        oev = sb("oe", [128, 512], F32)
        d1b = sb("d1b", [64, TC], F32)
        Bb = sb("Bb", [16, TC], F32)
        Cb = sb("Cb", [16, TC], F32)
        prt = sb("prt", [64, TC], F32)
        rw.wait(v)
        for et in range(DT):
            rw.inc(v.memset(xin[:, et * (TC + 3):et * (TC + 3) + 3], 0.0))
        emit_poll(ctx, sp, 3 * 8)
        for sc in range(8):
            dbc0 = ctx.io[f"l3_dbc_sh{sc // 2}"]
            co = (sc % 2) * 512
            dbc_sh = dbc0[:, :, co:co + 512]
            with contextlib.ExitStack() as s1:
                ps_s = s1.enter_context(nc.psum_tensor(f"m3s{sc}", [1, 512], F32))
                ps_bc = s1.enter_context(nc.psum_tensor(f"m3b{sc}", [128, 512], F32))
                ps_p1 = s1.enter_context(nc.psum_tensor(f"m3p1{sc}", [128, 512], F32))
                ps_p2 = s1.enter_context(nc.psum_tensor(f"m3p2{sc}", [128, 512], F32))
                ps_d1 = s1.enter_context(nc.psum_tensor(f"m3d{sc}", [64, 512], F32))
                ps_B = s1.enter_context(nc.psum_tensor(f"m3B{sc}", [16, 512], F32))
                ps_C = s1.enter_context(nc.psum_tensor(f"m3C{sc}", [16, 512], F32))
                for tc in range(1):
                    ch = sc
                    slot, c0 = ch // 4, (ch % 4) * 512
                    load_xn_chunk(ctx, sp, ld, xn_sh, xnt, c0, war_sem=mm, slot=slot)
                    ld.wait(a)
                    rw.wait(a)
                    for k in range(DT):
                        ev.inc(a.activation(out=xr[:, k * 512:(k + 1) * 512],
                                            in_=xnt[:, k * 512:(k + 1) * 512],
                                            func=AF.Square))
                    ev.wait(t)
                    ld.wait(t)
                    rw.wait(t)
                    for k in range(DT):
                        i1 = t.matmul(ps_s[0:1, 0:512], ctx.consts["ones128"][:],
                                      xr[:, k * 512:(k + 1) * 512],
                                      start=(k == 0), stop=(k == DT - 1))
                    mm.inc(i1)
                    mm.wait(v)
                    rw.inc(v.tensor_scalar_mul(out=rows[0:1, 0:512],
                                               in0=ps_s[0:1, 0:512], scalar1=1.0 / D))
                    rw.wait(a)
                    a.activation(out=rows[0:1, 512:1024], in_=rows[0:1, 0:512],
                                 func=AF.Ln, bias=ctx.consts["cpool"][0:1, 0:1])
                    rw.inc(a.activation(out=rows[0:1, 1024:1536], in_=rows[0:1, 512:1024],
                                        func=AF.Exp, scale=ctx.consts["cpool"][0:1, 1:2]))
                    rw.wait(t)
                    i2 = t.matmul(ps_bc[:, 0:512], ctx.consts["ones1"][:],
                                  rows[0:1, 1024:1536], start=True, stop=True)
                    mm.inc(i2)
                    mm.wait(v)
                    for k in range(DT):
                        v.tensor_mul(out=xr[:, k * 512:(k + 1) * 512],
                                     in0=xnt[:, k * 512:(k + 1) * 512],
                                     in1=ps_bc[:, 0:512])
                        rw.inc(v.tensor_scalar_mul(out=xr[:, k * 512:(k + 1) * 512],
                                                   in0=xr[:, k * 512:(k + 1) * 512],
                                                   scalar1=nwc[:, k:k + 1]))
                    rw.wait(t)
                    wl.wait(t)
                    for et in range(DT):
                        for k in range(DT):
                            i3 = t.matmul(ps_p1[:, 0:512],
                                          wsk[:, k * 1024 + et * 128:k * 1024 + et * 128 + 128],
                                          xr[:, k * 512:(k + 1) * 512],
                                          start=(k == 0), stop=(k == DT - 1))
                            i4 = t.matmul(ps_p2[:, 0:512],
                                          win[:, k * 1024 + et * 128:k * 1024 + et * 128 + 128],
                                          xr[:, k * 512:(k + 1) * 512],
                                          start=(k == 0), stop=(k == DT - 1))
                        mm.inc(i3)
                        mm.inc(i4)
                        mm.wait(a)
                        pe.wait(a)
                        ev.inc(a.activation(
                            out=skips[:, et * TC + tc * 512:et * TC + tc * 512 + 512],
                            in_=ps_p1[:, 0:512], func=AF.Silu))
                        ev.wait(v)
                        pe.wait(v)
                        xb0 = et * (TC + 3) + 3 + tc * 512
                        pe.inc(v.tensor_copy(out=xin[:, xb0:xb0 + 512],
                                             in_=ps_p2[:, 0:512]))
                        pe.wait(t)
                        ev.wait(t)
                pe.wait(v)
                ld.wait(v)
                so.wait(v)
                pj.wait(v)
                for et in range(DT):
                    xb = et * (TC + 3)
                    v.tensor_scalar_mul(out=xc[:, et * TC:et * TC + TC],
                                        in0=xin[:, xb:xb + TC],
                                        scalar1=cw[:, et * 4:et * 4 + 1])
                    for j in (1, 2, 3):
                        v.scalar_tensor_tensor(out=xc[:, et * TC:et * TC + TC],
                                               in0=xin[:, xb + j:xb + j + TC],
                                               scalar=cw[:, et * 4 + j:et * 4 + j + 1],
                                               in1=xc[:, et * TC:et * TC + TC],
                                               op0=ALU.mult, op1=ALU.add)
                    v.tensor_scalar_add(out=xc[:, et * TC:et * TC + TC],
                                        in0=xc[:, et * TC:et * TC + TC],
                                        scalar1=cb8[:, et:et + 1])
                    pj.inc(v.tensor_copy(out=xin[:, xb:xb + 3],
                                         in_=xin[:, xb + TC:xb + TC + 3]))
                pj.wait(a)
                for et in range(DT):
                    pj.inc(a.activation(out=xc[:, et * TC:et * TC + TC],
                                        in_=xc[:, et * TC:et * TC + TC], func=AF.Silu))
                pj.wait(t)
                for tc in range(1):
                    for et in range(DT):
                        i5 = t.matmul(ps_d1[:, 0:512], wd1[:, et * 64:(et + 1) * 64],
                                      xc[:, et * TC + tc * 512:et * TC + tc * 512 + 512],
                                      start=(et == 0), stop=(et == DT - 1))
                        i6 = t.matmul(ps_B[:, 0:512], wbc[:, et * 32:et * 32 + 16],
                                      xc[:, et * TC + tc * 512:et * TC + tc * 512 + 512],
                                      start=(et == 0), stop=(et == DT - 1))
                        i7 = t.matmul(ps_C[:, 0:512], wbc[:, et * 32 + 16:et * 32 + 32],
                                      xc[:, et * TC + tc * 512:et * TC + tc * 512 + 512],
                                      start=(et == 0), stop=(et == DT - 1))
                    pj.inc(i5)
                    pj.inc(i6)
                    pj.inc(i7)
                    pj.wait(v)
                    sx.wait(v)
                    pe.inc(v.tensor_copy(out=d1b[0:64, tc * 512:tc * 512 + 512],
                                         in_=ps_d1[:, 0:512]))
                    pe.inc(v.tensor_copy(out=Bb[0:16, tc * 512:tc * 512 + 512],
                                         in_=ps_B[:, 0:512]))
                    pe.inc(v.tensor_copy(out=Cb[0:16, tc * 512:tc * 512 + 512],
                                         in_=ps_C[:, 0:512]))
                    pe.wait(t)
                pe.wait(sp)
                slot_dma(ctx, sp, sx, (dbc_sh[0, 0:64], dbc_sh[1, 0:64]), d1b[0:64, :])
                slot_dma(ctx, sp, sx, (dbc_sh[0, 64:80], dbc_sh[1, 64:80]), Bb[0:16, :])
                slot_dma(ctx, sp, sx, (dbc_sh[0, 80:96], dbc_sh[1, 80:96]), Cb[0:16, :])
                sx.wait(sp)
                emit_flag_write(ctx, sp, 3 * 8 + 2 + sc)
                emit_poll(ctx, sp, 3 * 8 + 2 + sc)
                if not ctx.fused:
                    sx.wait(v)
                    sx.inc(v.memset(prt[0:64, :], 0.0))
                    sx.wait(sp)
                    for rng_ in ((0, 64), (64, 80), (80, 96)):
                        par = ctx.par_reg(sp)
                        with sp.If_cmp(par, 0, "IS_EQ"):
                            sx.inc(sp.dma_start(out=dbc_sh[1, rng_[0]:rng_[1]],
                                                in_=prt[0:rng_[1] - rng_[0], :]), 16)
                        with sp.Else():
                            sp.dma_start(out=dbc_sh[0, rng_[0]:rng_[1]],
                                         in_=prt[0:rng_[1] - rng_[0], :]).then_inc(sx.h, 16)
                    sx.wait(sp)
                other_slot_dma(ctx, sp, sx, prt[0:64, :], (dbc_sh[0, 0:64], dbc_sh[1, 0:64]))
                sx.wait(v)
                sx.inc(v.tensor_add(out=d1b[0:64, :], in0=d1b[0:64, :], in1=prt[0:64, :]))
                sx.wait(sp)
                other_slot_dma(ctx, sp, sx, prt[0:16, :], (dbc_sh[0, 64:80], dbc_sh[1, 64:80]))
                sx.wait(v)
                sx.inc(v.tensor_add(out=Bb[0:16, :], in0=Bb[0:16, :], in1=prt[0:16, :]))
                sx.wait(sp)
                other_slot_dma(ctx, sp, sx, prt[0:16, :], (dbc_sh[0, 80:96], dbc_sh[1, 80:96]))
                sx.wait(v)
                sx.inc(v.tensor_add(out=Cb[0:16, :], in0=Cb[0:16, :], in1=prt[0:16, :]))
                sx.inc(v.tensor_copy(out=d1f[0:64, :], in_=d1b[0:64, :]))
                sx.wait(sp)
                so.wait(sp)
                sx.inc(sp.dma_start(out=l3tot[sc, 64:80], in_=Bb[0:16, :]), 16)
                sx.inc(sp.dma_start(out=l3tot[sc, 80:96], in_=Cb[0:16, :]), 16)
            with contextlib.ExitStack() as s2:
                ps_z = s2.enter_context(nc.psum_tensor(f"m3z{sc}", [128, 512], F32))
                ps_y = s2.enter_context(nc.psum_tensor(f"m3y{sc}", [128, 1024], F32))
                sx.wait(t)
                for et in range(DT):
                    for tc in range(1):
                        i8 = t.matmul(ps_z[:, 0:512], wd2[:, et * 128:(et + 1) * 128],
                                      d1f[0:64, tc * 512:tc * 512 + 512],
                                      start=True, stop=True)
                        pj.inc(i8)
                        pj.wait(v)
                        sy.wait(v)
                        sy.inc(v.tensor_scalar_add(out=zs[:], in0=ps_z[:, 0:512],
                                                   scalar1=d2b[:, et:et + 1]))
                        sy.wait(t)
                        sy.wait(a)
                        sy.inc(a.activation(out=u1[:], in_=zs[:], func=AF.Abs))
                        sy.wait(a)
                        a.activation(out=u1[:], in_=u1[:], func=AF.Exp, scale=-1.0)
                        sy.inc(a.activation(out=u1[:], in_=u1[:], func=AF.Ln,
                                            bias=ctx.consts["cpool"][:, 5:6]))
                        sy.wait(v)
                        dcol = et * TC + tc * 512
                        v.tensor_scalar_max(out=zs[:], in0=zs[:], scalar1=0.0)
                        sy.inc(v.tensor_add(out=dl[:, dcol:dcol + 512], in0=zs[:],
                                            in1=u1[:]))
                sy.wait(v)
                for et in range(DT):
                    sy.inc(v.tensor_mul(out=ux[:, et * TC:(et + 1) * TC],
                                        in0=dl[:, et * TC:(et + 1) * TC],
                                        in1=xc[:, et * TC:(et + 1) * TC]))
                mm.wait(v)
                for et in range(DT):
                    sy.inc(v.memset(yb[:, et * TC:(et + 1) * TC], 0.0))
                for n in range(N3):
                    sy.wait(g)
                    sx.wait(g)
                    wl.inc(g.dma_start(out=Brep[:],
                                       in_=l3tot[sc, 64 + n:65 + n, :].broadcast_to((128, TC))), 16)
                    wl.inc(g.dma_start(out=Crep[:],
                                       in_=l3tot[sc, 80 + n:81 + n, :].broadcast_to((128, TC))), 16)
                    wl.wait(a)
                    wl.wait(v)
                    for et in range(DT):
                        col = et * 16 + n
                        sy.wait(a)
                        sy.inc(a.activation(out=at_[:], in_=dl[:, et * TC:(et + 1) * TC],
                                            func=AF.Exp, scale=A_sb[:, col:col + 1]))
                        sy.wait(v)
                        sy.inc(v.tensor_mul(out=bt_[:], in0=ux[:, et * TC:(et + 1) * TC],
                                            in1=Brep[:]))
                        sy.inc(v.tensor_tensor_scan(out=ht_[:], data0=at_[:], data1=bt_[:],
                                                    initial=hcar[:, col:col + 1],
                                                    op0=ALU.mult, op1=ALU.add))
                        sy.inc(v.tensor_copy(out=hcar[:, col:col + 1],
                                             in_=ht_[:, TC - 1:TC]))
                        v.tensor_mul(out=ht_[:], in0=ht_[:], in1=Crep[:])
                        sy.inc(v.tensor_add(out=yb[:, et * TC:(et + 1) * TC],
                                            in0=yb[:, et * TC:(et + 1) * TC], in1=ht_[:]))
                sy.wait(v)
                for et in range(DT):
                    v.scalar_tensor_tensor(out=yb[:, et * TC:(et + 1) * TC],
                                           in0=xc[:, et * TC:(et + 1) * TC],
                                           scalar=wdc[:, et:et + 1],
                                           in1=yb[:, et * TC:(et + 1) * TC],
                                           op0=ALU.mult, op1=ALU.add)
                    sy.inc(v.tensor_mul(out=yb[:, et * TC:(et + 1) * TC],
                                        in0=yb[:, et * TC:(et + 1) * TC],
                                        in1=skips[:, et * TC:(et + 1) * TC]))
                sy.wait(t)
                for tc in range(1):
                    ch = sc
                    a0 = ch * 512
                    for kd in range(DT):
                        reg = (kd % 2) * 512
                        for et in range(DT):
                            i9 = t.matmul(ps_y[:, reg:reg + 512],
                                          wou[:, et * 1024 + kd * 128:et * 1024 + kd * 128 + 128],
                                          yb[:, et * TC + tc * 512:et * TC + tc * 512 + 512],
                                          start=(et == 0), stop=(et == DT - 1))
                        mm.inc(i9)
                        mm.wait(v)
                        so.wait(v)
                        ev.inc(v.tensor_copy(out=oev[:], in_=ps_y[:, reg:reg + 512]))
                        ev.wait(sp)
                        par = ctx.par_reg(sp)
                        with sp.If_cmp(par, 0, "IS_EQ"):
                            dst = (s_own[kd, :, a0:a0 + 512] if ch < 4
                                   else s_sh[0, kd, :, a0 - T:a0 - T + 512])
                            so.inc(sp.dma_start(out=dst, in_=oev[:]), 16)
                        with sp.Else():
                            dst = (s_sh[1, kd, :, a0:a0 + 512] if ch < 4
                                   else s_own[kd, :, a0 - T:a0 - T + 512])
                            sp.dma_start(out=dst, in_=oev[:]).then_inc(so.h, 16)
                        mm.wait(t)
                        ev.wait(t)
        for _e in (t, v, a, sp, g):
            so.wait(_e)
            sy.wait(_e)
            mm.wait(_e)
            ev.wait(_e)
            sx.wait(_e)
            pe.wait(_e)
            pj.wait(_e)
        emit_flag_write(ctx, g, 3 * 8 + 1)
    return so


def build_fused(sample, dbg=False):
    nc = bass.Bass("TRN2", target_bir_lowering=False)
    stack = contextlib.ExitStack()
    ctx = Ctx(nc, stack, fused=True)
    with stack:
        declare_inputs(ctx, sample)
        declare_work(ctx)
        if dbg:
            for L in range(3):
                ctx.io[f"dbg_xl{L}"] = nc.dram_tensor(
                    f"dbg_xl{L}", [DT, 128, T], F32, kind="ExternalOutput").ap()
            ctx.io["dbg_s3"] = nc.dram_tensor("dbg_s3", [DT, 128, T], F32,
                                              kind="ExternalOutput").ap()
            ctx.io["dbg_g3"] = nc.dram_tensor("dbg_g3", [DT, 128, T], BF16,
                                              kind="ExternalOutput").ap()
        with nc.Block() as blk:
            with nc.body("main", valid_engines=ALL_E):
                setup_consts(ctx)
                for layer in range(4):
                    x_src = ctx.io["x0"] if layer == 0 else ctx.io["x_dram"]
                    emit_ln(ctx, layer, x_src, store_f32=(layer == 1))
                    nc.all_engine_barrier()
                    if layer in (0, 3):
                        emit_ffn03(ctx, layer)
                    elif layer == 1:
                        emit_moe(ctx)
                    else:
                        emit_ffn_l2(ctx)
                    nc.all_engine_barrier()
                    if layer in (0, 2):
                        emit_attn(ctx, layer)
                    elif layer == 1:
                        emit_l1_mixer(ctx)
                    else:
                        emit_l3_mixer(ctx)
                    nc.all_engine_barrier()
                    emit_combine(ctx, layer,
                                 has_partner=(layer != 1),
                                 bias_col=("l0_bo" if layer == 0 else None),
                                 add_xn=(layer == 3),
                                 x_src_name=("x0" if layer == 0 else "x_dram"))
                    nc.all_engine_barrier()
                    if dbg and layer < 3:
                        dxs = ctx.sem("dxs")
                        dxs.inc(ctx.sp.dma_start(out=ctx.io[f"dbg_xl{layer}"],
                                                 in_=ctx.io["x_dram"]), 16)
                        dxs.wait(ctx.sp)
                        nc.all_engine_barrier()
    return nc


def kernel(**inputs):
    rng = np.random.default_rng()
    nonce = rng.integers(1, 1 << 30, size=(1, 64)).astype(np.int32)
    in_maps = [prepare_core_inputs(inputs, c, nonce) for c in range(8)]
    nc = build_fused(in_maps[0])
    res = run_bass_kernel_spmd(nc, in_maps, core_ids=list(range(8)))
    out = np.empty((B, S, D), np.float32)
    for c in range(8):
        b, h = c // 2, c % 2
        o = np.asarray(res.results[c]["out"])
        out[b, h * T:(h + 1) * T] = o.reshape(D, T).T
    return out


# version 08 (final)
# speedup vs baseline: 1.7603x; 1.7603x over previous
import contextlib
import numpy as np
import ml_dtypes
import concourse.bass as bass
import concourse.mybir as mybir
from concourse.bass_utils import run_bass_kernel_spmd

F32 = mybir.dt.float32
U8 = mybir.dt.uint8
BF16 = mybir.dt.bfloat16
I32 = mybir.dt.int32
AF = mybir.ActivationFunctionType
ALU = mybir.AluOpType
ET = mybir.EngineType
ALL_E = [ET.PE, ET.Activation, ET.Pool, ET.DVE, ET.SP]

BF = ml_dtypes.bfloat16

B, S, D = 4, 4096, 1024
NH, DH = 16, 64
T = 2048
DT = D // 128
F_FFN = 4096
DI = 2048
NS1 = 16
E3, N3, DD = 2048, 16, 64
EPS = 1e-5


class Sem:
    def __init__(self, nc, stack, name):
        self.h = stack.enter_context(nc.semaphore(name))
        self.v = 0

    def inc(self, instr, n=1):
        instr.then_inc(self.h, n)
        self.v += n
        return instr

    def wait(self, eng, v=None):
        eng.wait_ge(self.h, self.v if v is None else v)


class Ctx:
    def __init__(self, nc, stack, fused):
        self.nc = nc
        self.stack = stack
        self.fused = fused
        self.t = nc.tensor
        self.v = nc.vector
        self.a = nc.scalar
        self.g = nc.gpsimd
        self.sp = nc.sync
        self.io = {}
        self.sems = {}
        self.consts = {}
        self.regs = {}
        self._uid = 0

    def sem(self, name):
        key = name if name.startswith("poll") else name.split("_")[-1]
        if key not in self.sems:
            self.sems[key] = Sem(self.nc, self.stack, key)
        return self.sems[key]

    def uid(self):
        self._uid += 1
        return self._uid

    def sbuf(self, name, shape, dtype):
        return self.stack.enter_context(self.nc.sbuf_tensor(name, shape, dtype))

    def par_reg(self, eng):
        key = ("par", eng.engine)
        if key not in self.regs:
            pid = eng.partition_id()
            self.regs[key] = eng.to_reg(pid % 2)
        return self.regs[key]

    def poll_regs(self, eng):
        key = ("poll", eng.engine)
        if key not in self.regs:
            nm = str(eng.engine)
            self.regs[key] = (eng.alloc_register(f"pr_n_{nm}"),
                              eng.alloc_register(f"pr_a_{nm}"),
                              eng.alloc_register(f"pr_b_{nm}"))
        return self.regs[key]


def slot_dma(ctx, eng, sem, dst_slots, src, n=16):
    par = ctx.par_reg(eng)
    with eng.If_cmp(par, 0, "IS_EQ"):
        sem.inc(eng.dma_start(out=dst_slots[0], in_=src), n)
    with eng.Else():
        eng.dma_start(out=dst_slots[1], in_=src).then_inc(sem.h, n)


def other_slot_dma(ctx, eng, sem, dst, src_slots, n=16):
    par = ctx.par_reg(eng)
    with eng.If_cmp(par, 0, "IS_EQ"):
        sem.inc(eng.dma_start(out=dst, in_=src_slots[1]), n)
    with eng.Else():
        eng.dma_start(out=dst, in_=src_slots[0]).then_inc(sem.h, n)


def emit_flag_write(ctx, eng, ex_id):
    if not ctx.fused:
        return
    fl = ctx.io["flags"]
    nb = ctx.consts["nonce"]
    fsem = ctx.sem("flagw")
    slot_dma(ctx, eng, fsem,
             (fl[0:1, ex_id * 16:ex_id * 16 + 1], fl[1:2, ex_id * 16:ex_id * 16 + 1]),
             nb[0:1, ex_id:ex_id + 1])
    fsem.wait(eng)


def emit_poll(ctx, eng, ex_id):
    if not ctx.fused:
        return
    consumer = eng
    g = ctx.g
    fl = ctx.io["flags"]
    nb = ctx.consts["nonce"]
    if "pollbuf" not in ctx.consts:
        ctx.consts["pollbuf"] = ctx.sbuf("c_pollbuf", [1, 2], I32)
    fs = ctx.consts["pollbuf"]
    nreg, f0, f1 = ctx.poll_regs(g)
    g.reg_load(nreg, nb[0:1, ex_id:ex_id + 1])
    u = ctx.uid()
    lp, chk, done, cont = f"plp{u}", f"plk{u}", f"pld{u}", f"plc{u}"
    src = fl[0:2, ex_id * 16:ex_id * 16 + 1].rearrange("a b -> b a")
    engT = g.engine
    pd = ctx.sem("pdone")
    pp = ctx.sem("ppoll")
    g.br(lp)
    with ctx.nc.body(lp, valid_engines=[engT]):
        with ctx.nc.allow_non_contiguous_dma(reason="2-element flag poll"):
            g.dma_start(out=fs[0:1, 0:2], in_=src).then_inc(pp.h, 16)
        g.reg_load(f0, fs[0:1, 0:1])
        g.reg_load(f1, fs[0:1, 1:2])
        g.br_cmp(f0, nreg, on_true=chk, on_false=lp, comp_op="IS_EQ")
    with ctx.nc.body(chk, valid_engines=[engT]):
        g.br_cmp(f1, nreg, on_true=done, on_false=lp, comp_op="IS_EQ")
    with ctx.nc.body(done, valid_engines=[engT]):
        pd.inc(g.nop())
    ctx.nc.switch_bb(f"plafter{u}")
    if consumer is not g:
        pd.wait(consumer)


def _fm(x_btd, b, half):
    sl = x_btd[b, half * T:(half + 1) * T, :]
    return np.ascontiguousarray(sl.T.reshape(DT, 128, T))


def _col8(vec):
    return np.ascontiguousarray(vec.reshape(DT, 128).T)


def prepare_core_inputs(inp, core, nonce):
    b, h = core // 2, core % 2
    d = {}
    d["x0"] = _fm(np.asarray(inp["L0_attn_x"]), b, h)
    d["nonce"] = nonce
    d["ones128"] = np.ones((128, 1), np.float32)
    cp = np.zeros((128, 8), np.float32)
    cp[:, 0] = EPS
    cp[:, 1] = -0.5
    cp[:, 2] = 0.7978845608028654
    cp[:, 3] = 0.5
    cp[:, 4] = -1.0
    cp[:, 5] = 1.0
    cp[:, 6] = -128.0
    d["cpool"] = cp
    d["id128"] = np.eye(128, dtype=np.float32)

    hs = slice(h * 512, h * 512 + 512)
    cs = slice(h * 1024, h * 1024 + 1024)

    wqkv = np.asarray(inp["L0_attn_W_qkv"])
    wq, wk, wv = wqkv[0:D], wqkv[D:2 * D], wqkv[2 * D:3 * D]
    d["l0_wqT"] = np.ascontiguousarray(wq[hs].T)
    d["l0_wkT"] = np.ascontiguousarray(wk[hs].T)
    d["l0_wvT"] = np.ascontiguousarray(wv[hs].T)
    bqkv = np.asarray(inp["L0_attn_b_qkv"])
    d["l0_bq"] = np.ascontiguousarray(bqkv[0:D][hs].reshape(4, 128).T)
    d["l0_bk"] = np.ascontiguousarray(bqkv[D:2 * D][hs].reshape(4, 128).T)
    d["l0_bv"] = np.ascontiguousarray(bqkv[2 * D:3 * D][hs].reshape(4, 128).T)
    d["l0_bv_row"] = np.ascontiguousarray(bqkv[2 * D:3 * D][hs].reshape(1, 512))
    d["l0_woT"] = np.ascontiguousarray(np.asarray(inp["L0_attn_W_o"]).T[hs])
    d["l0_bo"] = _col8(np.asarray(inp["L0_attn_b_o"]))

    for L in (0, 3):
        p = f"L{L}_ffn_"
        d[f"l{L}f_g"] = _col8(np.asarray(inp[p + "ln_gamma"]))
        d[f"l{L}f_b"] = _col8(np.asarray(inp[p + "ln_beta"]))
        d[f"l{L}f_q1"] = np.asarray(inp[p + "fc1_q"])
        d[f"l{L}f_s1"] = np.asarray(inp[p + "fc1_scales"])
        d[f"l{L}f_b1"] = np.ascontiguousarray(np.asarray(inp[p + "fc1_bias"]).reshape(32, 128).T)
        d[f"l{L}f_q2"] = np.asarray(inp[p + "fc2_q"])
        d[f"l{L}f_s2"] = np.asarray(inp[p + "fc2_scales"])
        d[f"l{L}f_b2"] = _col8(np.asarray(inp[p + "fc2_bias"]))

    w_in = np.asarray(inp["L1_attn_W_in"])
    d["l1_wgT"] = np.ascontiguousarray(w_in[:, 0:DI][:, cs])
    d["l1_wiT"] = np.ascontiguousarray(w_in[:, DI:2 * DI][:, cs])
    cw = np.asarray(inp["L1_attn_conv_w"])[:, 0, :][cs]
    d["l1_cw"] = np.ascontiguousarray(cw.reshape(8, 128, 4).transpose(1, 0, 2).reshape(128, 32))
    d["l1_cb"] = np.ascontiguousarray(np.asarray(inp["L1_attn_conv_b"])[cs].reshape(8, 128).T)
    d["l1_wd"] = np.ascontiguousarray(np.asarray(inp["L1_attn_W_delta"])[cs])
    d["l1_wb"] = np.ascontiguousarray(np.asarray(inp["L1_attn_W_B"])[cs])
    d["l1_wc"] = np.ascontiguousarray(np.asarray(inp["L1_attn_W_C"])[cs])
    d["l1_D"] = np.ascontiguousarray(np.asarray(inp["L1_attn_D"])[cs].reshape(8, 128).T)
    d["l1_bd"] = np.ascontiguousarray(np.asarray(inp["L1_attn_b_delta"]).reshape(16, 1))
    d["l1_wout"] = _col8(np.asarray(inp["L1_attn_w_out"]))
    d["l1_logA"] = (-0.5 * np.log1p(np.arange(1, NS1 + 1, dtype=np.float64) / NS1)
                    ).astype(np.float32).reshape(16, 1)

    d["moe_gwT"] = np.ascontiguousarray(np.asarray(inp["L1_ffn_gate_W"]))
    d["moe_gb"] = np.ascontiguousarray(np.asarray(inp["L1_ffn_gate_b"]).reshape(8, 1))
    d["moe_w1"] = np.asarray(inp["L1_ffn_W1"])
    d["moe_b1"] = np.ascontiguousarray(
        np.asarray(inp["L1_ffn_b1"]).reshape(8, 32, 128).transpose(0, 2, 1))
    d["moe_w2"] = np.asarray(inp["L1_ffn_W2"])
    d["moe_b2"] = np.asarray(inp["L1_ffn_b2"])
    sel8 = np.zeros((8, 8, 128), np.float32)
    for e in range(8):
        sel8[e, e, :] = 1.0
    d["moe_sel8"] = np.ascontiguousarray(sel8.transpose(1, 0, 2).reshape(8, 1024))
    d["tri8"] = np.ascontiguousarray(np.tril(np.ones((8, 8), np.float32), -1).T)

    d["l2_wqT"] = np.ascontiguousarray(np.asarray(inp["L2_attn_Wq"])[hs].T)
    d["l2_wkT"] = np.ascontiguousarray(np.asarray(inp["L2_attn_Wk"])[hs].T)
    d["l2_wvT"] = np.ascontiguousarray(np.asarray(inp["L2_attn_Wv"])[hs].T)
    d["l2_woT"] = np.ascontiguousarray(np.asarray(inp["L2_attn_Wo"]).T[hs])
    inv = 1.0 / 10000.0 ** (np.arange(0, DH, 2, dtype=np.float64) / DH)
    tpos = np.arange(S, dtype=np.float64)
    fr = np.outer(inv, tpos)
    emb = np.concatenate([fr, fr], axis=0)
    cos1 = np.cos(emb).astype(np.float32)
    sin1 = np.sin(emb).astype(np.float32)
    d["rope_cos"] = np.ascontiguousarray(np.concatenate([cos1, cos1], axis=0))
    d["rope_sin"] = np.ascontiguousarray(np.concatenate([sin1, sin1], axis=0))
    P = np.zeros((DH, DH), np.float32)
    for i in range(32):
        P[32 + i, i] = -1.0
        P[i, 32 + i] = 1.0
    P2 = np.zeros((128, 128), np.float32)
    P2[0:64, 0:64] = P
    P2[64:128, 64:128] = P
    d["ropeP"] = P2
    kk = np.arange(128).reshape(128, 1, 1)
    qq = np.arange(512).reshape(1, 1, 512)
    jj = np.arange(4).reshape(1, 4, 1)
    d["cmask"] = np.ascontiguousarray(
        (kk + jj * 128 <= qq).astype(np.float32).reshape(128, 4 * 512))
    hsel = np.zeros((128, 2), np.float32)
    hsel[0:64, 0] = 1.0
    hsel[64:128, 1] = 1.0
    d["half_sel"] = hsel

    d["l2f_u1"] = np.asarray(inp["L2_ffn_U1"])
    d["l2f_v1"] = np.asarray(inp["L2_ffn_V1"])
    d["l2f_b1"] = np.ascontiguousarray(np.asarray(inp["L2_ffn_b1"]).reshape(32, 128).T)
    d["l2f_u2"] = np.asarray(inp["L2_ffn_U2"])
    d["l2f_v2"] = np.asarray(inp["L2_ffn_V2"])
    d["l2f_b2"] = _col8(np.asarray(inp["L2_ffn_b2"]))

    es = cs
    d["l3_normw"] = _col8(np.asarray(inp["L3_attn_norm_w"]))
    d["l3_skipT"] = np.ascontiguousarray(np.asarray(inp["L3_attn_skip_w"])[es].T)
    d["l3_inT"] = np.ascontiguousarray(np.asarray(inp["L3_attn_in_w"])[es].T)
    cw3 = np.asarray(inp["L3_attn_conv_w"])[:, 0, :][es]
    d["l3_cw"] = np.ascontiguousarray(cw3.reshape(8, 128, 4).transpose(1, 0, 2).reshape(128, 32))
    d["l3_cb"] = np.ascontiguousarray(np.asarray(inp["L3_attn_conv_b"])[es].reshape(8, 128).T)
    d["l3_wd1T"] = np.ascontiguousarray(np.asarray(inp["L3_attn_Wd1"]).T[es])
    d["l3_wd2T"] = np.ascontiguousarray(np.asarray(inp["L3_attn_Wd2_w"])[es].T)
    d["l3_wd2b"] = np.ascontiguousarray(np.asarray(inp["L3_attn_Wd2_b"])[es].reshape(8, 128).T)
    d["l3_wbT"] = np.ascontiguousarray(np.asarray(inp["L3_attn_WB"]).T[es])
    d["l3_wcT"] = np.ascontiguousarray(np.asarray(inp["L3_attn_WC"]).T[es])
    d["l3_alog"] = np.ascontiguousarray(
        np.asarray(inp["L3_attn_A_log"])[es].reshape(8, 128, 16).transpose(1, 0, 2).reshape(128, 128))
    d["l3_WD"] = np.ascontiguousarray(np.asarray(inp["L3_attn_W_D"])[es].reshape(8, 128).T)
    d["l3_outT"] = np.ascontiguousarray(np.asarray(inp["L3_attn_out_w"]).T[es])
    return d


def declare_inputs(ctx, sample, names=None):
    nc = ctx.nc
    for k, a in sample.items():
        if names is not None and k not in names:
            continue
        dt = {np.dtype(np.float32): F32, np.dtype(np.int32): I32,
              np.dtype(BF): BF16}[a.dtype]
        ctx.io[k] = nc.dram_tensor(k, list(a.shape), dt, kind="ExternalInput").ap()


def declare_work(ctx, external=()):
    nc = ctx.nc
    ext = dict(e.split(":") for e in external)

    def dram(name, shape, dtype, shared=False):
        if name in ctx.io:
            return
        kind = {"in": "ExternalInput", "out": "ExternalOutput"}.get(ext.get(name), "Internal")
        space = "Shared" if (shared and ctx.fused) else "Local"
        ctx.io[name] = nc.dram_tensor(name, shape, dtype, kind=kind,
                                      addr_space=space).ap()

    if "out" not in ctx.io:
        ctx.io["out"] = nc.dram_tensor("out", [DT, 128, T], F32,
                                       kind="ExternalOutput").ap()
    dram("x_dram", [DT, 128, T], F32)
    dram("xn32_dram", [DT, 128, T], F32)
    for l in range(4):
        dram(f"g_dram{l}", [DT, 128, T], F32)
        dram(f"s_own{l}", [DT, 128, T], F32)
        dram(f"xn_sh{l}", [2, DT, 128, T], BF16, shared=True)
        dram(f"s_sh{l}", [2, DT, 128, T], F32, shared=True)
    dram("l1_dbc_sh", [2, 4, 16, S], F32, shared=True)
    for c in range(4):
        dram(f"l3_dbc_sh{c}", [2, 96, 1024], F32, shared=True)
    if ctx.fused:
        ctx.io["flags"] = nc.dram_tensor("flags", [2, 64 * 16], I32,
                                         addr_space="Shared").ap()


def setup_consts(ctx):
    c = ctx.consts
    c["ones128"] = ctx.sbuf("c_ones128", [128, 1], BF16)
    c["ones1"] = ctx.sbuf("c_ones1", [1, 128], F32)
    c["ones1b"] = ctx.sbuf("c_ones1b", [1, 128], BF16)
    c["cpool"] = ctx.sbuf("c_cpool", [128, 8], F32)
    if ctx.fused:
        c["nonce"] = ctx.sbuf("c_nonce", [1, 64], I32)
        c["pollbuf"] = ctx.sbuf("c_pollbuf", [1, 2], I32)
    ld = ctx.sem("const_ld")
    ld.inc(ctx.g.dma_start(out=c["ones128"][:], in_=ctx.io["ones128"]), 16)
    ld.inc(ctx.sp.dma_start(out=c["ones1"][:],
                            in_=ctx.io["ones128"].rearrange("a b -> b a")), 16)
    ld.inc(ctx.g.dma_start(out=c["ones1b"][:],
                           in_=ctx.io["ones128"].rearrange("a b -> b a")), 16)
    ld.inc(ctx.sp.dma_start(out=c["cpool"][:], in_=ctx.io["cpool"]), 16)
    if ctx.fused:
        ld.inc(ctx.sp.dma_start(out=c["nonce"][:], in_=ctx.io["nonce"]), 16)
    for e in (ctx.t, ctx.v, ctx.a, ctx.g, ctx.sp):
        ld.wait(e)


def emit_ln(ctx, layer, x_src, store_f32=False):
    nc, t, v, a, sp = ctx.nc, ctx.t, ctx.v, ctx.a, ctx.sp
    xn_sh = ctx.io[f"xn_sh{layer}"]
    xn32 = ctx.io.get("xn32_dram")
    with contextlib.ExitStack() as st:
        sb = lambda n, s, dt_: st.enter_context(nc.sbuf_tensor(f"ln{layer}{n}", s, dt_))
        xt = sb("xt", [128, DT * 512], F32)
        xw = sb("xw", [128, DT * 512], F32)
        xb = sb("xb", [128, DT * 512], BF16)
        sq = sb("sq", [128, DT * 512], BF16)
        rows = sb("rw", [1, 512 * 7], F32)
        xnt = sb("xn", [128, DT * 512], BF16)
        ps_s = st.enter_context(nc.psum_tensor(f"ln{layer}ps", [1, 1024], F32))
        ps_b = st.enter_context(nc.psum_tensor(f"ln{layer}pb", [128, 1024], F32))
        ld = ctx.sem(f"ln{layer}_ld")
        cv = ctx.sem(f"ln{layer}_cv")
        mm = ctx.sem(f"ln{layer}_mm")
        rw = ctx.sem(f"ln{layer}_rw")
        bc = ctx.sem(f"ln{layer}_bc")
        xn = ctx.sem(f"ln{layer}_xn")
        so = ctx.sem(f"ln{layer}_st")
        R = lambda i: rows[0:1, i * 512:(i + 1) * 512]
        for c in range(4):
            c0 = c * 512
            for k in range(DT):
                ld.inc(sp.dma_start(out=xt[:, k * 512:(k + 1) * 512],
                                    in_=x_src[k, :, c0:c0 + 512]), 16)
            ld.wait(v)
            ld.wait(a)
            for k in range(DT):
                s = slice(k * 512, (k + 1) * 512)
                cv.inc(v.tensor_copy(out=xb[:, s], in_=xt[:, s]))
                cv.inc(a.activation(out=sq[:, s], in_=xb[:, s], func=AF.Square))
            cv.wait(t)
            for k in range(DT):
                s = slice(k * 512, (k + 1) * 512)
                i1 = t.matmul(ps_s[0:1, 0:512], ctx.consts["ones128"][:], xb[:, s],
                              start=(k == 0), stop=(k == DT - 1))
                i2 = t.matmul(ps_s[0:1, 512:1024], ctx.consts["ones128"][:], sq[:, s],
                              start=(k == 0), stop=(k == DT - 1))
            mm.inc(i1)
            mm.inc(i2)
            mm.wait(v)
            v.tensor_scalar_mul(out=R(2), in0=ps_s[0:1, 0:512], scalar1=1.0 / D)
            v.tensor_scalar_mul(out=R(1), in0=ps_s[0:1, 512:1024], scalar1=1.0 / D)
            v.tensor_mul(out=R(0), in0=R(2), in1=R(2))
            rw.inc(v.tensor_sub(out=R(3), in0=R(1), in1=R(0)))
            rw.wait(a)
            a.activation(out=R(6), in_=R(3), func=AF.Ln,
                         bias=ctx.consts["cpool"][0:1, 0:1])
            rw.inc(a.activation(out=R(4), in_=R(6), func=AF.Exp,
                                scale=ctx.consts["cpool"][0:1, 1:2]))
            rw.wait(v)
            rw.inc(v.scalar_tensor_tensor(out=R(5), in0=R(2), scalar=-1.0, in1=R(4),
                                          op0=ALU.mult, op1=ALU.mult))
            rw.wait(t)
            bc.inc(t.matmul(ps_b[:, 0:512], ctx.consts["ones1"][:], R(4),
                            start=True, stop=True))
            bc.inc(t.matmul(ps_b[:, 512:1024], ctx.consts["ones1"][:], R(5),
                            start=True, stop=True))
            bc.wait(v)
            so.wait(v)
            for k in range(DT):
                s = slice(k * 512, (k + 1) * 512)
                v.tensor_mul(out=xw[:, s], in0=xt[:, s], in1=ps_b[:, 0:512])
                if store_f32:
                    v.tensor_add(out=xw[:, s], in0=xw[:, s], in1=ps_b[:, 512:1024])
                    xn.inc(v.tensor_copy(out=xnt[:, s], in_=xw[:, s]))
                else:
                    xn.inc(v.tensor_add(out=xnt[:, s], in0=xw[:, s], in1=ps_b[:, 512:1024]))
            xn.wait(sp)
            for k in range(DT):
                slot0 = xn_sh[0, k, :, c0:c0 + 512]
                slot1 = xn_sh[1, k, :, c0:c0 + 512]
                slot_dma(ctx, sp, so, (slot0, slot1), xnt[:, k * 512:(k + 1) * 512])
                if store_f32:
                    so.inc(sp.dma_start(out=xn32[k, :, c0:c0 + 512],
                                        in_=xw[:, k * 512:(k + 1) * 512]), 16)
        for _e in (t, v, a, sp, ctx.g):
            so.wait(_e)
        emit_flag_write(ctx, ctx.g, layer * 8)
    return so


def emit_ffn03(ctx, layer):
    L = layer
    nc, t, v, a, sp, g = ctx.nc, ctx.t, ctx.v, ctx.a, ctx.sp, ctx.g
    xn_sh = ctx.io[f"xn_sh{L}"]
    g_dram = ctx.io[f"g_dram{L}"]
    q1, s1 = ctx.io[f"l{L}f_q1"], ctx.io[f"l{L}f_s1"]
    q2, s2 = ctx.io[f"l{L}f_q2"], ctx.io[f"l{L}f_s2"]
    with contextlib.ExitStack() as st:
        sb = lambda n, s_, dt_: st.enter_context(nc.sbuf_tensor(f"f{L}{n}", s_, dt_))
        xnt = sb("xn", [128, DT * 512], BF16)
        hbuf = sb("h", [128, DT * T], BF16)
        sq = sb("sq", [128, DT * 512], BF16)
        rows = sb("rw", [1, 512 * 7], F32)
        acc = sb("acc", [128, DT * T], F32)
        w1q = sb("w1q", [128, 8 * 512], BF16)
        s1r = sb("s1r", [128, 8 * 512], BF16)
        w1d = sb("w1d", [128, 2 * 8 * 512], BF16)
        w2q = sb("w2q", [128, 4 * 1024], BF16)
        s2r = sb("s2r", [128, 4 * 1024], BF16)
        w2d = sb("w2d", [128, 2 * 4 * 1024], BF16)
        gact = sb("ga", [128, 2 * 4 * 512], BF16)
        ps_s = st.enter_context(nc.psum_tensor(f"f{L}ps", [1, 1024], F32))
        ps_b = st.enter_context(nc.psum_tensor(f"f{L}pb", [128, 1024], F32))
        ps1 = st.enter_context(nc.psum_tensor(f"f{L}p1", [128, 1024], F32))
        ps2 = st.enter_context(nc.psum_tensor(f"f{L}p2", [128, 1024], F32))
        ld = ctx.sem(f"f{L}_ld")
        wl = ctx.sem(f"f{L}_wl")
        dq = ctx.sem(f"f{L}_dq")
        cv = ctx.sem(f"f{L}_cv")
        mm = ctx.sem(f"f{L}_mm")
        rw = ctx.sem(f"f{L}_rw")
        bc = ctx.sem(f"f{L}_bc")
        m1 = ctx.sem(f"f{L}_m1")
        e1 = ctx.sem(f"f{L}_e1")
        m2 = ctx.sem(f"f{L}_m2")
        e2 = ctx.sem(f"f{L}_e2")
        go = ctx.sem(f"f{L}_go")
        so = ctx.sem(f"f{L}_so")
        R = lambda i: rows[0:1, i * 512:(i + 1) * 512]
        gcol_sb = sb("gc", [128, 16], F32)
        b1col_sb = sb("b1c", [128, 32], F32)
        b2col_sb = sb("b2c", [128, 8], F32)
        ld.inc(sp.dma_start(out=gcol_sb[:, 0:8], in_=ctx.io[f"l{L}f_g"]), 16)
        ld.inc(sp.dma_start(out=gcol_sb[:, 8:16], in_=ctx.io[f"l{L}f_b"]), 16)
        ld.inc(sp.dma_start(out=b1col_sb[:], in_=ctx.io[f"l{L}f_b1"]), 16)
        ld.inc(sp.dma_start(out=b2col_sb[:], in_=ctx.io[f"l{L}f_b2"]), 16)

        def load_own_xn(c0, e2s=None):
            e1.wait(sp)
            for k in range(DT):
                dst = xnt[:, k * 512:(k + 1) * 512]
                par = ctx.par_reg(sp)
                with sp.If_cmp(par, 0, "IS_EQ"):
                    ld.inc(sp.dma_start(out=dst, in_=xn_sh[0, k, :, c0:c0 + 512]), 16)
                with sp.Else():
                    sp.dma_start(out=dst, in_=xn_sh[1, k, :, c0:c0 + 512]).then_inc(ld.h, 16)

        for c in range(4):
            c0 = c * 512
            load_own_xn(c0)
            ld.wait(a)
            for k in range(DT):
                src = xnt[:, k * 512:(k + 1) * 512]
                cv.inc(a.activation(out=sq[:, k * 512:(k + 1) * 512], in_=src,
                                    func=AF.Square))
            cv.wait(t)
            ld.wait(t)
            for k in range(DT):
                src = xnt[:, k * 512:(k + 1) * 512]
                i1 = t.matmul(ps_s[0:1, 0:512], ctx.consts["ones128"][:], src,
                              start=(k == 0), stop=(k == DT - 1))
                i2 = t.matmul(ps_s[0:1, 512:1024], ctx.consts["ones128"][:],
                              sq[:, k * 512:(k + 1) * 512],
                              start=(k == 0), stop=(k == DT - 1))
            mm.inc(i1)
            mm.inc(i2)
            mm.wait(v)
            v.tensor_scalar_mul(out=R(2), in0=ps_s[0:1, 0:512], scalar1=1.0 / D)
            v.tensor_scalar_mul(out=R(1), in0=ps_s[0:1, 512:1024], scalar1=1.0 / D)
            v.tensor_mul(out=R(0), in0=R(2), in1=R(2))
            rw.inc(v.tensor_sub(out=R(3), in0=R(1), in1=R(0)))
            rw.wait(a)
            a.activation(out=R(6), in_=R(3), func=AF.Ln,
                         bias=ctx.consts["cpool"][0:1, 0:1])
            rw.inc(a.activation(out=R(4), in_=R(6), func=AF.Exp,
                                scale=ctx.consts["cpool"][0:1, 1:2]))
            rw.wait(v)
            rw.inc(v.scalar_tensor_tensor(out=R(5), in0=R(2), scalar=-1.0, in1=R(4),
                                          op0=ALU.mult, op1=ALU.mult))
            rw.wait(t)
            bc.inc(t.matmul(ps_b[:, 0:512], ctx.consts["ones1"][:], R(4),
                            start=True, stop=True))
            bc.inc(t.matmul(ps_b[:, 512:1024], ctx.consts["ones1"][:], R(5),
                            start=True, stop=True))
            bc.wait(v)
            if "dbg_rows" in ctx.io:
                rw.wait(sp)
                dr = ctx.sem(f"f{L}_dr")
                dr.inc(sp.dma_start(out=ctx.io["dbg_rows"][c, 0:1, :],
                                    in_=rows[0:1, :]), 16)
                dr.inc(sp.dma_start(out=ctx.io["dbg_xnt"][c, :, :],
                                    in_=xnt[:, 0:DT * 512]), 16)
                dr.wait(sp)
            for k in range(DT):
                src = xnt[:, k * 512:(k + 1) * 512]
                scr = sq[:, k * 512:(k + 1) * 512]
                dsth = hbuf[:, k * T + c0:k * T + c0 + 512]
                v.tensor_mul(out=scr, in0=src, in1=ps_b[:, 0:512])
                v.tensor_add(out=dsth, in0=scr, in1=ps_b[:, 512:1024])
                e1.inc(v.tensor_scalar(out=dsth, in0=dsth, scalar1=gcol_sb[:, k:k + 1],
                                       scalar2=gcol_sb[:, 8 + k:9 + k],
                                       op0=ALU.mult, op1=ALU.add))

        if "dbg_h" in ctx.io:
            e1.wait(sp)
            dbs = ctx.sem(f"f{L}_db")
            for kh in range(DT):
                dbs.inc(sp.dma_start(out=ctx.io["dbg_h"][kh, :, :],
                                     in_=hbuf[:, kh * T:(kh + 1) * T]), 16)
            dbs.wait(sp)
        NFB = 8
        for fb in range(NFB + 1):
            pb = fb % 2
            if fb < NFB:
                f0 = fb * 512
                dq.wait(g)
                for k in range(DT):
                    wl.inc(g.dma_start(out=w1q[:, k * 512:(k + 1) * 512],
                                       in_=q1[k * 128:(k + 1) * 128, f0:f0 + 512]), 16)
                    wl.inc(g.dma_start(
                        out=s1r[:, k * 512:(k + 1) * 512],
                        in_=s1[k:k + 1, f0:f0 + 512].broadcast_to((128, 512))), 16)
                for fk in range(4):
                    fr0 = f0 + fk * 128
                    wl.inc(g.dma_start(out=w2q[:, fk * 1024:(fk + 1) * 1024],
                                       in_=q2[fr0:fr0 + 128, :]), 16)
                    wl.inc(g.dma_start(
                        out=s2r[:, fk * 1024:(fk + 1) * 1024],
                        in_=s2[fr0 // 128:fr0 // 128 + 1, :].broadcast_to((128, 1024))), 16)
                wl.wait(v)
                m1.wait(v)
                m2.wait(v)
                for k in range(DT):
                    dq.inc(v.scalar_tensor_tensor(
                        out=w1d[:, (pb * 8 + k) * 512:(pb * 8 + k + 1) * 512],
                        in0=w1q[:, k * 512:(k + 1) * 512], scalar=-128.0,
                        in1=s1r[:, k * 512:(k + 1) * 512],
                        op0=ALU.add, op1=ALU.mult))
                for fk in range(4):
                    dq.inc(v.scalar_tensor_tensor(
                        out=w2d[:, (pb * 4 + fk) * 1024:(pb * 4 + fk + 1) * 1024],
                        in0=w2q[:, fk * 1024:(fk + 1) * 1024], scalar=-128.0,
                        in1=s2r[:, fk * 1024:(fk + 1) * 1024],
                        op0=ALU.add, op1=ALU.mult))
            if fb == 0:
                continue
            qb = (fb - 1) % 2
            fprev = (fb - 1) * 512
            dq.wait(t)
            e1.wait(t)
            for c in range(4):
                c0 = c * 512
                pc = c % 2
                for fk in range(4):
                    pfk = fk % 2
                    for k in range(DT):
                        i1 = t.matmul(ps1[:, pfk * 512:pfk * 512 + 512],
                                      w1d[:, (qb * 8 + k) * 512 + fk * 128:
                                          (qb * 8 + k) * 512 + fk * 128 + 128],
                                      hbuf[:, k * T + c0:k * T + c0 + 512],
                                      start=(k == 0), stop=(k == DT - 1))
                    m1.inc(i1)
                    m1.wait(a)
                    e1.inc(a.activation(
                        out=gact[:, (qb * 4 + fk) * 512:(qb * 4 + fk + 1) * 512],
                        in_=ps1[:, pfk * 512:pfk * 512 + 512], func=AF.Silu,
                        bias=b1col_sb[:, (fprev // 128 + fk):(fprev // 128 + fk) + 1]))
                    e1.wait(t, e1.v - 1)
                e1.wait(t)
                for kh in range(DT):
                    pk = kh % 2
                    for fk in range(4):
                        i2 = t.matmul(ps2[:, pk * 512:pk * 512 + 512],
                                      w2d[:, (qb * 4 + fk) * 1024 + kh * 128:
                                          (qb * 4 + fk) * 1024 + kh * 128 + 128],
                                      gact[:, (qb * 4 + fk) * 512:(qb * 4 + fk + 1) * 512],
                                      start=(fk == 0), stop=(fk == 3))
                    m2.inc(i2)
                    m2.wait(v)
                    dsta = acc[:, kh * T + c0:kh * T + c0 + 512]
                    if fb == 1:
                        e2.inc(v.tensor_copy(out=dsta,
                                             in_=ps2[:, pk * 512:pk * 512 + 512]))
                    else:
                        e2.inc(v.tensor_add(out=dsta, in0=dsta,
                                            in1=ps2[:, pk * 512:pk * 512 + 512]))
                    e2.wait(t, e2.v - 1)

        if "dbg_acc" in ctx.io:
            e2.wait(sp)
            dbs2 = ctx.sem(f"f{L}_db2")
            for kh in range(DT):
                dbs2.inc(sp.dma_start(out=ctx.io["dbg_acc"][kh, :, :],
                                      in_=acc[:, kh * T:(kh + 1) * T]), 16)
            dbs2.wait(sp)
        e2.wait(v)
        for c in range(4):
            c0 = c * 512
            go.wait(sp)
            load_own_xn(c0)
            ld.wait(v)
            for kh in range(DT):
                sl = slice(kh * T + c0, kh * T + c0 + 512)
                v.tensor_scalar(out=acc[:, sl], in0=acc[:, sl],
                                scalar1=b2col_sb[:, kh:kh + 1], scalar2=0.5,
                                op0=ALU.add, op1=ALU.mult)
                go.inc(v.tensor_add(out=hbuf[:, sl], in0=acc[:, sl],
                                    in1=xnt[:, kh * 512:(kh + 1) * 512]))
            go.wait(a)
            so.wait(a)
            for kh in range(DT):
                sl = slice(kh * T + c0, kh * T + c0 + 512)
                go.inc(a.activation(out=acc[:, sl], in_=hbuf[:, sl], func=AF.Silu))
            go.wait(sp)
            for kh in range(DT):
                sl = slice(kh * T + c0, kh * T + c0 + 512)
                so.inc(sp.dma_start(out=g_dram[kh, :, c0:c0 + 512],
                                    in_=acc[:, sl]), 16)
        for _e in (t, v, a, sp, g):
            so.wait(_e)
    return so


def load_xn_chunk(ctx, eng, ld, xn_sh, dst, c0, war_sem=None, slot=None):
    if war_sem is not None:
        war_sem.wait(eng)
    for k in range(DT):
        d = dst[:, k * 512:(k + 1) * 512]
        if slot is not None:
            ld.inc(eng.dma_start(out=d, in_=xn_sh[slot, k, :, c0:c0 + 512]), 16)
        else:
            par = ctx.par_reg(eng)
            with eng.If_cmp(par, 0, "IS_EQ"):
                ld.inc(eng.dma_start(out=d, in_=xn_sh[0, k, :, c0:c0 + 512]), 16)
            with eng.Else():
                eng.dma_start(out=d, in_=xn_sh[1, k, :, c0:c0 + 512]).then_inc(ld.h, 16)


def emit_combine(ctx, layer, has_partner, bias_col=None, add_xn=False,
                 x_src_name="x_dram"):
    nc, v, sp, a = ctx.nc, ctx.v, ctx.sp, ctx.a
    L = layer
    x_dst = ctx.io["out"] if L == 3 else ctx.io["x_dram"]
    x_src = ctx.io[x_src_name]
    s_own = ctx.io[f"s_own{L}"]
    s_sh = ctx.io[f"s_sh{L}"]
    g_dram = ctx.io[f"g_dram{L}"]
    xn_sh = ctx.io[f"xn_sh{L}"]
    if has_partner:
        emit_poll(ctx, sp, L * 8 + 1)
    with contextlib.ExitStack() as st:
        sb = lambda n, s_, dt_: st.enter_context(nc.sbuf_tensor(f"cb{L}{n}", s_, dt_))
        xt = sb("x", [128, DT * 512], F32)
        st1 = sb("s1", [128, DT * 512], F32)
        st2 = sb("s2", [128, DT * 512], F32)
        gt = sb("g", [128, DT * 512], F32)
        xnt = sb("xn", [128, DT * 512], BF16) if add_xn else None
        stot = sb("st", [128, DT * 512], F32)
        bcol = None
        ld = ctx.sem(f"cb{L}_ld")
        dn = ctx.sem(f"cb{L}_dn")
        so = ctx.sem(f"cb{L}_so")
        if bias_col is not None:
            bcol = sb("bc", [128, 8], F32)
            ld.inc(sp.dma_start(out=bcol[:], in_=ctx.io[bias_col]), 16)
        for c in range(4):
            c0 = c * 512
            dn.wait(sp)
            so.wait(sp)
            for k in range(DT):
                s = slice(k * 512, (k + 1) * 512)
                ld.inc(sp.dma_start(out=xt[:, s], in_=x_src[k, :, c0:c0 + 512]), 16)
                ld.inc(sp.dma_start(out=st1[:, s], in_=s_own[k, :, c0:c0 + 512]), 16)
                ld.inc(sp.dma_start(out=gt[:, s], in_=g_dram[k, :, c0:c0 + 512]), 16)
                if has_partner:
                    other_slot_dma(ctx, sp, ld, st2[:, s],
                                   (s_sh[0, k, :, c0:c0 + 512], s_sh[1, k, :, c0:c0 + 512]))
            if add_xn:
                load_xn_chunk(ctx, sp, ld, xn_sh, xnt, c0)
            ld.wait(v)
            for k in range(DT):
                s = slice(k * 512, (k + 1) * 512)
                if has_partner:
                    v.tensor_add(out=stot[:, s], in0=st1[:, s], in1=st2[:, s])
                else:
                    v.tensor_copy(out=stot[:, s], in_=st1[:, s])
                if bias_col is not None:
                    v.tensor_scalar_add(out=stot[:, s], in0=stot[:, s],
                                        scalar1=bcol[:, k:k + 1])
                if add_xn:
                    v.tensor_add(out=stot[:, s], in0=stot[:, s], in1=xnt[:, s])
                if f"dbg_s{L}" in ctx.io:
                    dn.inc(v.tensor_copy(out=st1[:, s], in_=stot[:, s]))
                v.tensor_mul(out=stot[:, s], in0=stot[:, s], in1=gt[:, s])
                dn.inc(v.tensor_add(out=xt[:, s], in0=xt[:, s], in1=stot[:, s]))
            dn.wait(sp)
            if f"dbg_s{L}" in ctx.io:
                for k in range(DT):
                    s = slice(k * 512, (k + 1) * 512)
                    dn.inc(sp.dma_start(out=ctx.io[f"dbg_s{L}"][k, :, c0:c0 + 512],
                                        in_=st1[:, s]), 16)
                    dn.inc(sp.dma_start(out=ctx.io[f"dbg_g{L}"][k, :, c0:c0 + 512],
                                        in_=gt[:, s]), 16)
            for k in range(DT):
                s = slice(k * 512, (k + 1) * 512)
                so.inc(sp.dma_start(out=x_dst[k, :, c0:c0 + 512], in_=xt[:, s]), 16)
        for _e in (v, sp, a):
            so.wait(_e)
    return so


def emit_ffn_l2(ctx):
    nc, t, v, a, sp, g = ctx.nc, ctx.t, ctx.v, ctx.a, ctx.sp, ctx.g
    xn_sh = ctx.io["xn_sh2"]
    g_dram = ctx.io["g_dram2"]
    C_G = 0.044715
    with contextlib.ExitStack() as st:
        sb = lambda n, s_, dt_: st.enter_context(nc.sbuf_tensor(f"q2{n}", s_, dt_))
        xnt = sb("xn", [128, DT * 512], BF16)
        u1 = sb("u1", [128, DT * 256], BF16)
        v1 = sb("v1", [128, 2 * 4096], BF16)
        u2 = sb("u2", [128, 32 * 256], BF16)
        v2 = sb("v2", [128, 2 * 1024], BF16)
        b1c = sb("b1", [128, 32], F32)
        b2c = sb("b2", [128, 8], F32)
        pb = sb("pb", [128, 2 * 512], BF16)
        hb = sb("hb", [128, 32 * 512], BF16)
        qb = sb("qb", [128, 2 * 512], BF16)
        xg = sb("xg", [128, 512], BF16)
        x2 = sb("x2", [128, 512], BF16)
        wk = sb("wk", [128, 512], BF16)
        tn = sb("tn", [128, 512], BF16)
        ob = sb("ob", [128, DT * 512], BF16)
        ps_p = st.enter_context(nc.psum_tensor("q2pp", [128, 1024], F32))
        ps_h = st.enter_context(nc.psum_tensor("q2ph", [128, 1024], F32))
        ps_q = st.enter_context(nc.psum_tensor("q2pq", [128, 1024], F32))
        ps_o = st.enter_context(nc.psum_tensor("q2po", [128, 1024], F32))
        wl = ctx.sem("q2_wl")
        ld = ctx.sem("q2_ld")
        mp = ctx.sem("q2_mp")
        ep = ctx.sem("q2_ep")
        mh = ctx.sem("q2_mh")
        eh = ctx.sem("q2_eh")
        mq = ctx.sem("q2_mq")
        eq = ctx.sem("q2_eq")
        mo = ctx.sem("q2_mo")
        eo = ctx.sem("q2_eo")
        so = ctx.sem("q2_so")
        wl.inc(g.dma_start(out=u1[:].rearrange("p (a r) -> p a r", a=DT),
                           in_=ctx.io["l2f_u1"].rearrange("(a p) r -> p a r", p=128)), 16)
        wl.inc(g.dma_start(out=v1[:].rearrange("p (a f) -> p a f", a=2),
                           in_=ctx.io["l2f_v1"].rearrange("(a p) f -> p a f", p=128)), 16)
        wl.inc(g.dma_start(out=u2[:].rearrange("p (a r) -> p a r", a=32),
                           in_=ctx.io["l2f_u2"].rearrange("(a p) r -> p a r", p=128)), 16)
        wl.inc(g.dma_start(out=v2[:].rearrange("p (a d2) -> p a d2", a=2),
                           in_=ctx.io["l2f_v2"].rearrange("(a p) d2 -> p a d2", p=128)), 16)
        ld.inc(sp.dma_start(out=b1c[:], in_=ctx.io["l2f_b1"]), 16)
        ld.inc(sp.dma_start(out=b2c[:], in_=ctx.io["l2f_b2"]), 16)
        LAM = ctx.consts["cpool"][:, 2:3]
        for c in range(4):
            c0 = c * 512
            load_xn_chunk(ctx, sp, ld, xn_sh, xnt, c0, war_sem=mp)
            so.wait(v)
            so.wait(a)
            eo.wait(v)
            ld.wait(t)
            wl.wait(t)
            for r in range(2):
                for k in range(DT):
                    i1 = t.matmul(ps_p[:, r * 512:(r + 1) * 512],
                                  u1[:, k * 256 + r * 128:k * 256 + r * 128 + 128],
                                  xnt[:, k * 512:(k + 1) * 512],
                                  start=(k == 0), stop=(k == DT - 1))
                mp.inc(i1)
            mp.wait(v)
            for r in range(2):
                ep.inc(v.tensor_copy(out=pb[:, r * 512:(r + 1) * 512],
                                     in_=ps_p[:, r * 512:(r + 1) * 512]))
            ep.wait(t)
            for f in range(32):
                pf = f % 2
                for r in range(2):
                    i2 = t.matmul(ps_h[:, pf * 512:(pf + 1) * 512],
                                  v1[:, r * 4096 + f * 128:r * 4096 + f * 128 + 128],
                                  pb[:, r * 512:(r + 1) * 512],
                                  start=(r == 0), stop=(r == 1))
                mh.inc(i2)
                mh.wait(v)
                eh.inc(v.tensor_scalar_add(out=xg[:], in0=ps_h[:, pf * 512:(pf + 1) * 512],
                                           scalar1=b1c[:, f:f + 1]))
                eh.wait(a)
                eh.inc(a.activation(out=x2[:], in_=xg[:], func=AF.Square))
                eh.wait(v)
                v.tensor_scalar(out=wk[:], in0=x2[:], scalar1=C_G, scalar2=1.0,
                                op0=ALU.mult, op1=ALU.add)
                eh.inc(v.tensor_mul(out=wk[:], in0=wk[:], in1=xg[:]))
                eh.wait(a)
                eh.inc(a.activation(out=tn[:], in_=wk[:], func=AF.Tanh, scale=LAM))
                eh.wait(v)
                v.tensor_scalar(out=tn[:], in0=tn[:], scalar1=1.0, scalar2=0.5,
                                op0=ALU.add, op1=ALU.mult)
                eh.inc(v.tensor_mul(out=hb[:, f * 512:(f + 1) * 512], in0=tn[:],
                                    in1=xg[:]))
                eh.wait(t)
            for r in range(2):
                for f in range(32):
                    i3 = t.matmul(ps_q[:, r * 512:(r + 1) * 512],
                                  u2[:, f * 256 + r * 128:f * 256 + r * 128 + 128],
                                  hb[:, f * 512:(f + 1) * 512],
                                  start=(f == 0), stop=(f == 31))
                mq.inc(i3)
            mq.wait(v)
            for r in range(2):
                eq.inc(v.tensor_copy(out=qb[:, r * 512:(r + 1) * 512],
                                     in_=ps_q[:, r * 512:(r + 1) * 512]))
            eq.wait(t)
            for kh in range(DT):
                po = kh % 2
                for r in range(2):
                    i4 = t.matmul(ps_o[:, po * 512:(po + 1) * 512],
                                  v2[:, r * 1024 + kh * 128:r * 1024 + kh * 128 + 128],
                                  qb[:, r * 512:(r + 1) * 512],
                                  start=(r == 0), stop=(r == 1))
                mo.inc(i4)
                mo.wait(v)
                eo.inc(v.tensor_scalar_add(out=ob[:, kh * 512:(kh + 1) * 512],
                                           in0=ps_o[:, po * 512:(po + 1) * 512],
                                           scalar1=b2c[:, kh:kh + 1]))
                eo.wait(t)
            eo.wait(a)
            for kh in range(DT):
                eo.inc(a.activation(out=hb[:, kh * 512:(kh + 1) * 512],
                                    in_=ob[:, kh * 512:(kh + 1) * 512], func=AF.Silu))
            eo.wait(g)
            for kh in range(DT):
                so.inc(g.dma_start(out=g_dram[kh, :, c0:c0 + 512],
                                    in_=hb[:, kh * 512:(kh + 1) * 512]), 16)
        for _e in (t, v, a, sp, g):
            so.wait(_e)
    return so


def emit_moe(ctx):
    nc, t, v, a, sp, g = ctx.nc, ctx.t, ctx.v, ctx.a, ctx.sp, ctx.g
    xn_sh = ctx.io["xn_sh1"]
    g_dram = ctx.io["g_dram1"]
    with contextlib.ExitStack() as st:
        sb = lambda n, s_, dt_: st.enter_context(nc.sbuf_tensor(f"mo{n}", s_, dt_))
        xnb = sb("xn", [128, DT * T], BF16)
        selrow = sb("sr", [8, T], BF16)
        OUT = sb("out", [128, DT * T], BF16)
        acc = sb("acc", [128, DT * T], BF16)
        b2sb = sb("b2", [8, 1024], BF16)
        gbc = sb("gb", [8, 1], F32)
        sel8 = sb("s8", [8, 1024], BF16)
        idb = sb("idb", [128, 128], BF16)
        mask = sb("mk", [128, 512], U8)
        m1c = sb("m1c", [128, 1], F32)
        ge_t = sb("ge", [128, 8], F32)
        selt = sb("sel", [128, 8], BF16)
        ntk = sb("ntk", [128, 1], BF16)
        ld = ctx.sem("mo_ld")
        wl = ctx.sem("mo_wl")
        gm = ctx.sem("mo_gm")
        gv = ctx.sem("mo_gv")
        m1 = ctx.sem("mo_m1")
        e1 = ctx.sem("mo_e1")
        m2 = ctx.sem("mo_m2")
        e2 = ctx.sem("mo_e2")
        mg = ctx.sem("mo_mg")
        so = ctx.sem("mo_so")
        ld.inc(sp.dma_start(out=gbc[:], in_=ctx.io["moe_gb"]), 16)
        ld.inc(g.dma_start(out=sel8[:], in_=ctx.io["moe_sel8"]), 16)
        ld.inc(g.dma_start(out=idb[:], in_=ctx.io["id128"]), 16)
        ld.inc(g.dma_start(out=b2sb[:], in_=ctx.io["moe_b2"]), 16)
        for c in range(4):
            c0 = c * 512
            for k in range(DT):
                dst = xnb[:, k * T + c0:k * T + c0 + 512]
                par = ctx.par_reg(sp)
                with sp.If_cmp(par, 0, "IS_EQ"):
                    ld.inc(sp.dma_start(out=dst, in_=xn_sh[0, k, :, c0:c0 + 512]), 16)
                with sp.Else():
                    sp.dma_start(out=dst, in_=xn_sh[1, k, :, c0:c0 + 512]).then_inc(ld.h, 16)
        with contextlib.ExitStack() as st2:
            ps_g = st2.enter_context(nc.psum_tensor("mopg", [8, 512], F32))
            ps_t1 = st2.enter_context(nc.psum_tensor("mopt1", [128, 8], F32))
            ps_t2 = st2.enter_context(nc.psum_tensor("mopt2", [8, 128], F32))
            idf = st2.enter_context(nc.sbuf_tensor("moidf", [128, 128], F32))
            lgf = st2.enter_context(nc.sbuf_tensor("molgf", [8, T], F32))
            gw = st2.enter_context(nc.sbuf_tensor("mogw", [128, DT * 8], F32))
            xn32 = st2.enter_context(nc.sbuf_tensor("mox32", [128, DT * 512], F32))
            ld.inc(sp.dma_start(out=gw[:].rearrange("p (a e) -> p a e", a=DT),
                                in_=ctx.io["moe_gwT"].rearrange("(a p) e -> p a e", p=128)), 16)
            ld.inc(sp.dma_start(out=idf[:], in_=ctx.io["id128"]), 16)
            ld.wait(t)
            for c in range(4):
                c0 = c * 512
                gm.wait(sp)
                for k in range(DT):
                    ld.inc(sp.dma_start(out=xn32[:, k * 512:(k + 1) * 512],
                                        in_=ctx.io["xn32_dram"][k, :, c0:c0 + 512]), 16)
                ld.wait(t)
                for k in range(DT):
                    i1 = t.matmul(ps_g[:, 0:512], gw[:, k * 8:(k + 1) * 8],
                                  xn32[:, k * 512:(k + 1) * 512],
                                  start=(k == 0), stop=(k == DT - 1))
                gm.inc(i1)
                gm.wait(v)
                gv.inc(v.tensor_scalar_add(out=lgf[0:8, c0:c0 + 512], in0=ps_g[:, 0:512],
                                           scalar1=gbc[:, 0:1]))
                gv.wait(t)
            tr8 = st2.enter_context(nc.sbuf_tensor("motr8", [8, 8], F32))
            geT = st2.enter_context(nc.sbuf_tensor("mogeT", [8, 128], F32))
            eqb = st2.enter_context(nc.sbuf_tensor("moeq", [8, 128], F32))
            ps_t3 = st2.enter_context(nc.psum_tensor("mopt3", [8, 128], F32))
            ld.inc(sp.dma_start(out=tr8[:], in_=ctx.io["tri8"]), 16)
            ld.wait(t)
            ld.wait(v)
            for stt in range(16):
                t0 = stt * 128
                gv.wait(t)
                gm.inc(t.transpose(ps_t1[:, :], lgf[0:8, t0:t0 + 128], idf[0:8, 0:8]))
                gm.wait(v)
                gv.inc(v.reduce_max(out=m1c[:], in_=ps_t1[:, :],
                                    axis=mybir.AxisListType.X))
                gv.wait(v)
                gv.inc(v.tensor_single_scalar(out=ge_t[:], in_=ps_t1[:, :],
                                              scalar=m1c[:, 0:1], op=ALU.is_ge))
                gv.wait(t)
                gm.inc(t.transpose(ps_t2[:, :], ge_t[:], idf[:]))
                gm.wait(v)
                gv.inc(v.tensor_copy(out=geT[:], in_=ps_t2[:, :]))
                gv.wait(t)
                gm.inc(t.matmul(ps_t3[:, :], tr8[:], geT[:], start=True, stop=True))
                gm.wait(v)
                v.tensor_single_scalar(out=eqb[:], in_=ps_t3[:, :], scalar=0.0,
                                       op=ALU.is_equal)
                gv.inc(v.tensor_mul(out=selrow[0:8, t0:t0 + 128], in0=eqb[:],
                                    in1=geT[:]))
            gv.wait(t)
            if "dbg_sel" in ctx.io:
                gv.wait(sp)
                dbs = ctx.sem("mo_dbs")
                dbs.inc(sp.dma_start(out=ctx.io["dbg_sel"], in_=selrow[0:8, :]), 16)
                dbs.inc(sp.dma_start(out=ctx.io["dbg_lgf"], in_=lgf[0:8, :]), 16)
                dbs.wait(sp)
        for _e in (g, sp, t, v, a):
            gv.wait(_e)
            gm.wait(_e)
            ld.wait(_e)
        w1_io = ctx.io["moe_w1"]
        w2_io = ctx.io["moe_w2"]
        w1h = st.enter_context(nc.sbuf_tensor("mow1", [128, 8 * 2048], BF16))
        w2h = st.enter_context(nc.sbuf_tensor("mow2", [128, 16 * 1024], BF16))
        gel = st.enter_context(nc.sbuf_tensor("mogel", [128, 16 * 512], BF16))
        b1c = st.enter_context(nc.sbuf_tensor("mob1", [128, 32], F32))
        ps1 = st.enter_context(nc.psum_tensor("mop1", [128, 1024], F32))
        ps2 = st.enter_context(nc.psum_tensor("mop2", [128, 1024], F32))
        ps_bc = st.enter_context(nc.psum_tensor("mopbc", [128, 512], F32))
        ps_b2 = st.enter_context(nc.psum_tensor("mopb2", [128, 512], F32))
        for e in range(8):
            e1.wait(sp)
            ld.inc(sp.dma_start(out=b1c[:], in_=ctx.io["moe_b1"][e]), 16)
            for fh in range(2):
                fb = fh * 2048
                m1.wait(g)
                for k in range(DT):
                    wl.inc(g.dma_start(out=w1h[:, k * 2048:(k + 1) * 2048],
                                       in_=w1_io[e, k * 128:(k + 1) * 128, fb:fb + 2048]), 16)
                wl.wait(t)
                ld.wait(a)
                m2.wait(g)
                for f in range(16):
                    wl.inc(g.dma_start(out=w2h[:, f * 1024:(f + 1) * 1024],
                                       in_=w2_io[e, fb + f * 128:fb + f * 128 + 128, :]), 16)
                for c in range(4):
                    c0 = c * 512
                    for f in range(16):
                        pf = f % 2
                        for k in range(DT):
                            i1 = t.matmul(ps1[:, pf * 512:(pf + 1) * 512],
                                          w1h[:, k * 2048 + f * 128:k * 2048 + f * 128 + 128],
                                          xnb[:, k * T + c0:k * T + c0 + 512],
                                          start=(k == 0), stop=(k == DT - 1))
                        m1.inc(i1)
                        m1.wait(a)
                        e1.inc(a.activation(out=gel[:, f * 512:(f + 1) * 512],
                                            in_=ps1[:, pf * 512:(pf + 1) * 512],
                                            func=AF.Gelu,
                                            bias=b1c[:, fh * 16 + f:fh * 16 + f + 1]))
                        e1.wait(t, e1.v - 1)
                    if e == 0 and fh == 1 and c == 0 and "dbg_gel" in ctx.io:
                        e1.wait(sp)
                        dbg = ctx.sem("mo_dbg2")
                        dbg.inc(sp.dma_start(out=ctx.io["dbg_gel"], in_=gel[:, 0:8192]), 16)
                        dbg.inc(sp.dma_start(out=ctx.io["dbg_w1h"], in_=w1h[:, 0:2048]), 16)
                        dbg.inc(sp.dma_start(out=ctx.io["dbg_xnb"], in_=xnb[:, 0:512]), 16)
                        dbg.wait(sp)
                    if c == 0:
                        wl.wait(t)
                    e1.wait(t)
                    for kh in range(DT):
                        pk = kh % 2
                        for f in range(16):
                            i2 = t.matmul(ps2[:, pk * 512:(pk + 1) * 512],
                                          w2h[:, f * 1024 + kh * 128:f * 1024 + kh * 128 + 128],
                                          gel[:, f * 512:(f + 1) * 512],
                                          start=(f == 0), stop=(f == 15))
                        m2.inc(i2)
                        m2.wait(v)
                        dsta = acc[:, kh * T + c0:kh * T + c0 + 512]
                        if fh == 0:
                            e2.inc(v.tensor_copy(out=dsta,
                                                 in_=ps2[:, pk * 512:(pk + 1) * 512]))
                        else:
                            if kh == 0:
                                mg.inc(t.matmul(ps_bc[:, 0:512],
                                                sel8[0:8, e * 128:(e + 1) * 128],
                                                selrow[0:8, c0:c0 + 512],
                                                start=True, stop=True))
                                mg.wait(v)
                                e2.inc(v.tensor_copy(out=mask[:], in_=ps_bc[:, 0:512]))
                                e2.wait(t)
                            v.tensor_add(out=dsta, in0=dsta,
                                         in1=ps2[:, pk * 512:(pk + 1) * 512])
                            e2.inc(v.copy_predicated(out=OUT[:, kh * T + c0:kh * T + c0 + 512],
                                                     mask=mask[:], data=dsta))
                        e2.wait(t, e2.v - 1)
                if fh == 1 and e == 0 and "dbg_acc" in ctx.io:
                    e2.wait(sp)
                    dba = ctx.sem("mo_dba")
                    dba.inc(sp.dma_start(out=ctx.io["dbg_acc"], in_=acc[:, 0:T]), 16)
                    dba.wait(sp)
                if fh == 1 and e == 0 and "dbg_mask" in ctx.io:
                    e2.wait(sp)
                    dbm = ctx.sem("mo_dbm")
                    dbm.inc(sp.dma_start(out=ctx.io["dbg_mask"], in_=mask[:]), 16)
                    dbm.inc(sp.dma_start(out=ctx.io["dbg_out0"],
                                         in_=OUT[:, 0:T]), 16)
                    dbm.wait(sp)
        e2.wait(v)
        for c in range(4):
            c0 = c * 512
            for kh in range(DT):
                mg.inc(t.matmul(ps_b2[:, 0:512], b2sb[0:8, kh * 128:(kh + 1) * 128],
                                selrow[0:8, c0:c0 + 512], start=True, stop=True))
                mg.wait(v)
                sl = slice(kh * T + c0, kh * T + c0 + 512)
                e2.inc(v.tensor_add(out=OUT[:, sl], in0=OUT[:, sl], in1=ps_b2[:, 0:512]))
                e2.wait(t)
        e2.wait(a)
        for c in range(4):
            c0 = c * 512
            for kh in range(DT):
                sl = slice(kh * T + c0, kh * T + c0 + 512)
                so.inc(a.activation(out=acc[:, sl], in_=OUT[:, sl], func=AF.Silu))
        so.wait(g)
        for kh in range(DT):
            so.inc(g.dma_start(out=g_dram[kh, :, :], in_=acc[:, kh * T:(kh + 1) * T]), 16)
        for _e in (t, v, a, sp, g):
            so.wait(_e)
            e2.wait(_e)
            mg.wait(_e)
    return so


def emit_attn(ctx, layer):
    L = layer
    nc, t, v, a, sp, g = ctx.nc, ctx.t, ctx.v, ctx.a, ctx.sp, ctx.g
    xn_sh = ctx.io[f"xn_sh{L}"]
    has_bias = (L == 0)
    has_rope = (L == 2)
    pfx = "l0_" if L == 0 else "l2_"
    with contextlib.ExitStack() as st:
        sb = lambda n, s_, dt_: st.enter_context(nc.sbuf_tensor(f"at{L}{n}", s_, dt_))
        QT = sb("qt", [128, 4 * S], BF16)
        KT = sb("kt", [128, 4 * S], BF16)
        VB = sb("vb", [128, 32 * 512], BF16)
        OT = sb("ot", [128, 4 * S], BF16)
        nbnd = sb("nb", [128, 8], F32)
        ld = ctx.sem(f"at{L}_ld")
        wl = ctx.sem(f"at{L}_wl")
        mm = ctx.sem(f"at{L}_mm")
        ev = ctx.sem(f"at{L}_ev")
        sc = ctx.sem(f"at{L}_sc")
        ex = ctx.sem(f"at{L}_ex")
        pv = ctx.sem(f"at{L}_pv")
        on = ctx.sem(f"at{L}_on")
        om = ctx.sem(f"at{L}_om")
        so = ctx.sem(f"at{L}_so")
        with contextlib.ExitStack() as stA:
            sba = lambda n, s_, dt_: stA.enter_context(nc.sbuf_tensor(f"atA{L}{n}", s_, dt_))
            wq = sba("wq", [128, DT * 512], BF16)
            wk = sba("wk", [128, DT * 512], BF16)
            wv = sba("wv", [128, DT * 512], BF16)
            xnt = sba("xn", [128, DT * 512], BF16)
            ps_q = stA.enter_context(nc.psum_tensor(f"atA{L}q", [128, 1024], F32))
            ps_v = stA.enter_context(nc.psum_tensor(f"atA{L}v", [128, 1024], F32))
            ps_r = stA.enter_context(nc.psum_tensor(f"atA{L}r", [128, 512], F32))
            for nm, dst in (("wqT", wq), ("wkT", wk), ("wvT", wv)):
                wl.inc(g.dma_start(out=dst[:].rearrange("p (k m) -> p k m", k=DT),
                                   in_=ctx.io[pfx + nm].rearrange("(k p) m -> p k m", p=128)), 16)
            bq = bk = bvr = None
            if has_bias:
                bq = sba("bq", [128, 4], F32)
                bk = sba("bk", [128, 4], F32)
                bvr = sba("bv", [128, 512], F32)
                ld.inc(sp.dma_start(out=bq[:], in_=ctx.io["l0_bq"]), 16)
                ld.inc(sp.dma_start(out=bk[:], in_=ctx.io["l0_bk"]), 16)
                ld.inc(sp.dma_start(
                    out=bvr[:],
                    in_=ctx.io["l0_bv_row"].broadcast_to((128, 512))), 16)
            cosr = sinr = ropeP = None
            if has_rope:
                cosr = sba("cos", [128, S], BF16)
                sinr = sba("sin", [128, S], BF16)
                ropeP = sba("rp", [128, 128], BF16)
                rsc = sba("rs", [128, 1024], BF16)
                wl.inc(g.dma_start(out=cosr[:], in_=ctx.io["rope_cos"]), 16)
                wl.inc(g.dma_start(out=sinr[:], in_=ctx.io["rope_sin"]), 16)
                wl.inc(g.dma_start(out=ropeP[:], in_=ctx.io["ropeP"]), 16)
            emit_poll(ctx, sp, L * 8)
            for ch in range(8):
                slot, c0 = ch // 4, (ch % 4) * 512
                a0 = ch * 512
                ev.wait(sp)
                for k in range(DT):
                    ld.inc(sp.dma_start(out=xnt[:, k * 512:(k + 1) * 512],
                                        in_=xn_sh[slot, k, :, c0:c0 + 512]), 16)
                ld.wait(t)
                wl.wait(t)
                for which, wT, bcol, dstT in (("q", wq, bq, QT), ("k", wk, bk, KT)):
                    for hp in range(4):
                        reg = (hp % 2) * 512
                        for k in range(DT):
                            i1 = t.matmul(ps_q[:, reg:reg + 512],
                                          wT[:, k * 512 + hp * 128:k * 512 + hp * 128 + 128],
                                          xnt[:, k * 512:(k + 1) * 512],
                                          start=(k == 0), stop=(k == DT - 1))
                        mm.inc(i1)
                        dcol = slice(hp * S + a0, hp * S + a0 + 512)
                        if not has_rope:
                            mm.wait(v)
                            if has_bias:
                                ev.inc(v.tensor_scalar_add(out=dstT[:, dcol],
                                                           in0=ps_q[:, reg:reg + 512],
                                                           scalar1=bcol[:, hp:hp + 1]))
                            else:
                                ev.inc(v.tensor_copy(out=dstT[:, dcol],
                                                     in_=ps_q[:, reg:reg + 512]))
                            ev.wait(t)
                        else:
                            mm.wait(v)
                            v.tensor_mul(out=rsc[:, 0:512], in0=ps_q[:, reg:reg + 512],
                                         in1=sinr[:, a0:a0 + 512])
                            ev.inc(v.tensor_mul(out=rsc[:, 512:1024],
                                                in0=ps_q[:, reg:reg + 512],
                                                in1=cosr[:, a0:a0 + 512]))
                            ev.wait(t)
                            mm.inc(t.matmul(ps_r[:, 0:512], ropeP[:], rsc[:, 0:512],
                                            start=True, stop=True))
                            mm.wait(v)
                            ev.inc(v.tensor_add(out=dstT[:, dcol], in0=rsc[:, 512:1024],
                                                in1=ps_r[:, 0:512]))
                            ev.wait(t)
                for tt in range(4):
                    for half2 in range(2):
                        reg = half2 * 256
                        for k in range(DT):
                            i2 = t.matmul(ps_v[:, reg:reg + 256],
                                          xnt[:, k * 512 + tt * 128:k * 512 + tt * 128 + 128],
                                          wv[:, k * 512 + half2 * 256:k * 512 + half2 * 256 + 256],
                                          start=(k == 0), stop=(k == DT - 1))
                        mm.inc(i2)
                    mm.wait(v)
                    dst = VB[:, (ch * 4 + tt) * 512:(ch * 4 + tt) * 512 + 512]
                    if has_bias:
                        v.tensor_add(out=dst[:, 0:256], in0=ps_v[:, 0:256],
                                     in1=bvr[:, 0:256])
                        ev.inc(v.tensor_add(out=dst[:, 256:512], in0=ps_v[:, 256:512],
                                            in1=bvr[:, 256:512]))
                    else:
                        v.tensor_copy(out=dst[:, 0:256], in_=ps_v[:, 0:256])
                        ev.inc(v.tensor_copy(out=dst[:, 256:512], in_=ps_v[:, 256:512]))
                    ev.wait(t)
            for _e in (t, v, a, sp, g):
                ev.wait(_e)
                mm.wait(_e)
                ld.wait(_e)
                wl.wait(_e)
        with contextlib.ExitStack() as stB:
            sbb = lambda n, s_, dt_: stB.enter_context(nc.sbuf_tensor(f"atB{L}{n}", s_, dt_))
            hsel = sbb("hs", [128, 2], BF16)
            sqv = sbb("sq", [128, 512], BF16)
            nrow = sbb("nr", [1, 4 * 512], F32)
            mrow = sbb("mr", [1, 32], F32)
            ps_n = stB.enter_context(nc.psum_tensor(f"atB{L}n", [1, 512], F32))
            ps_bc = stB.enter_context(nc.psum_tensor(f"atB{L}b", [128, 8], F32))
            wl.inc(g.dma_start(out=hsel[:], in_=ctx.io["half_sel"]), 16)
            wl.wait(t)
            bs = ctx.sem(f"at{L}_bs")
            for hp in range(4):
                for ab in range(2):
                    h8 = hp * 2 + ab
                    for which, srcT in (("q", QT), ("k", KT)):
                        for cc in range(8):
                            colr = slice(hp * S + cc * 512, hp * S + cc * 512 + 512)
                            mm.wait(a)
                            ex.inc(a.activation(out=sqv[:], in_=srcT[:, colr],
                                                func=AF.Square))
                            ex.wait(t)
                            bs.wait(t)
                            i3 = t.matmul(ps_n[0:1, 0:512], hsel[:, ab:ab + 1], sqv[:],
                                          start=True, stop=True)
                            mm.inc(i3)
                            mm.wait(v)
                            bs.inc(v.reduce_max(out=mrow[0:1, cc:cc + 1],
                                                in_=ps_n[0:1, 0:512],
                                                axis=mybir.AxisListType.X))
                        idx = 16 if which == "q" else 24
                        bs.inc(v.reduce_max(out=mrow[0:1, idx + h8:idx + h8 + 1],
                                            in_=mrow[0:1, 0:8], axis=mybir.AxisListType.X))
            bs.wait(v)
            bs.inc(v.tensor_mul(out=mrow[0:1, 0:8], in0=mrow[0:1, 16:24],
                                in1=mrow[0:1, 24:32]))
            bs.wait(a)
            a.activation(out=mrow[0:1, 8:16], in_=mrow[0:1, 0:8], func=AF.Sqrt)
            ex.inc(a.activation(out=mrow[0:1, 8:16], in_=mrow[0:1, 8:16],
                                func=AF.Copy, scale=-0.125))
            ex.wait(t)
            i4 = t.matmul(ps_bc[:, 0:8], ctx.consts["ones1"][:], mrow[0:1, 8:16],
                          start=True, stop=True)
            mm.inc(i4)
            mm.wait(v)
            on.inc(v.tensor_copy(out=nbnd[:], in_=ps_bc[:, 0:8]))
            for _e in (t, v, a, sp, g):
                on.wait(_e)
                bs.wait(_e)
                ex.wait(_e)
                mm.wait(_e)
        with contextlib.ExitStack() as stC:
            sbc = lambda n, s_, dt_: stC.enter_context(nc.sbuf_tensor(f"atC{L}{n}", s_, dt_))
            cmk = sbc("cm", [128, 4 * 512], BF16)
            expa = sbc("ea", [128, 512], BF16)
            expb = sbc("eb", [128, 512], BF16)
            scr = sbc("sc", [128, 512], F32)
            reca = sbc("ra", [1, 512], F32)
            recb = sbc("rb", [1, 512], F32)
            onrm = sbc("on", [128, 512], BF16)
            ps_sa = stC.enter_context(nc.psum_tensor(f"atC{L}a", [128, 512], F32))
            ps_sb2 = stC.enter_context(nc.psum_tensor(f"atC{L}b", [128, 512], F32))
            ps_o = stC.enter_context(nc.psum_tensor(f"atC{L}o", [128, 512], F32))
            ps_m = stC.enter_context(nc.psum_tensor(f"atC{L}m", [1, 1024], F32))
            ps_rep = stC.enter_context(nc.psum_tensor(f"atC{L}r", [128, 512], F32))
            wl.inc(g.dma_start(out=cmk[:], in_=ctx.io["cmask"]), 16)
            wl.wait(t)
            wl.wait(v)
            if "dbg_nb" in ctx.io:
                onb = ctx.sem(f"at{L}_onb")
                on.wait(sp)
                onb.inc(sp.dma_start(out=ctx.io["dbg_nb"], in_=nbnd[:]), 16)
                onb.wait(sp)
            for hp in range(4):
                base_a, base_b = 0, 64
                for qt in range(8):
                    q0 = qt * 512
                    nkt = (q0 + 512) // 128
                    for kt in range(nkt):
                        k0 = kt * 128
                        diag = k0 >= q0
                        ex.wait(t)
                        i5 = t.matmul(ps_sa[:, 0:512],
                                      KT[0:64, hp * S + k0:hp * S + k0 + 128],
                                      QT[0:64, hp * S + q0:hp * S + q0 + 512],
                                      start=True, stop=True)
                        sc.inc(i5)
                        i6 = t.matmul(ps_sb2[:, 0:512],
                                      KT[64:128, hp * S + k0:hp * S + k0 + 128],
                                      QT[64:128, hp * S + q0:hp * S + q0 + 512],
                                      start=True, stop=True)
                        sc.inc(i6)
                        sc.wait(a)
                        pv.wait(a)
                        ex.inc(a.activation(out=expa[:], in_=ps_sa[:, 0:512], func=AF.Exp,
                                            scale=0.125, bias=nbnd[:, hp * 2:hp * 2 + 1]))
                        ex.inc(a.activation(out=expb[:], in_=ps_sb2[:, 0:512], func=AF.Exp,
                                            scale=0.125, bias=nbnd[:, hp * 2 + 1:hp * 2 + 2]))
                        if diag:
                            j = (k0 - q0) // 128
                            ex.wait(v)
                            pv.wait(v)
                            v.tensor_mul(out=expa[:], in0=expa[:],
                                         in1=cmk[:, j * 512:(j + 1) * 512])
                            ex.inc(v.tensor_mul(out=expb[:], in0=expb[:],
                                                in1=cmk[:, j * 512:(j + 1) * 512]))
                        if "dbg_exp" in ctx.io and hp == 0 and qt == 0:
                            dxe = ctx.sem(f"at{L}_dxe")
                            ex.wait(sp)
                            dxe.inc(sp.dma_start(
                                out=ctx.io["dbg_exp"][kt], in_=expa[:]), 16)
                            dxe.wait(sp)
                        ex.wait(t)
                        i7 = t.matmul(ps_o[0:64, 0:512],
                                      VB[:, kt * 512 + hp * 128:kt * 512 + hp * 128 + 64],
                                      expa[:], start=(kt == 0), stop=(kt == nkt - 1))
                        i8 = t.matmul(ps_o[64:128, 0:512],
                                      VB[:, kt * 512 + hp * 128 + 64:kt * 512 + hp * 128 + 128],
                                      expb[:], start=(kt == 0), stop=(kt == nkt - 1),
                                      tile_position=(0, 64))
                        i9 = t.matmul(ps_m[0:1, 0:512], ctx.consts["ones128"][:], expa[:],
                                      start=(kt == 0), stop=(kt == nkt - 1))
                        i10 = t.matmul(ps_m[0:1, 512:1024], ctx.consts["ones128"][:], expb[:],
                                       start=(kt == 0), stop=(kt == nkt - 1))
                        pv.inc(i7)
                        pv.inc(i8)
                        pv.inc(i9)
                        pv.inc(i10)
                    if "dbg_pm" in ctx.io and hp == 0 and qt == 0:
                        pv.wait(v)
                        dpm = ctx.sem(f"at{L}_dpm")
                        dpm.inc(v.tensor_copy(out=scr[:, 0:512], in_=ps_o[:, 0:512]))
                        dpm.wait(sp)
                        dpm.inc(sp.dma_start(out=ctx.io["dbg_po"], in_=scr[:, 0:512]), 16)
                        pv.wait(a)
                        dpm.inc(a.activation(out=scr[0:1, 0:512], in_=ps_m[0:1, 0:512],
                                             func=AF.Copy))
                        dpm.wait(sp)
                        dpm.inc(sp.dma_start(out=ctx.io["dbg_pm"], in_=scr[0:1, 0:512]), 16)
                        dpm.wait(sp)
                        dpm.wait(t)
                        dpm.wait(a)
                    pv.wait(a)
                    a.activation(out=reca[:], in_=ps_m[0:1, 0:512], func=AF.Ln)
                    a.activation(out=recb[:], in_=ps_m[0:1, 512:1024], func=AF.Ln)
                    on.inc(a.activation(out=reca[:], in_=reca[:], func=AF.Exp, scale=-1.0))
                    on.inc(a.activation(out=recb[:], in_=recb[:], func=AF.Exp, scale=-1.0))
                    on.wait(t)
                    i11 = t.matmul(ps_rep[0:64, 0:512], ctx.consts["ones1"][0:1, 0:64],
                                   reca[:], start=True, stop=True)
                    i12 = t.matmul(ps_rep[64:128, 0:512], ctx.consts["ones1"][0:1, 0:64],
                                   recb[:], start=True, stop=True, tile_position=(0, 64))
                    mm.inc(i11)
                    mm.inc(i12)
                    mm.wait(v)
                    pv.wait(v)
                    v.tensor_copy(out=onrm[:], in_=ps_o[:, 0:512])
                    on.inc(v.tensor_mul(out=OT[:, hp * S + q0:hp * S + q0 + 512],
                                        in0=onrm[:], in1=ps_rep[:, 0:512]))
                    on.wait(t)
                    on.wait(a)
            for _e in (t, v, a, sp, g):
                on.wait(_e)
                pv.wait(_e)
                sc.wait(_e)
                ex.wait(_e)
            if "dbg_ot" in ctx.io:
                dot = ctx.sem(f"at{L}_dot")
                dot.inc(sp.dma_start(out=ctx.io["dbg_ot"], in_=OT[:, 0:S]), 16)
                dot.wait(sp)
        with contextlib.ExitStack() as stD:
            sbd = lambda n, s_, dt_: stD.enter_context(nc.sbuf_tensor(f"atD{L}{n}", s_, dt_))
            wo = sbd("wo", [128, 4 * 1024], BF16)
            oev = sbd("oe", [128, 512], F32)
            ps_p = stD.enter_context(nc.psum_tensor(f"atD{L}p", [128, 1024], F32))
            wl.inc(g.dma_start(out=wo[:].rearrange("p (k m) -> p k m", k=4),
                               in_=ctx.io[pfx + "woT"].rearrange("(k p) m -> p k m", p=128)), 16)
            wl.wait(t)
            s_own = ctx.io[f"s_own{L}"]
            s_sh = ctx.io[f"s_sh{L}"]
            for ch in range(8):
                a0 = ch * 512
                for kd in range(DT):
                    reg = (kd % 2) * 512
                    for hp in range(4):
                        i13 = t.matmul(ps_p[:, reg:reg + 512],
                                       wo[:, hp * 1024 + kd * 128:hp * 1024 + kd * 128 + 128],
                                       OT[:, hp * S + a0:hp * S + a0 + 512],
                                       start=(hp == 0), stop=(hp == 3))
                    om.inc(i13)
                    om.wait(v)
                    so.wait(v)
                    ev.inc(v.tensor_copy(out=oev[:], in_=ps_p[:, reg:reg + 512]))
                    ev.wait(sp)
                    par = ctx.par_reg(sp)
                    with sp.If_cmp(par, 0, "IS_EQ"):
                        dst = (s_own[kd, :, a0:a0 + 512] if ch < 4
                               else s_sh[0, kd, :, a0 - T:a0 - T + 512])
                        so.inc(sp.dma_start(out=dst, in_=oev[:]), 16)
                    with sp.Else():
                        dst = (s_sh[1, kd, :, a0:a0 + 512] if ch < 4
                               else s_own[kd, :, a0 - T:a0 - T + 512])
                        sp.dma_start(out=dst, in_=oev[:]).then_inc(so.h, 16)
                    om.wait(t)
                    ev.wait(t)
            for _e in (t, v, a, sp, g):
                so.wait(_e)
                om.wait(_e)
                ev.wait(_e)
            emit_flag_write(ctx, g, L * 8 + 1)
    return so


def emit_l1_mixer(ctx):
    nc, t, v, a, sp, g = ctx.nc, ctx.t, ctx.v, ctx.a, ctx.sp, ctx.g
    xn_sh = ctx.io["xn_sh1"]
    s_own = ctx.io["s_own1"]
    dbc_sh = ctx.io["l1_dbc_sh"]
    with contextlib.ExitStack() as st:
        sb = lambda n, s_, dt_: st.enter_context(nc.sbuf_tensor(f"m1{n}", s_, dt_))
        ld = ctx.sem("m1_ld")
        wl = ctx.sem("m1_wl")
        mm = ctx.sem("m1_mm")
        ev = ctx.sem("m1_ev")
        cv = ctx.sem("m1_cv")
        pj = ctx.sem("m1_pj")
        pe = ctx.sem("m1_pe")
        sx = ctx.sem("m1_sx")
        so = ctx.sem("m1_so")
        with contextlib.ExitStack() as s1:
            sb1 = lambda n, s_, dt_: s1.enter_context(nc.sbuf_tensor(f"m1a{n}", s_, dt_))
            wg = sb1("wg", [128, DT * 1024], BF16)
            wi = sb1("wi", [128, DT * 1024], BF16)
            wd = sb1("wd", [128, DT * 16], BF16)
            wb = sb1("wb", [128, DT * 16], BF16)
            wc = sb1("wc", [128, DT * 16], BF16)
            wD = sb1("wD", [128, DT], BF16)
            cw = sb1("cw", [128, 32], F32)
            cb8 = sb1("cb8", [128, 8], F32)
            xnt = sb1("xn", [128, DT * 512], BF16)
            sg = sb1("sg", [128, 512], BF16)
            xg = sb1("xg", [128, DT * 515], BF16)
            xc = sb1("xc", [128, DT * 512], BF16)
            dbuf = sb1("db", [16, S], F32)
            bbuf = sb1("bb", [16, S], F32)
            cbuf = sb1("cb", [16, S], F32)
            Dbuf = sb1("Db", [1, S], F32)
            ps_g = s1.enter_context(nc.psum_tensor("m1pg", [128, 512], F32))
            ps_i = s1.enter_context(nc.psum_tensor("m1pi", [128, 512], F32))
            ps_d = s1.enter_context(nc.psum_tensor("m1pd", [16, 512], F32))
            ps_b = s1.enter_context(nc.psum_tensor("m1pb", [16, 512], F32))
            ps_c = s1.enter_context(nc.psum_tensor("m1pc", [16, 512], F32))
            ps_D = s1.enter_context(nc.psum_tensor("m1pD", [1, 512], F32))
            wl.inc(g.dma_start(out=wg[:].rearrange("p (k m) -> p k m", k=DT),
                               in_=ctx.io["l1_wgT"].rearrange("(k p) m -> p k m", p=128)), 16)
            wl.inc(g.dma_start(out=wi[:].rearrange("p (k m) -> p k m", k=DT),
                               in_=ctx.io["l1_wiT"].rearrange("(k p) m -> p k m", p=128)), 16)
            wl.inc(g.dma_start(out=wd[:].rearrange("p (k m) -> p k m", k=DT),
                               in_=ctx.io["l1_wd"].rearrange("(k p) m -> p k m", p=128)), 16)
            wl.inc(g.dma_start(out=wb[:].rearrange("p (k m) -> p k m", k=DT),
                               in_=ctx.io["l1_wb"].rearrange("(k p) m -> p k m", p=128)), 16)
            wl.inc(g.dma_start(out=wc[:].rearrange("p (k m) -> p k m", k=DT),
                               in_=ctx.io["l1_wc"].rearrange("(k p) m -> p k m", p=128)), 16)
            wl.inc(g.dma_start(out=wD[:], in_=ctx.io["l1_D"]), 16)
            ld.inc(sp.dma_start(out=cw[:], in_=ctx.io["l1_cw"]), 16)
            ld.inc(sp.dma_start(out=cb8[:], in_=ctx.io["l1_cb"]), 16)
            ld.wait(v)
            for ct in range(DT):
                cv.inc(v.memset(xg[:, ct * 515:ct * 515 + 3], 0.0))
            emit_poll(ctx, sp, 1 * 8)
            for ch in range(8):
                slot, c0 = ch // 4, (ch % 4) * 512
                load_xn_chunk(ctx, sp, ld, xn_sh, xnt, c0,
                              war_sem=mm, slot=slot)
                ld.wait(t)
                wl.wait(t)
                for ct in range(DT):
                    for k in range(DT):
                        i1 = t.matmul(ps_g[:, 0:512],
                                      wg[:, k * 1024 + ct * 128:k * 1024 + ct * 128 + 128],
                                      xnt[:, k * 512:(k + 1) * 512],
                                      start=(k == 0), stop=(k == DT - 1))
                        i2 = t.matmul(ps_i[:, 0:512],
                                      wi[:, k * 1024 + ct * 128:k * 1024 + ct * 128 + 128],
                                      xnt[:, k * 512:(k + 1) * 512],
                                      start=(k == 0), stop=(k == DT - 1))
                    mm.inc(i1)
                    mm.inc(i2)
                    mm.wait(a)
                    cv.wait(a)
                    ev.inc(a.activation(out=sg[:], in_=ps_g[:, 0:512], func=AF.Silu))
                    ev.wait(v)
                    cv.wait(v)
                    cv.inc(v.tensor_mul(out=xg[:, ct * 515 + 3:ct * 515 + 515],
                                        in0=sg[:], in1=ps_i[:, 0:512]))
                    cv.wait(t)
                    ev.wait(t)
                    xbase = ct * 515
                    v.tensor_scalar_mul(out=xc[:, ct * 512:(ct + 1) * 512],
                                        in0=xg[:, xbase:xbase + 512],
                                        scalar1=cw[:, ct * 4:ct * 4 + 1])
                    for j in (1, 2, 3):
                        v.scalar_tensor_tensor(out=xc[:, ct * 512:(ct + 1) * 512],
                                               in0=xg[:, xbase + j:xbase + j + 512],
                                               scalar=cw[:, ct * 4 + j:ct * 4 + j + 1],
                                               in1=xc[:, ct * 512:(ct + 1) * 512],
                                               op0=ALU.mult, op1=ALU.add)
                    cv.inc(v.tensor_scalar_add(out=xc[:, ct * 512:(ct + 1) * 512],
                                               in0=xc[:, ct * 512:(ct + 1) * 512],
                                               scalar1=cb8[:, ct:ct + 1]))
                    cv.inc(v.tensor_copy(out=xg[:, xbase:xbase + 3],
                                         in_=xg[:, xbase + 512:xbase + 515]))
                cv.wait(t)
                for ct in range(DT):
                    i3 = t.matmul(ps_d[:, 0:512], wd[:, ct * 16:(ct + 1) * 16],
                                  xc[:, ct * 512:(ct + 1) * 512],
                                  start=(ct == 0), stop=(ct == DT - 1))
                    i4 = t.matmul(ps_b[:, 0:512], wb[:, ct * 16:(ct + 1) * 16],
                                  xc[:, ct * 512:(ct + 1) * 512],
                                  start=(ct == 0), stop=(ct == DT - 1))
                    i5 = t.matmul(ps_c[:, 0:512], wc[:, ct * 16:(ct + 1) * 16],
                                  xc[:, ct * 512:(ct + 1) * 512],
                                  start=(ct == 0), stop=(ct == DT - 1))
                    i6 = t.matmul(ps_D[:, 0:512], wD[:, ct:ct + 1],
                                  xc[:, ct * 512:(ct + 1) * 512],
                                  start=(ct == 0), stop=(ct == DT - 1))
                for i_ in (i3, i4, i5, i6):
                    pj.inc(i_)
                pj.wait(v)
                a0 = ch * 512
                pe.inc(v.tensor_copy(out=dbuf[0:16, a0:a0 + 512], in_=ps_d[:, 0:512]))
                pe.inc(v.tensor_copy(out=bbuf[0:16, a0:a0 + 512], in_=ps_b[:, 0:512]))
                pe.inc(v.tensor_copy(out=cbuf[0:16, a0:a0 + 512], in_=ps_c[:, 0:512]))
                pe.inc(v.tensor_copy(out=Dbuf[0:1, a0:a0 + 512], in_=ps_D[:, 0:512]))
                pe.wait(t)
            if "dbg1_xc" in ctx.io:
                cv.wait(sp)
                pe.wait(sp)
                dx = ctx.sem("m1_dx")
                dx.inc(sp.dma_start(out=ctx.io["dbg1_xc"], in_=xc[:]), 16)
                dx.inc(sp.dma_start(out=ctx.io["dbg1_d"], in_=dbuf[0:16, :]), 16)
                dx.wait(sp)
            pe.wait(sp)
            slot_dma(ctx, sp, sx, (dbc_sh[0, 0], dbc_sh[1, 0]), dbuf[0:16, :])
            slot_dma(ctx, sp, sx, (dbc_sh[0, 1], dbc_sh[1, 1]), bbuf[0:16, :])
            slot_dma(ctx, sp, sx, (dbc_sh[0, 2], dbc_sh[1, 2]), cbuf[0:16, :])
            slot_dma(ctx, sp, sx, (dbc_sh[0, 3, 0:1], dbc_sh[1, 3, 0:1]), Dbuf[0:1, :])
            sx.wait(sp)
            emit_flag_write(ctx, sp, 1 * 8 + 2)
            emit_poll(ctx, sp, 1 * 8 + 2)
            pd = sb1("pd", [16, S], F32)
            if not ctx.fused:
                sx.wait(v)
                sx.inc(v.memset(pd[0:16, :], 0.0))
                sx.wait(sp)
                other_slot_dma(ctx, sp, sx, None, None) if False else None
                for idx in range(3):
                    par = ctx.par_reg(sp)
                    with sp.If_cmp(par, 0, "IS_EQ"):
                        sx.inc(sp.dma_start(out=dbc_sh[1, idx], in_=pd[0:16, :]), 16)
                    with sp.Else():
                        sp.dma_start(out=dbc_sh[0, idx], in_=pd[0:16, :]).then_inc(sx.h, 16)
                par = ctx.par_reg(sp)
                with sp.If_cmp(par, 0, "IS_EQ"):
                    sx.inc(sp.dma_start(out=dbc_sh[1, 3, 0:1], in_=pd[0:1, :]), 16)
                with sp.Else():
                    sp.dma_start(out=dbc_sh[0, 3, 0:1], in_=pd[0:1, :]).then_inc(sx.h, 16)
                sx.wait(sp)
            bdcol = sb1("bd", [16, 1], F32)
            ld.inc(sp.dma_start(out=bdcol[:], in_=ctx.io["l1_bd"]), 16)
            for idx, buf in ((0, dbuf), (1, bbuf), (2, cbuf)):
                other_slot_dma(ctx, sp, sx, pd[0:16, :],
                               (dbc_sh[0, idx], dbc_sh[1, idx]))
                sx.wait(v)
                ld.wait(v)
                instr = v.tensor_add(out=buf[0:16, :], in0=buf[0:16, :], in1=pd[0:16, :])
                sx.inc(instr)
                sx.wait(sp)
            other_slot_dma(ctx, sp, sx, pd[0:1, :],
                           (dbc_sh[0, 3, 0:1], dbc_sh[1, 3, 0:1]))
            sx.wait(v)
            sx.inc(v.tensor_add(out=Dbuf[0:1, :], in0=Dbuf[0:1, :], in1=pd[0:1, :]))
            sx.inc(v.tensor_scalar_add(out=dbuf[0:16, :], in0=dbuf[0:16, :],
                                       scalar1=bdcol[0:16, 0:1]))
            la = sb1("la", [16, 1], F32)
            ld.inc(sp.dma_start(out=la[:], in_=ctx.io["l1_logA"]), 16)
            a_t = sb1("at", [16, S], BF16)
            Dbf = sb1("Dbf", [1, S], BF16)
            b_t = sb1("bt", [16, S], BF16)
            h_t = sb1("ht", [16, S], BF16)
            y_t = sb1("yt", [1, S], F32)
            ps_y = s1.enter_context(nc.psum_tensor("m1py", [1, 512], F32))
            sx.wait(a)
            ld.wait(a)
            sx.inc(a.activation(out=a_t[0:16, :], in_=dbuf[0:16, :], func=AF.Exp,
                                scale=la[0:16, 0:1]))
            sx.wait(v)
            sx.inc(v.tensor_mul(out=b_t[0:16, :], in0=dbuf[0:16, :], in1=bbuf[0:16, :]))
            sx.inc(v.tensor_tensor_scan(out=h_t[0:16, :], data0=a_t[0:16, :],
                                        data1=b_t[0:16, :], initial=0.0,
                                        op0=ALU.mult, op1=ALU.add))
            sx.inc(v.tensor_mul(out=h_t[0:16, :], in0=h_t[0:16, :], in1=cbuf[0:16, :]))
            sx.inc(v.tensor_copy(out=Dbf[0:1, :], in_=Dbuf[0:1, :]))
            sx.wait(t)
            for ch in range(8):
                a0 = ch * 512
                i7 = t.matmul(ps_y[0:1, 0:512], ctx.consts["ones128"][0:16, 0:1],
                              h_t[0:16, a0:a0 + 512], start=True, stop=False)
                i8 = t.matmul(ps_y[0:1, 0:512], ctx.consts["ones1b"][0:1, 0:1],
                              Dbf[0:1, a0:a0 + 512],
                              start=False, stop=True)
                mm.inc(i8)
                mm.wait(v)
                pe.inc(v.tensor_copy(out=y_t[0:1, a0:a0 + 512], in_=ps_y[0:1, 0:512]))
                pe.wait(t)
            if "dbg1_y" in ctx.io:
                pe.wait(sp)
                sx.wait(sp)
                dy = ctx.sem("m1_dy")
                dy.inc(sp.dma_start(out=ctx.io["dbg1_y"], in_=y_t[0:1, :]), 16)
                dy.inc(sp.dma_start(out=ctx.io["dbg1_h"], in_=h_t[0:16, :]), 16)
                dy.wait(sp)
            woc = sb1("wo", [128, 8], F32)
            sev = sb1("se", [128, 512], F32)
            ld.inc(sp.dma_start(out=woc[:], in_=ctx.io["l1_wout"]), 16)
            ps_r = s1.enter_context(nc.psum_tensor("m1pr", [128, 512], F32))
            ld.wait(v)
            pe.wait(t)
            for c in range(4):
                for kd in range(DT):
                    par = ctx.par_reg(t)
                    with t.If_cmp(par, 0, "IS_EQ"):
                        i9 = t.matmul(ps_r[:, 0:512], ctx.consts["ones1"][:],
                                      y_t[0:1, c * 512:c * 512 + 512],
                                      start=True, stop=True)
                        mm.inc(i9)
                    with t.Else():
                        t.matmul(ps_r[:, 0:512], ctx.consts["ones1"][:],
                                 y_t[0:1, T + c * 512:T + c * 512 + 512],
                                 start=True, stop=True).then_inc(mm.h, 1)
                    mm.wait(v)
                    so.wait(v)
                    ev.inc(v.tensor_scalar_mul(out=sev[:], in0=ps_r[:, 0:512],
                                               scalar1=woc[:, kd:kd + 1]))
                    ev.wait(sp)
                    so.inc(sp.dma_start(out=s_own[kd, :, c * 512:(c + 1) * 512],
                                        in_=sev[:]), 16)
                    ev.wait(t)
            for _e in (t, v, a, sp, g):
                so.wait(_e)
                mm.wait(_e)
                ev.wait(_e)
                sx.wait(_e)
                pe.wait(_e)
                cv.wait(_e)
    return so


def emit_l3_mixer(ctx):
    nc, t, v, a, sp, g = ctx.nc, ctx.t, ctx.v, ctx.a, ctx.sp, ctx.g
    xn_sh = ctx.io["xn_sh3"]
    s_own = ctx.io["s_own3"]
    s_sh = ctx.io["s_sh3"]
    TC = 512
    l3tot = ctx.nc.dram_tensor("l3_tot", [8, 96, TC], F32).ap()
    with contextlib.ExitStack() as st:
        sb = lambda n, s_, dt_: st.enter_context(nc.sbuf_tensor(f"m3{n}", s_, dt_))
        ld = ctx.sem("m3_ld")
        wl = ctx.sem("m3_wl")
        mm = ctx.sem("m3_mm")
        ev = ctx.sem("m3_ev")
        rw = ctx.sem("m3_rw")
        pj = ctx.sem("m3_pj")
        pe = ctx.sem("m3_pe")
        sx = ctx.sem("m3_sx")
        sy = ctx.sem("m3_sy")
        so = ctx.sem("m3_so")
        win = sb("win", [128, DT * 1024], BF16)
        wsk = sb("wsk", [128, DT * 1024], BF16)
        wou = sb("wou", [128, DT * 1024], BF16)
        wd1 = sb("wd1", [128, DT * 64], BF16)
        wbc = sb("wbc", [128, DT * 32], BF16)
        wd2 = sb("wd2", [64, 1024], BF16)
        nwc = sb("nwc", [128, 8], F32)
        cw = sb("cw", [128, 32], F32)
        cb8 = sb("cb8", [128, 8], F32)
        d2b = sb("d2b", [128, 8], F32)
        wdc = sb("wdc", [128, 8], F32)
        alog = sb("alog", [128, 128], F32)
        A_sb = sb("A", [128, 128], F32)
        hcar = sb("hc", [128, 128], F32)
        wl.inc(g.dma_start(out=win[:].rearrange("p (k m) -> p k m", k=DT),
                           in_=ctx.io["l3_inT"].rearrange("(k p) m -> p k m", p=128)), 16)
        wl.inc(g.dma_start(out=wsk[:].rearrange("p (k m) -> p k m", k=DT),
                           in_=ctx.io["l3_skipT"].rearrange("(k p) m -> p k m", p=128)), 16)
        wl.inc(g.dma_start(out=wou[:].rearrange("p (k m) -> p k m", k=DT),
                           in_=ctx.io["l3_outT"].rearrange("(k p) m -> p k m", p=128)), 16)
        wl.inc(g.dma_start(out=wd1[:].rearrange("p (k m) -> p k m", k=DT),
                           in_=ctx.io["l3_wd1T"].rearrange("(k p) m -> p k m", p=128)), 16)
        wl.inc(g.dma_start(out=wbc[:].rearrange("p (k m) -> p k m", k=DT)[:, :, 0:16],
                           in_=ctx.io["l3_wbT"].rearrange("(k p) m -> p k m", p=128)), 16)
        wl.inc(g.dma_start(out=wbc[:].rearrange("p (k m) -> p k m", k=DT)[:, :, 16:32],
                           in_=ctx.io["l3_wcT"].rearrange("(k p) m -> p k m", p=128)), 16)
        wl.inc(g.dma_start(out=wd2[:], in_=ctx.io["l3_wd2T"]), 16)
        ld.inc(sp.dma_start(out=nwc[:], in_=ctx.io["l3_normw"]), 16)
        ld.inc(sp.dma_start(out=cw[:], in_=ctx.io["l3_cw"]), 16)
        ld.inc(sp.dma_start(out=cb8[:], in_=ctx.io["l3_cb"]), 16)
        ld.inc(sp.dma_start(out=d2b[:], in_=ctx.io["l3_wd2b"]), 16)
        ld.inc(sp.dma_start(out=wdc[:], in_=ctx.io["l3_WD"]), 16)
        ld.inc(sp.dma_start(out=alog[:], in_=ctx.io["l3_alog"]), 16)
        ld.wait(a)
        rw.inc(a.activation(out=A_sb[:], in_=alog[:], func=AF.Exp))
        rw.wait(v)
        rw.inc(v.tensor_scalar_mul(out=A_sb[:], in0=A_sb[:], scalar1=-1.0))
        rw.inc(v.memset(hcar[:], 0.0))
        xnt = sb("xnt", [128, DT * 512], BF16)
        rows = sb("rws", [1, 512 * 3], F32)
        xr = sb("xr", [128, DT * 512], BF16)
        skips = sb("skp", [128, DT * TC], BF16)
        xin = sb("xin", [128, DT * (TC + 3)], BF16)
        xc = sb("xc", [128, DT * TC], BF16)
        d1f = sb("d1f", [64, TC], BF16)
        dl = sb("dl", [128, DT * TC], F32)
        zs = sb("zs", [128, 512], F32)
        u1 = sb("u1", [128, 512], F32)
        ux = sb("ux", [128, DT * TC], BF16)
        at_ = sb("at", [128, TC], F32)
        bt_ = sb("bt", [128, TC], F32)
        ht_ = sb("ht", [128, TC], F32)
        Brep = sb("Br", [128, TC], BF16)
        Crep = sb("Cr", [128, TC], BF16)
        yb = sb("yb", [128, DT * TC], BF16)
        oev = sb("oe", [128, 512], F32)
        d1b = sb("d1b", [64, TC], F32)
        Bb = sb("Bb", [16, TC], F32)
        Cb = sb("Cb", [16, TC], F32)
        prt = sb("prt", [64, TC], F32)
        rw.wait(v)
        for et in range(DT):
            rw.inc(v.memset(xin[:, et * (TC + 3):et * (TC + 3) + 3], 0.0))
        emit_poll(ctx, sp, 3 * 8)
        for sc in range(8):
            dbc0 = ctx.io[f"l3_dbc_sh{sc // 2}"]
            co = (sc % 2) * 512
            dbc_sh = dbc0[:, :, co:co + 512]
            with contextlib.ExitStack() as s1:
                ps_s = s1.enter_context(nc.psum_tensor(f"m3s{sc}", [1, 512], F32))
                ps_bc = s1.enter_context(nc.psum_tensor(f"m3b{sc}", [128, 512], F32))
                ps_p1 = s1.enter_context(nc.psum_tensor(f"m3p1{sc}", [128, 512], F32))
                ps_p2 = s1.enter_context(nc.psum_tensor(f"m3p2{sc}", [128, 512], F32))
                ps_d1 = s1.enter_context(nc.psum_tensor(f"m3d{sc}", [64, 512], F32))
                ps_B = s1.enter_context(nc.psum_tensor(f"m3B{sc}", [16, 512], F32))
                ps_C = s1.enter_context(nc.psum_tensor(f"m3C{sc}", [16, 512], F32))
                for tc in range(1):
                    ch = sc
                    slot, c0 = ch // 4, (ch % 4) * 512
                    load_xn_chunk(ctx, sp, ld, xn_sh, xnt, c0, war_sem=mm, slot=slot)
                    ld.wait(a)
                    rw.wait(a)
                    for k in range(DT):
                        ev.inc(a.activation(out=xr[:, k * 512:(k + 1) * 512],
                                            in_=xnt[:, k * 512:(k + 1) * 512],
                                            func=AF.Square))
                    ev.wait(t)
                    ld.wait(t)
                    rw.wait(t)
                    for k in range(DT):
                        i1 = t.matmul(ps_s[0:1, 0:512], ctx.consts["ones128"][:],
                                      xr[:, k * 512:(k + 1) * 512],
                                      start=(k == 0), stop=(k == DT - 1))
                    mm.inc(i1)
                    mm.wait(v)
                    rw.inc(v.tensor_scalar_mul(out=rows[0:1, 0:512],
                                               in0=ps_s[0:1, 0:512], scalar1=1.0 / D))
                    rw.wait(a)
                    a.activation(out=rows[0:1, 512:1024], in_=rows[0:1, 0:512],
                                 func=AF.Ln, bias=ctx.consts["cpool"][0:1, 0:1])
                    rw.inc(a.activation(out=rows[0:1, 1024:1536], in_=rows[0:1, 512:1024],
                                        func=AF.Exp, scale=ctx.consts["cpool"][0:1, 1:2]))
                    rw.wait(t)
                    i2 = t.matmul(ps_bc[:, 0:512], ctx.consts["ones1"][:],
                                  rows[0:1, 1024:1536], start=True, stop=True)
                    mm.inc(i2)
                    mm.wait(v)
                    for k in range(DT):
                        v.tensor_mul(out=xr[:, k * 512:(k + 1) * 512],
                                     in0=xnt[:, k * 512:(k + 1) * 512],
                                     in1=ps_bc[:, 0:512])
                        rw.inc(v.tensor_scalar_mul(out=xr[:, k * 512:(k + 1) * 512],
                                                   in0=xr[:, k * 512:(k + 1) * 512],
                                                   scalar1=nwc[:, k:k + 1]))
                    rw.wait(t)
                    wl.wait(t)
                    for et in range(DT):
                        for k in range(DT):
                            i3 = t.matmul(ps_p1[:, 0:512],
                                          wsk[:, k * 1024 + et * 128:k * 1024 + et * 128 + 128],
                                          xr[:, k * 512:(k + 1) * 512],
                                          start=(k == 0), stop=(k == DT - 1))
                            i4 = t.matmul(ps_p2[:, 0:512],
                                          win[:, k * 1024 + et * 128:k * 1024 + et * 128 + 128],
                                          xr[:, k * 512:(k + 1) * 512],
                                          start=(k == 0), stop=(k == DT - 1))
                        mm.inc(i3)
                        mm.inc(i4)
                        mm.wait(a)
                        pe.wait(a)
                        ev.inc(a.activation(
                            out=skips[:, et * TC + tc * 512:et * TC + tc * 512 + 512],
                            in_=ps_p1[:, 0:512], func=AF.Silu))
                        ev.wait(v)
                        pe.wait(v)
                        xb0 = et * (TC + 3) + 3 + tc * 512
                        pe.inc(v.tensor_copy(out=xin[:, xb0:xb0 + 512],
                                             in_=ps_p2[:, 0:512]))
                        pe.wait(t)
                        ev.wait(t)
                pe.wait(v)
                ld.wait(v)
                so.wait(v)
                pj.wait(v)
                for et in range(DT):
                    xb = et * (TC + 3)
                    v.tensor_scalar_mul(out=xc[:, et * TC:et * TC + TC],
                                        in0=xin[:, xb:xb + TC],
                                        scalar1=cw[:, et * 4:et * 4 + 1])
                    for j in (1, 2, 3):
                        v.scalar_tensor_tensor(out=xc[:, et * TC:et * TC + TC],
                                               in0=xin[:, xb + j:xb + j + TC],
                                               scalar=cw[:, et * 4 + j:et * 4 + j + 1],
                                               in1=xc[:, et * TC:et * TC + TC],
                                               op0=ALU.mult, op1=ALU.add)
                    v.tensor_scalar_add(out=xc[:, et * TC:et * TC + TC],
                                        in0=xc[:, et * TC:et * TC + TC],
                                        scalar1=cb8[:, et:et + 1])
                    pj.inc(v.tensor_copy(out=xin[:, xb:xb + 3],
                                         in_=xin[:, xb + TC:xb + TC + 3]))
                pj.wait(a)
                for et in range(DT):
                    pj.inc(a.activation(out=xc[:, et * TC:et * TC + TC],
                                        in_=xc[:, et * TC:et * TC + TC], func=AF.Silu))
                pj.wait(t)
                for tc in range(1):
                    for et in range(DT):
                        i5 = t.matmul(ps_d1[:, 0:512], wd1[:, et * 64:(et + 1) * 64],
                                      xc[:, et * TC + tc * 512:et * TC + tc * 512 + 512],
                                      start=(et == 0), stop=(et == DT - 1))
                        i6 = t.matmul(ps_B[:, 0:512], wbc[:, et * 32:et * 32 + 16],
                                      xc[:, et * TC + tc * 512:et * TC + tc * 512 + 512],
                                      start=(et == 0), stop=(et == DT - 1))
                        i7 = t.matmul(ps_C[:, 0:512], wbc[:, et * 32 + 16:et * 32 + 32],
                                      xc[:, et * TC + tc * 512:et * TC + tc * 512 + 512],
                                      start=(et == 0), stop=(et == DT - 1))
                    pj.inc(i5)
                    pj.inc(i6)
                    pj.inc(i7)
                    pj.wait(v)
                    sx.wait(v)
                    pe.inc(v.tensor_copy(out=d1b[0:64, tc * 512:tc * 512 + 512],
                                         in_=ps_d1[:, 0:512]))
                    pe.inc(v.tensor_copy(out=Bb[0:16, tc * 512:tc * 512 + 512],
                                         in_=ps_B[:, 0:512]))
                    pe.inc(v.tensor_copy(out=Cb[0:16, tc * 512:tc * 512 + 512],
                                         in_=ps_C[:, 0:512]))
                    pe.wait(t)
                pe.wait(sp)
                slot_dma(ctx, sp, sx, (dbc_sh[0, 0:64], dbc_sh[1, 0:64]), d1b[0:64, :])
                slot_dma(ctx, sp, sx, (dbc_sh[0, 64:80], dbc_sh[1, 64:80]), Bb[0:16, :])
                slot_dma(ctx, sp, sx, (dbc_sh[0, 80:96], dbc_sh[1, 80:96]), Cb[0:16, :])
                sx.wait(sp)
                emit_flag_write(ctx, sp, 3 * 8 + 2 + sc)
                emit_poll(ctx, sp, 3 * 8 + 2 + sc)
                if not ctx.fused:
                    sx.wait(v)
                    sx.inc(v.memset(prt[0:64, :], 0.0))
                    sx.wait(sp)
                    for rng_ in ((0, 64), (64, 80), (80, 96)):
                        par = ctx.par_reg(sp)
                        with sp.If_cmp(par, 0, "IS_EQ"):
                            sx.inc(sp.dma_start(out=dbc_sh[1, rng_[0]:rng_[1]],
                                                in_=prt[0:rng_[1] - rng_[0], :]), 16)
                        with sp.Else():
                            sp.dma_start(out=dbc_sh[0, rng_[0]:rng_[1]],
                                         in_=prt[0:rng_[1] - rng_[0], :]).then_inc(sx.h, 16)
                    sx.wait(sp)
                other_slot_dma(ctx, sp, sx, prt[0:64, :], (dbc_sh[0, 0:64], dbc_sh[1, 0:64]))
                sx.wait(v)
                sx.inc(v.tensor_add(out=d1b[0:64, :], in0=d1b[0:64, :], in1=prt[0:64, :]))
                sx.wait(sp)
                other_slot_dma(ctx, sp, sx, prt[0:16, :], (dbc_sh[0, 64:80], dbc_sh[1, 64:80]))
                sx.wait(v)
                sx.inc(v.tensor_add(out=Bb[0:16, :], in0=Bb[0:16, :], in1=prt[0:16, :]))
                sx.wait(sp)
                other_slot_dma(ctx, sp, sx, prt[0:16, :], (dbc_sh[0, 80:96], dbc_sh[1, 80:96]))
                sx.wait(v)
                sx.inc(v.tensor_add(out=Cb[0:16, :], in0=Cb[0:16, :], in1=prt[0:16, :]))
                sx.inc(v.tensor_copy(out=d1f[0:64, :], in_=d1b[0:64, :]))
                sx.wait(sp)
                so.wait(sp)
                sx.inc(sp.dma_start(out=l3tot[sc, 64:80], in_=Bb[0:16, :]), 16)
                sx.inc(sp.dma_start(out=l3tot[sc, 80:96], in_=Cb[0:16, :]), 16)
            with contextlib.ExitStack() as s2:
                ps_z = s2.enter_context(nc.psum_tensor(f"m3z{sc}", [128, 512], F32))
                ps_y = s2.enter_context(nc.psum_tensor(f"m3y{sc}", [128, 1024], F32))
                sx.wait(t)
                for et in range(DT):
                    for tc in range(1):
                        i8 = t.matmul(ps_z[:, 0:512], wd2[:, et * 128:(et + 1) * 128],
                                      d1f[0:64, tc * 512:tc * 512 + 512],
                                      start=True, stop=True)
                        pj.inc(i8)
                        pj.wait(v)
                        sy.wait(v)
                        sy.inc(v.tensor_scalar_add(out=zs[:], in0=ps_z[:, 0:512],
                                                   scalar1=d2b[:, et:et + 1]))
                        sy.wait(t)
                        sy.wait(a)
                        sy.inc(a.activation(out=u1[:], in_=zs[:], func=AF.Abs))
                        sy.wait(a)
                        a.activation(out=u1[:], in_=u1[:], func=AF.Exp, scale=-1.0)
                        sy.inc(a.activation(out=u1[:], in_=u1[:], func=AF.Ln,
                                            bias=ctx.consts["cpool"][:, 5:6]))
                        sy.wait(v)
                        dcol = et * TC + tc * 512
                        v.tensor_scalar_max(out=zs[:], in0=zs[:], scalar1=0.0)
                        sy.inc(v.tensor_add(out=dl[:, dcol:dcol + 512], in0=zs[:],
                                            in1=u1[:]))
                sy.wait(v)
                for et in range(DT):
                    sy.inc(v.tensor_mul(out=ux[:, et * TC:(et + 1) * TC],
                                        in0=dl[:, et * TC:(et + 1) * TC],
                                        in1=xc[:, et * TC:(et + 1) * TC]))
                mm.wait(v)
                for et in range(DT):
                    sy.inc(v.memset(yb[:, et * TC:(et + 1) * TC], 0.0))
                for n in range(N3):
                    sy.wait(g)
                    sx.wait(g)
                    wl.inc(g.dma_start(out=Brep[:],
                                       in_=l3tot[sc, 64 + n:65 + n, :].broadcast_to((128, TC))), 16)
                    wl.inc(g.dma_start(out=Crep[:],
                                       in_=l3tot[sc, 80 + n:81 + n, :].broadcast_to((128, TC))), 16)
                    wl.wait(a)
                    wl.wait(v)
                    for et in range(DT):
                        col = et * 16 + n
                        sy.wait(a)
                        sy.inc(a.activation(out=at_[:], in_=dl[:, et * TC:(et + 1) * TC],
                                            func=AF.Exp, scale=A_sb[:, col:col + 1]))
                        sy.wait(v)
                        sy.inc(v.tensor_mul(out=bt_[:], in0=ux[:, et * TC:(et + 1) * TC],
                                            in1=Brep[:]))
                        sy.inc(v.tensor_tensor_scan(out=ht_[:], data0=at_[:], data1=bt_[:],
                                                    initial=hcar[:, col:col + 1],
                                                    op0=ALU.mult, op1=ALU.add))
                        sy.inc(v.tensor_copy(out=hcar[:, col:col + 1],
                                             in_=ht_[:, TC - 1:TC]))
                        v.tensor_mul(out=ht_[:], in0=ht_[:], in1=Crep[:])
                        sy.inc(v.tensor_add(out=yb[:, et * TC:(et + 1) * TC],
                                            in0=yb[:, et * TC:(et + 1) * TC], in1=ht_[:]))
                sy.wait(v)
                for et in range(DT):
                    v.scalar_tensor_tensor(out=yb[:, et * TC:(et + 1) * TC],
                                           in0=xc[:, et * TC:(et + 1) * TC],
                                           scalar=wdc[:, et:et + 1],
                                           in1=yb[:, et * TC:(et + 1) * TC],
                                           op0=ALU.mult, op1=ALU.add)
                    sy.inc(v.tensor_mul(out=yb[:, et * TC:(et + 1) * TC],
                                        in0=yb[:, et * TC:(et + 1) * TC],
                                        in1=skips[:, et * TC:(et + 1) * TC]))
                sy.wait(t)
                for tc in range(1):
                    ch = sc
                    a0 = ch * 512
                    for kd in range(DT):
                        reg = (kd % 2) * 512
                        for et in range(DT):
                            i9 = t.matmul(ps_y[:, reg:reg + 512],
                                          wou[:, et * 1024 + kd * 128:et * 1024 + kd * 128 + 128],
                                          yb[:, et * TC + tc * 512:et * TC + tc * 512 + 512],
                                          start=(et == 0), stop=(et == DT - 1))
                        mm.inc(i9)
                        mm.wait(v)
                        so.wait(v)
                        ev.inc(v.tensor_copy(out=oev[:], in_=ps_y[:, reg:reg + 512]))
                        ev.wait(sp)
                        par = ctx.par_reg(sp)
                        with sp.If_cmp(par, 0, "IS_EQ"):
                            dst = (s_own[kd, :, a0:a0 + 512] if ch < 4
                                   else s_sh[0, kd, :, a0 - T:a0 - T + 512])
                            so.inc(sp.dma_start(out=dst, in_=oev[:]), 16)
                        with sp.Else():
                            dst = (s_sh[1, kd, :, a0:a0 + 512] if ch < 4
                                   else s_own[kd, :, a0 - T:a0 - T + 512])
                            sp.dma_start(out=dst, in_=oev[:]).then_inc(so.h, 16)
                        mm.wait(t)
                        ev.wait(t)
        for _e in (t, v, a, sp, g):
            so.wait(_e)
            sy.wait(_e)
            mm.wait(_e)
            ev.wait(_e)
            sx.wait(_e)
            pe.wait(_e)
            pj.wait(_e)
        emit_flag_write(ctx, g, 3 * 8 + 1)
    return so


def build_fused(sample, dbg=False):
    nc = bass.Bass("TRN2", target_bir_lowering=False)
    stack = contextlib.ExitStack()
    ctx = Ctx(nc, stack, fused=True)
    with stack:
        declare_inputs(ctx, sample)
        declare_work(ctx)
        if dbg:
            for L in range(3):
                ctx.io[f"dbg_xl{L}"] = nc.dram_tensor(
                    f"dbg_xl{L}", [DT, 128, T], F32, kind="ExternalOutput").ap()
            ctx.io["dbg_s3"] = nc.dram_tensor("dbg_s3", [DT, 128, T], F32,
                                              kind="ExternalOutput").ap()
            ctx.io["dbg_g3"] = nc.dram_tensor("dbg_g3", [DT, 128, T], BF16,
                                              kind="ExternalOutput").ap()
        with nc.Block() as blk:
            with nc.body("main", valid_engines=ALL_E):
                setup_consts(ctx)
                for layer in range(4):
                    x_src = ctx.io["x0"] if layer == 0 else ctx.io["x_dram"]
                    emit_ln(ctx, layer, x_src, store_f32=(layer == 1))
                    nc.all_engine_barrier()
                    if layer in (0, 3):
                        emit_ffn03(ctx, layer)
                    elif layer == 1:
                        emit_moe(ctx)
                    else:
                        emit_ffn_l2(ctx)
                    nc.all_engine_barrier()
                    if layer in (0, 2):
                        emit_attn(ctx, layer)
                    elif layer == 1:
                        emit_l1_mixer(ctx)
                    else:
                        emit_l3_mixer(ctx)
                    nc.all_engine_barrier()
                    emit_combine(ctx, layer,
                                 has_partner=(layer != 1),
                                 bias_col=("l0_bo" if layer == 0 else None),
                                 add_xn=(layer == 3),
                                 x_src_name=("x0" if layer == 0 else "x_dram"))
                    nc.all_engine_barrier()
                    if dbg and layer < 3:
                        dxs = ctx.sem("dxs")
                        dxs.inc(ctx.sp.dma_start(out=ctx.io[f"dbg_xl{layer}"],
                                                 in_=ctx.io["x_dram"]), 16)
                        dxs.wait(ctx.sp)
                        nc.all_engine_barrier()
    return nc


def kernel(**inputs):
    rng = np.random.default_rng()
    nonce = rng.integers(1, 1 << 30, size=(1, 64)).astype(np.int32)
    in_maps = [prepare_core_inputs(inputs, c, nonce) for c in range(8)]
    nc = build_fused(in_maps[0])
    res = run_bass_kernel_spmd(nc, in_maps, core_ids=list(range(8)))
    out = np.empty((B, S, D), np.float32)
    for c in range(8):
        b, h = c // 2, c % 2
        o = np.asarray(res.results[c]["out"])
        out[b, h * T:(h + 1) * T] = o.reshape(D, T).T
    return out
